# version 18
# baseline (speedup 1.0000x reference)
import numpy as np
from collections import deque
from contextlib import ExitStack

import concourse.bass as bass
import concourse.mybir as mybir
from concourse.bass_utils import run_bass_kernel_spmd

F32 = mybir.dt.float32
BF16 = mybir.dt.bfloat16
AF = mybir.ActivationFunctionType
ALU = mybir.AluOpType
AX = mybir.AxisListType

S = 4096
D = 1024
DFF = 3072
T = 512
NT = S // T
NCH = D // 128
NFF = DFF // 128
MAGIC = 12582912.0
EXPSC = float(np.float32(1.0) / np.float32(1024.0))


class Sched:
    ENGS = ("sync", "tensor", "vector", "scalar", "gpsimd")

    def __init__(self, nc):
        self.nc = nc
        self.q = {e: [] for e in self.ENGS}
        self.cnt = {}

    def emit(self, eng, fn, sem=None, k=1):
        ev = None
        if sem is not None:
            self.cnt[sem.name] = self.cnt.get(sem.name, 0) + k
            ev = (sem, self.cnt[sem.name])
            self.q[eng].append(lambda e, fn=fn, sem=sem, k=k: fn(e).then_inc(sem, k))
        else:
            self.q[eng].append(fn)
        return ev

    def wait(self, eng, *evs):
        best = {}
        for e in evs:
            if e is None:
                continue
            sem, val = e
            if best.get(sem.name, (None, -1))[1] < val:
                best[sem.name] = (sem, val)
        for sem, val in best.values():
            self.q[eng].append(lambda e, sem=sem, val=val: e.wait_ge(sem, val))

    def barrier(self, sB):
        self.cnt[sB.name] = self.cnt.get(sB.name, 0) + len(self.ENGS)
        tgt = self.cnt[sB.name]
        for eng in self.ENGS:
            self.q[eng].append(lambda e, sB=sB: e.nop().then_inc(sB, 1))
            self.q[eng].append(lambda e, sB=sB, tgt=tgt: e.wait_ge(sB, tgt))

    def run(self):
        with self.nc.Block() as block:
            block.sync(lambda e: [f(e) for f in self.q["sync"]])
            block.tensor(lambda e: [f(e) for f in self.q["tensor"]])
            block.vector(lambda e: [f(e) for f in self.q["vector"]])
            block.scalar(lambda e: [f(e) for f in self.q["scalar"]])
            block.gpsimd(lambda e: [f(e) for f in self.q["gpsimd"]])


def build_nc(n_cores=8):
    nc = bass.Bass("TRN2", target_bir_lowering=False, detect_race_conditions=False)

    x_in = nc.dram_tensor("x_in", [S, D], F32, kind="ExternalInput").ap()
    w_q = nc.dram_tensor("w_q", [D, D], F32, kind="ExternalInput").ap()
    b_q = nc.dram_tensor("b_q", [D], F32, kind="ExternalInput").ap()
    w_k = nc.dram_tensor("w_k", [D, D], F32, kind="ExternalInput").ap()
    b_k = nc.dram_tensor("b_k", [D], F32, kind="ExternalInput").ap()
    w_v = nc.dram_tensor("w_v", [D, D], F32, kind="ExternalInput").ap()
    b_v = nc.dram_tensor("b_v", [D], F32, kind="ExternalInput").ap()
    w_o = nc.dram_tensor("w_o", [D, D], F32, kind="ExternalInput").ap()
    b_o = nc.dram_tensor("b_o", [D], F32, kind="ExternalInput").ap()
    w_gate = nc.dram_tensor("w_gate", [DFF, D], F32, kind="ExternalInput").ap()
    w_up = nc.dram_tensor("w_up", [DFF, D], F32, kind="ExternalInput").ap()
    w_down = nc.dram_tensor("w_down", [D, DFF], F32, kind="ExternalInput").ap()
    ident = nc.dram_tensor("ident", [128, 128], F32, kind="ExternalInput").ap()
    seg16 = nc.dram_tensor("seg16", [128, NCH * 16], F32, kind="ExternalInput").ap()
    sel64 = nc.dram_tensor("sel64", [16, NCH * 128], F32, kind="ExternalInput").ap()
    ones_row = nc.dram_tensor("ones_row", [1, 128], F32, kind="ExternalInput").ap()

    out = nc.dram_tensor("out", [S, D], F32, kind="ExternalOutput").ap()

    xn_s = nc.dram_tensor("xn_s", [NT, NCH, 128, T], F32, kind="Internal").ap()
    xnb_s = nc.dram_tensor("xnb_s", [NT, NCH, 128, T], BF16, kind="Internal").ap()
    acc_s = nc.dram_tensor("acc_s", [NT, 4, 128, D], F32, kind="Internal").ap()
    up_s = nc.dram_tensor("up_s", [NT, NFF, 128, T], F32, kind="Internal").ap()
    gate_s = nc.dram_tensor("gate_s", [NT, NFF, 128, T], F32, kind="Internal").ap()
    t_s = nc.dram_tensor("t_s", [NT, NFF, 128, T], F32, kind="Internal").ap()
    o_s = nc.dram_tensor("o_s", [NT, NFF, 128, T], F32, kind="Internal").ap()
    wupT_s = nc.dram_tensor("wupT_s", [NFF, NCH, 128, 128], BF16, kind="Internal").ap()
    wgateT_s = nc.dram_tensor("wgateT_s", [NFF, NCH, 128, 128], BF16, kind="Internal").ap()
    wdownT_s = nc.dram_tensor("wdownT_s", [NFF, 128, D], BF16, kind="Internal").ap()
    cc_in = [nc.dram_tensor(f"cc_in{i}", [1, 16], F32, kind="Internal").ap() for i in range(4)]
    cc_out = [nc.dram_tensor(f"cc_out{i}", [1, 16], F32, kind="Internal").ap() for i in range(4)]
    xp_s = nc.dram_tensor("xp_s", [128, 1], F32, kind="Internal").ap()

    ps_sc = [nc.alloc_psum_tensor(f"ps_sc{i}", [128, 1024], F32).ap() for i in range(3)]
    ps_mm = [nc.alloc_psum_tensor(f"ps_mm{i}", [128, 512], F32).ap() for i in range(2)]

    es = ExitStack()
    NSL = 28
    sDma = [es.enter_context(nc.semaphore(f"sD{i}")) for i in range(NSL)]
    sT = es.enter_context(nc.semaphore("sT"))
    sV = es.enter_context(nc.semaphore("sV"))
    sS = es.enter_context(nc.semaphore("sS"))
    sG = es.enter_context(nc.semaphore("sG"))
    sC = es.enter_context(nc.semaphore("sC"))
    sB = es.enter_context(nc.semaphore("sB"))

    K = Sched(nc)
    A = nc.alloc_sbuf_tensor

    idn = A("idn", [128, 128], F32).ap()
    idn_bf = A("idn_bf", [128, 128], BF16).ap()
    seg = A("seg", [128, NCH * 16], F32).ap()
    sel_bf = A("sel_bf", [16, NCH * 128], BF16).ap()
    ones1 = A("ones1", [1, 128], F32).ap()
    bqs = A("bqs", [128, NCH], F32).ap()
    bks = A("bks", [128, NCH], F32).ap()
    bvs = A("bvs", [128, NCH], F32).ap()
    bos = A("bos", [128, NCH], F32).ap()
    mx_xn = A("mx_xn", [128, 1], F32).ap()
    mx_up = A("mx_up", [128, 1], F32).ap()
    mx_gate = A("mx_gate", [128, 1], F32).ap()
    mx_t = A("mx_t", [128, 1], F32).ap()
    mx_o = A("mx_o", [128, 1], F32).ap()
    mxrow = A("mxrow", [1, 128], F32).ap()
    scals = A("scals", [1, 16], F32).ap()
    rx_b = A("rx_b", [128, 1], F32).ap()
    mup_b = A("mup_b", [128, 1], F32).ap()
    mg_b = A("mg_b", [128, 1], F32).ap()
    sgh_b = A("sgh_b", [128, 1], F32).ap()
    mt_b = A("mt_b", [128, 1], F32).ap()
    mo_b = A("mo_b", [128, 1], F32).ap()
    so_b = A("so_b", [128, 1], F32).ap()
    dummy = A("dump_exp", [128, 1024], BF16).ap()
    junkA = A("junkA", [128, 64], F32).ap()
    junkB = A("junkB", [128, 64], F32).ap()
    K.emit("vector", lambda e: e.memset(junkA, 0.0))

    def vgap(n=1):
        for _ in range(n):
            K.emit("vector", lambda e: e.tensor_copy(out=junkB, in_=junkA))

    dma_rr = [0]

    def dma(ap_out, ap_in, eng="sync", slow=False):
        sem = sDma[dma_rr[0] % NSL]
        dma_rr[0] += 1
        prev = K.cnt.get(sem.name, 0)
        if prev:
            K.q[eng].append(lambda e, sem=sem, prev=prev: e.wait_ge(sem, prev))
        return K.emit(eng, lambda e, o=ap_out, i=ap_in, s=slow: e.dma_start(
            out=o, in_=i, allow_slow_non_contiguous=s), sem, 16)

    selbf_ev = K.emit("gpsimd", lambda e: e.dma_start(out=sel_bf, in_=sel64), sG, 16)
    ld_ev = [dma(idn, ident), dma(seg, seg16), dma(ones1, ones_row),
             dma(bqs, b_q.rearrange("(c p) -> p c", p=128), slow=True),
             dma(bks, b_k.rearrange("(c p) -> p c", p=128), slow=True),
             dma(bvs, b_v.rearrange("(c p) -> p c", p=128), slow=True),
             dma(bos, b_o.rearrange("(c p) -> p c", p=128), slow=True)]
    K.emit("scalar", lambda e: e.preload_act_table(AF.Exp))
    for t in (mx_xn, mx_up, mx_gate, mx_t, mx_o, scals):
        K.emit("vector", lambda e, t=t: e.memset(t, 0.0))
    K.wait("vector", *ld_ev)
    idnbf_ev = K.emit("vector", lambda e: e.tensor_copy(out=idn_bf, in_=idn), sV)

    free = {"sc0": None, "sc1": None, "sc2": None, "mm0": None, "mm1": None}
    mmi = [0]
    sci = [0]

    def next_mm():
        i = mmi[0] % 2
        mmi[0] += 1
        return (f"mm{i}", ps_mm[i])

    next_ms = next_mm

    def next_sc():
        i = sci[0] % 3
        sci[0] += 1
        return (f"sc{i}", ps_sc[i])

    scopeW = ExitStack()
    Wq = scopeW.enter_context(nc.sbuf_tensor("Wq", [128, NCH * D], BF16)).ap()
    Wk = scopeW.enter_context(nc.sbuf_tensor("Wk", [128, NCH * D], BF16)).ap()
    Wv = scopeW.enter_context(nc.sbuf_tensor("Wv", [128, NCH * D], BF16)).ap()
    Wo = scopeW.enter_context(nc.sbuf_tensor("Wo", [128, NCH * D], BF16)).ap()

    prep = ExitStack()
    Aw = lambda name, shape, dt: prep.enter_context(nc.sbuf_tensor(name, shape, dt)).ap()
    NPB = 2
    wst = [Aw(f"wst{i}", [128, 1024], F32) for i in range(NPB)]
    wlp = Aw("wlp", [128, 1024], F32)
    bmax = Aw("bmax", [128, 32], F32)
    brec = Aw("brec", [128, 32], F32)
    wbf = [Aw(f"wbf{i}", [128, 1024], BF16) for i in range(NPB)]
    wtrb = [Aw(f"wtrb{i}", [128, 1024], BF16) for i in range(NPB)]

    wst_free = [None] * NPB
    for wi, (wdr, wsb) in enumerate(((w_q, Wq), (w_k, Wk), (w_v, Wv), (w_o, Wo))):
        for kc in range(NCH):
            i2 = (wi * NCH + kc) % NPB
            K.wait("sync", wst_free[i2])
            lev = dma(wst[i2][:, 0:D], wdr[kc * 128:(kc + 1) * 128, :])
            K.wait("vector", lev)
            wst_free[i2] = K.emit(
                "vector", lambda e, i2=i2, wsb=wsb, kc=kc: e.tensor_copy(
                    out=wsb[:, kc * D:(kc + 1) * D], in_=wst[i2][:, 0:D]), sV)
    qkvo_ready = wst_free[:]

    wbf_free = [None] * NPB
    wtrb_free = [None] * NPB
    prep_stores = []

    def lpbq_chunk(wdr, oc, n_in, dst_ap_fn, ci, c0=0):
        i2 = ci % NPB
        nb = n_in // 32
        nbk = n_in // 128
        K.wait("sync", wst_free[i2])
        lev = dma(wst[i2][:, 0:n_in], wdr[oc * 128:(oc + 1) * 128, c0:c0 + n_in])
        K.wait("vector", lev, wbf_free[i2])
        K.emit("vector", lambda e: e.reduce_max(
            out=bmax[:, 0:nb], in_=wst[i2][:, 0:n_in].rearrange("p (b t) -> p b t", b=nb),
            axis=AX.X, apply_absolute_value=True))
        vgap()
        K.emit("vector", lambda e: e.tensor_scalar(
            out=bmax[:, 0:nb], in0=bmax[:, 0:nb], scalar1=1.0 / 7.0, scalar2=1e-12,
            op0=ALU.mult, op1=ALU.max))
        vgap()
        K.emit("vector", lambda e: e.reciprocal(out=brec[:, 0:nb], in_=bmax[:, 0:nb]))
        vgap()
        wst_free[i2] = K.emit("vector", lambda e: e.tensor_tensor(
            out=wlp[:, 0:n_in].rearrange("p (b t) -> p b t", b=nb),
            in0=wst[i2][:, 0:n_in].rearrange("p (b t) -> p b t", b=nb),
            in1=brec[:, 0:nb].rearrange("p (b o) -> p b o", o=1).broadcast_to([128, nb, 32]),
            op=ALU.mult), sV)
        K.emit("vector", lambda e: e.tensor_scalar(
            out=wlp[:, 0:n_in], in0=wlp[:, 0:n_in], scalar1=MAGIC, scalar2=MAGIC,
            op0=ALU.add, op1=ALU.subtract))
        K.emit("vector", lambda e: e.tensor_scalar(
            out=wlp[:, 0:n_in], in0=wlp[:, 0:n_in], scalar1=7.0, scalar2=-8.0,
            op0=ALU.min, op1=ALU.max))
        cast_ev = K.emit("vector", lambda e: e.tensor_tensor(
            out=wbf[i2][:, 0:n_in].rearrange("p (b t) -> p b t", b=nb),
            in0=wlp[:, 0:n_in].rearrange("p (b t) -> p b t", b=nb),
            in1=bmax[:, 0:nb].rearrange("p (b o) -> p b o", o=1).broadcast_to([128, nb, 32]),
            op=ALU.mult), sV)
        cpevs = []
        tev = None
        for ic in range(nbk):
            nm, ps = next_mm()
            K.wait("tensor", cast_ev, free[nm], idnbf_ev)
            tev = K.emit("tensor", lambda e, ps=ps, i2=i2, ic=ic: e.transpose(
                ps.bitcast(BF16)[:, 0:128], wbf[i2][:, ic * 128:(ic + 1) * 128], idn_bf), sT)
            K.wait("vector", tev, wtrb_free[i2] if ic == 0 else None)
            cpev = K.emit("vector", lambda e, ps=ps, i2=i2, ic=ic: e.tensor_copy(
                out=wtrb[i2][:, ic * 128:(ic + 1) * 128], in_=ps.bitcast(BF16)[:, 0:128]), sV)
            free[nm] = cpev
            cpevs.append(cpev)
        wbf_free[i2] = tev
        K.wait("sync", *cpevs)
        sev = dma(dst_ap_fn(oc), wtrb[i2][:, 0:nbk * 128].rearrange("p (b f) -> p b f", f=128))
        wtrb_free[i2] = sev
        prep_stores.append(sev)

    wready = {}
    lpbq_thunks = deque()
    ci = 0
    for oc in range(NFF):
        lpbq_thunks.append(lambda oc=oc, ci=ci: lpbq_chunk(
            w_up, oc, D, lambda oc: wupT_s[oc].transpose([1, 0, 2]), ci,
        )); ci += 1
        lpbq_thunks.append(("mark", ("up", oc)))
    for oc in range(NFF):
        lpbq_thunks.append(lambda oc=oc, ci=ci: lpbq_chunk(
            w_gate, oc, D, lambda oc: wgateT_s[oc].transpose([1, 0, 2]), ci,
        )); ci += 1
        lpbq_thunks.append(("mark", ("gate", oc)))
    for oc in range(NCH):
        for sub in range(3):
            lpbq_thunks.append(lambda oc=oc, ci=ci, sub=sub: lpbq_chunk(
                w_down, oc, D,
                lambda oc, sub=sub: wdownT_s[sub * 8:(sub + 1) * 8, :,
                                             oc * 128:(oc + 1) * 128].transpose([1, 0, 2]),
                ci, c0=sub * D)); ci += 1
    lpbq_thunks.append(("mark", ("down", 0)))

    pp = ExitStack()
    Ap = lambda name, shape, dt: pp.enter_context(nc.sbuf_tensor(name, shape, dt)).ap()
    px2 = [Ap("px0", [128, 4096], F32)] * 2
    pxn = Ap("pxn", [128, 4096], F32)
    pfm = Ap("pfm", [128, 4096], F32)
    psq = pfm
    pbf = Ap("pbf", [128, 4096], BF16)
    stt = Ap("stt", [128, 48], F32)
    sum4 = stt[:, 0:4]; sq4 = stt[:, 4:8]; m4 = stt[:, 8:12]; a4 = stt[:, 12:16]
    y4 = stt[:, 16:20]; t4a = stt[:, 20:24]; e4 = stt[:, 28:32]; xm1 = stt[:, 32:33]

    xns_ev = [None] * NT
    xnb_ev = [None] * NT
    pp_st = []
    px_free = [None, None]
    lx_evs = [None] * NT
    for tile in range(NT):
        t0 = tile * T
        K.wait("sync", px_free[0], *pp_st[-2:])
        lx_evs[tile] = dma(px2[tile % 2].rearrange("p (j f) -> p j f", j=4),
                           x_in[t0:t0 + T, :].rearrange("(j p) f -> p j f", p=128))
        px = px2[tile % 2]
        lx = lx_evs[tile]
        K.wait("vector", lx)
        K.emit("vector", lambda e, px=px: e.reduce_sum(
            out=sum4, in_=px.rearrange("p (j f) -> p j f", j=4), axis=AX.X))
        K.emit("vector", lambda e, px=px: e.tensor_tensor(out=psq, in0=px, in1=px, op=ALU.mult))
        K.emit("vector", lambda e: e.reduce_sum(
            out=sq4, in_=psq.rearrange("p (j f) -> p j f", j=4), axis=AX.X))
        K.emit("vector", lambda e: e.tensor_scalar(
            out=m4, in0=sum4, scalar1=1.0 / D, scalar2=None, op0=ALU.mult))
        vgap()
        K.emit("vector", lambda e: e.tensor_scalar(
            out=a4, in0=sq4, scalar1=1.0 / D, scalar2=None, op0=ALU.mult))
        K.emit("vector", lambda e: e.tensor_tensor(out=e4, in0=m4, in1=m4, op=ALU.mult))
        vgap()
        K.emit("vector", lambda e: e.tensor_tensor(out=a4, in0=a4, in1=e4, op=ALU.subtract))
        vgap()
        K.emit("vector", lambda e: e.tensor_scalar(
            out=a4, in0=a4, scalar1=1e-5, scalar2=None, op0=ALU.add))
        vgap()
        K.emit("vector", lambda e: e.reciprocal(out=y4, in_=a4))
        vgap()
        for _ in range(6):
            K.emit("vector", lambda e: e.tensor_tensor(out=t4a, in0=y4, in1=y4, op=ALU.mult))
            vgap()
            K.emit("vector", lambda e: e.tensor_tensor(out=t4a, in0=t4a, in1=a4, op=ALU.mult))
            vgap()
            K.emit("vector", lambda e: e.tensor_scalar(
                out=t4a, in0=t4a, scalar1=-0.5, scalar2=1.5, op0=ALU.mult, op1=ALU.add))
            vgap()
            K.emit("vector", lambda e: e.tensor_tensor(out=y4, in0=y4, in1=t4a, op=ALU.mult))
            vgap()
        vxn = None
        for j in range(4):
            vxn = K.emit("vector", lambda e, j=j, px=px: e.tensor_scalar(
                out=pxn[:, j * 1024:(j + 1) * 1024], in0=px[:, j * 1024:(j + 1) * 1024],
                scalar1=m4[:, j:j + 1], scalar2=y4[:, j:j + 1],
                op0=ALU.subtract, op1=ALU.mult), sV)
        K.emit("vector", lambda e: e.reduce_max(
            out=xm1, in_=pxn, axis=AX.X, apply_absolute_value=True))
        vgap()
        K.emit("vector", lambda e: e.tensor_tensor(out=mx_xn, in0=mx_xn, in1=xm1, op=ALU.max))
        vfm = []
        for c in range(NCH):
            nm, ps = next_ms()
            K.wait("tensor", vxn, free[nm])
            tev = None
            for j in range(4):
                tev = K.emit("tensor", lambda e, ps=ps, j=j, c=c: e.transpose(
                    ps[:, j * 128:(j + 1) * 128],
                    pxn[:, j * 1024 + c * 128: j * 1024 + (c + 1) * 128], idn), sT)
            K.wait("vector", tev)
            K.emit("vector", lambda e, ps=ps, c=c: e.tensor_copy(
                out=pfm[:, c * 512:(c + 1) * 512], in_=ps))
            ev = K.emit("vector", lambda e, ps=ps, c=c: e.tensor_copy(
                out=pbf[:, c * 512:(c + 1) * 512], in_=ps), sV)
            free[nm] = ev
            vfm.append(ev)
        px_free[0] = vxn
        K.wait("sync", *vfm)
        xns_ev[tile] = dma(xn_s[tile].transpose([1, 0, 2]),
                           pfm.rearrange("p (c t) -> p c t", c=NCH))
        xnb_ev[tile] = dma(xnb_s[tile].transpose([1, 0, 2]),
                           pbf.rearrange("p (c t) -> p c t", c=NCH))
        pp_st = [xns_ev[tile], xnb_ev[tile]]
    pp.close()
    K.barrier(sB)

    def allreduce_max(idx, srcs, n_vals, wait_evs=()):
        pk = []
        for i, src in enumerate(srcs):
            K.wait("sync", *wait_evs)
            ev0 = dma(xp_s, src)
            K.wait("sync", ev0)
            ev = dma(mxrow[0:1, 0:128], xp_s.rearrange("p o -> o p"))
            K.wait("vector", ev)
            pk.append(K.emit("vector", lambda e, i=i: e.reduce_max(
                out=scals[0:1, i:i + 1], in_=mxrow, axis=AX.X), sV))
            vgap()
        z = K.emit("vector", lambda e, n_vals=n_vals: e.memset(scals[0:1, n_vals:8], 0.0), sV)
        K.wait("gpsimd", z, *pk)
        gev = K.emit("gpsimd", lambda e, idx=idx: e.dma_start(
            out=cc_in[idx], in_=scals[0:1, 0:16]), sG, 16)
        K.wait("gpsimd", gev)
        cev = K.emit("gpsimd", lambda e, idx=idx: e.collective_compute(
            "AllReduce", ALU.max, replica_groups=[list(range(n_cores))],
            ins=[cc_in[idx]], outs=[cc_out[idx]]), sC, 1)
        K.wait("gpsimd", cev)
        return K.emit("gpsimd", lambda e, idx=idx: e.dma_start(
            out=scals[0:1, 0:16], in_=cc_out[idx]), sG, 16)

    def bcast(dst, src_col, wait_ev):
        nm, ps = next_ms()
        K.wait("tensor", wait_ev, free[nm])
        mev = K.emit("tensor", lambda e, ps=ps, src_col=src_col: e.matmul(
            ps[:, 0:1], ones1, scals[0:1, src_col:src_col + 1], start=True, stop=True), sT)
        K.wait("vector", mev)
        ev = K.emit("vector", lambda e, ps=ps, dst=dst: e.tensor_copy(out=dst, in_=ps[:, 0:1]), sV)
        vgap()
        free[nm] = ev
        return ev

    ar1 = allreduce_max(0, [mx_xn], 1)
    K.wait("vector", ar1)
    K.emit("vector", lambda e: e.tensor_scalar(
        out=scals[0:1, 8:9], in0=scals[0:1, 0:1], scalar1=1.0 / 32767.0, scalar2=1e-12,
        op0=ALU.mult, op1=ALU.max))
    vgap()
    sx_ev = K.emit("vector", lambda e: e.reciprocal(out=scals[0:1, 9:10], in_=scals[0:1, 8:9]), sV)
    vgap()
    rx_ev = bcast(rx_b, 9, sx_ev)

    main = ExitStack()
    scopeATT = ExitStack()
    Am = lambda name, shape, dt: scopeATT.enter_context(nc.sbuf_tensor(name, shape, dt)).ap()
    x_j = Am("x_j", [128, 1024], F32)
    xn_bf = Am("xn_bf", [128, 4096], BF16)
    Qb = Am("Qb", [128, 4096], BF16)
    Kb = Am("Kb", [128, 4096], BF16)
    Vf = Am("Vf", [128, 4096], F32)
    attn_f = Vf
    Qrep2 = [Am(f"Qrep{i}", [128, 4096], BF16) for i in range(2)]
    qk2 = [Am(f"qk2_{i}", [128, 512], F32) for i in range(2)]
    DVb = Am("DVb", [128, 4096], BF16)
    dn = Am("dn", [128, 64], F32)
    dnpAll2 = [Am(f"dnpAll{i}", [128, 128], F32) for i in range(2)]
    rec = Am("rec", [128, 64], F32)
    numer = Am("numer", [16, 512], F32)
    recD = Am("recD", [16, 512], F32)
    Dg_bf = Am("Dg_bf", [16, 512], BF16)
    scopeB = ExitStack()
    scopeCD = ExitStack()
    Ab = lambda name, shape, dt: scopeB.enter_context(nc.sbuf_tensor(name, shape, dt)).ap()
    xn_rd = Ab("xn_rd", [128, 2048], F32)
    xnq = Ab("xnq", [128, 4096], BF16)
    NWUB = 7
    wub = [Ab(f"wub{i}", [128, NCH * 128], BF16) for i in range(NWUB)]
    spl = [Ab(f"spl{i}", [128, 512], F32) for i in range(3)]
    splmx = Ab("splmx", [128, 1], F32)

    bstate = {"tile": -1, "xnq_ev": None, "xnq_users": [], "xn_rd_user": None}
    wub_free = [None] * NWUB
    wub_ld = [None] * NWUB
    spl_free = [None] * 3
    wubi = [0]
    bdone_evs = []
    bchain_list = []

    def prefetch_wchain():
        i = wubi[0]
        if i >= len(bchain_list):
            return
        wubi[0] += 1
        tile, which, oc = bchain_list[i]
        wsel = i % NWUB
        w_src = wupT_s if which == 0 else wgateT_s
        wkey = ("up" if which == 0 else "gate", oc)
        K.wait("sync", wub_free[wsel], wready[wkey])
        wub_ld[wsel] = dma(wub[wsel].rearrange("p (c f) -> p c f", c=NCH),
                           w_src[oc].transpose([1, 0, 2]))

    def prep_xnq(tile):
        qev = None
        for h in range(2):
            K.wait("sync", xns_ev[tile],
                   *(bstate["xnq_users"] if h == 0 else []), bstate["xn_rd_user"])
            lev = dma(xn_rd.rearrange("p (c t) -> p c t", c=NCH // 2),
                      xn_s[tile, h * 4:(h + 1) * 4].transpose([1, 0, 2]))
            K.wait("vector", lev, rx_ev)
            K.emit("vector", lambda e: e.tensor_scalar(
                out=xn_rd, in0=xn_rd, scalar1=rx_b, scalar2=None, op0=ALU.mult))
            K.emit("vector", lambda e: e.tensor_scalar(
                out=xn_rd, in0=xn_rd, scalar1=MAGIC, scalar2=MAGIC,
                op0=ALU.add, op1=ALU.subtract))
            qev = K.emit("vector", lambda e, h=h: e.tensor_scalar(
                out=xnq[:, h * 2048:(h + 1) * 2048], in0=xn_rd,
                scalar1=32767.0, scalar2=-32768.0,
                op0=ALU.min, op1=ALU.max), sV)
            bstate["xn_rd_user"] = qev
        bstate["tile"] = tile
        bstate["xnq_ev"] = qev
        bstate["xnq_users"] = []

    def b_chain(ci, tile, which, oc):
        if bstate["tile"] != tile:
            prep_xnq(tile)
        while wubi[0] < min(ci + NWUB, len(bchain_list)):
            prefetch_wchain()
        wsel = ci % NWUB
        wev = wub_ld[wsel]
        dst_s = up_s if which == 0 else gate_s
        mxacc = mx_up if which == 0 else mx_gate
        nm, ps = next_mm()
        K.wait("tensor", free[nm], wev, bstate["xnq_ev"])
        mev = None
        for ic in range(NCH):
            mev = K.emit("tensor", lambda e, ps=ps, wsel=wsel, ic=ic: e.matmul(
                ps, wub[wsel][:, ic * 128:(ic + 1) * 128],
                xnq[:, ic * 512:(ic + 1) * 512],
                start=(ic == 0), stop=(ic == NCH - 1)), sT)
        wub_free[wsel] = mev
        bstate["xnq_users"].append(mev)
        ssel = (which * NFF + oc) % 3
        sb = spl[ssel]
        K.wait("vector", mev, spl_free[ssel])
        ev = K.emit("vector", lambda e, ps=ps, sb=sb: e.tensor_copy(out=sb, in_=ps), sV)
        free[nm] = ev
        K.emit("vector", lambda e, sb=sb: e.reduce_max(
            out=splmx, in_=sb, axis=AX.X, apply_absolute_value=True))
        vgap()
        mxev = K.emit("vector", lambda e, mxacc=mxacc: e.tensor_tensor(
            out=mxacc, in0=mxacc, in1=splmx, op=ALU.max), sV)
        K.wait("sync", ev)
        spl_free[ssel] = dma(dst_s[tile, oc], sb)
        bdone_evs.append(mxev)

    for tile in range(NT):
        for oc in range(NFF):
            bchain_list.append((tile, 0, oc))
            bchain_list.append((tile, 1, oc))
    filler = deque()
    for ci, (tile, which, oc) in enumerate(bchain_list):
        filler.append(lambda ci=ci, tile=tile, which=which, oc=oc: b_chain(ci, tile, which, oc))

    cur_tile = [0]

    def pump(n):
        if cur_tile[0] == 0:
            return
        for _ in range(n):
            if filler:
                filler.popleft()()

    cd = {}

    NBLK = NT * NFF
    BPG = 3
    gate_flat = gate_s.rearrange("n c p t -> (n c) p t")
    t_flat = t_s.rearrange("n c p t -> (n c) p t")
    up_flat = up_s.rearrange("n c p t -> (n c) p t")
    o_flat = o_s.rearrange("n c p t -> (n c) p t")

    state = {"t_st": None, "last_tev": None, "o_st": None, "last_oev": None,
             "mup_ev": None, "mg_ev": None, "sgh_ev": None, "mt_ev": None,
             "mo_ev": None, "so_ev": None}

    def emit_ar2():
        scopeB.close()
        Ac = lambda name, shape, dt: scopeCD.enter_context(nc.sbuf_tensor(name, shape, dt)).ap()
        cd["g_rd"] = [Ac(f"g_rd{p}", [128, BPG * 512], F32) for p in range(2)]
        cd["sig_f"] = [Ac(f"sig_f{p}", [128, BPG * 512], F32) for p in range(2)]
        cd["t_wr"] = [Ac(f"t_wr{p}", [128, BPG * 512], F32) for p in range(2)]
        cd["cmx"] = Ac("cmx", [128, 1], F32)
        cd["bdone"] = (bstate["xnq_users"][-1] if bstate["xnq_users"] else None,
                       wub_free[0], wub_free[1], wub_free[2], wub_free[3])
        ar2 = allreduce_max(1, [mx_up, mx_gate], 2, wait_evs=spl_free)
        K.wait("vector", ar2)
        K.emit("vector", lambda e: e.tensor_tensor(
            out=scals[0:1, 0:2], in0=scals[0:1, 0:2],
            in1=scals[0:1, 8:9].broadcast_to([1, 2]), op=ALU.mult))
        vgap()
        K.emit("vector", lambda e: e.tensor_scalar(
            out=scals[0:1, 0:2], in0=scals[0:1, 0:2], scalar1=1.0 / 32767.0, scalar2=1e-12,
            op0=ALU.mult, op1=ALU.max))
        vgap()
        K.emit("vector", lambda e: e.tensor_copy(out=scals[0:1, 10:12], in_=scals[0:1, 0:2]))
        K.emit("vector", lambda e: e.reciprocal(out=scals[0:1, 2:4], in_=scals[0:1, 0:2]))
        vgap()
        K.emit("vector", lambda e: e.tensor_tensor(
            out=scals[0:1, 2:4], in0=scals[0:1, 2:4],
            in1=scals[0:1, 8:9].broadcast_to([1, 2]), op=ALU.mult))
        vgap()
        K.emit("vector", lambda e: e.tensor_scalar(
            out=scals[0:1, 4:5], in0=scals[0:1, 11:12], scalar1=0.5, scalar2=None, op0=ALU.mult))
        vgap()
        b2 = K.emit("vector", lambda e: e.tensor_scalar(
            out=scals[0:1, 12:13], in0=scals[0:1, 11:12], scalar1=1.0 / 65536.0, scalar2=None,
            op0=ALU.mult), sV)
        vgap()
        state["mup_ev"] = bcast(mup_b, 2, b2)
        state["mg_ev"] = bcast(mg_b, 3, state["mup_ev"])
        state["sgh_ev"] = bcast(sgh_b, 4, state["mg_ev"])

    vfiller = deque()

    def vpump(n):
        for _ in range(n):
            while vfiller and isinstance(vfiller[0], tuple):
                _, key = vfiller.popleft()
                wready[key] = prep_stores[-1]
            if vfiller:
                vfiller.popleft()()
        while vfiller and isinstance(vfiller[0], tuple):
            _, key = vfiller.popleft()
            wready[key] = prep_stores[-1]

    vfiller.extend(lpbq_thunks)
    lpbq_thunks = None

    def enqueue_C(g0, g1):
        cmx = cd["cmx"]
        guser = [None, None]
        tst = [None, None]

        for g in range(g0, g1):
            blk = g * BPG
            p = g % 2
            g_rd, sig_f, t_wr = cd["g_rd"][p], cd["sig_f"][p], cd["t_wr"][p]

            def step_load(g=g, p=p, g_rd=g_rd, blk=blk):
                K.wait("sync", guser[p])
                if g == 0:
                    K.wait("tensor", *cd["bdone"])
                    K.wait("sync", *cd["bdone"], *spl_free)
                cd[("lev", p)] = dma(g_rd.rearrange("p (b t) -> p b t", b=BPG),
                                     gate_flat[blk:blk + BPG].transpose([1, 0, 2]))
            vfiller.append(step_load)

            def c1(p=p, g_rd=g_rd):
                K.wait("vector", cd[("lev", p)], state["mg_ev"], state["sgh_ev"])
                K.emit("vector", lambda e, g_rd=g_rd: e.tensor_scalar(
                    out=g_rd, in0=g_rd, scalar1=mg_b, scalar2=None, op0=ALU.mult))
            def c2(g_rd=g_rd):
                K.emit("vector", lambda e, g_rd=g_rd: e.tensor_scalar(
                    out=g_rd, in0=g_rd, scalar1=MAGIC, scalar2=MAGIC, op0=ALU.add, op1=ALU.subtract))
            def c3(p=p, g_rd=g_rd, sig_f=sig_f):
                gq_ev = K.emit("vector", lambda e, g_rd=g_rd: e.tensor_scalar(
                    out=g_rd, in0=g_rd, scalar1=32767.0, scalar2=-32768.0,
                    op0=ALU.min, op1=ALU.max), sV)
                K.wait("scalar", gq_ev, state["sgh_ev"])
                cd[("tn_ev", p)] = K.emit("scalar", lambda e, g_rd=g_rd, sig_f=sig_f: e.activation(
                    out=sig_f, in_=g_rd, func=AF.Tanh, scale=sgh_b), sS)
            def c4(p=p, sig_f=sig_f):
                K.wait("vector", cd[("tn_ev", p)])
                K.emit("vector", lambda e, sig_f=sig_f: e.tensor_scalar(
                    out=sig_f, in0=sig_f, scalar1=32768.0, scalar2=32768.0,
                    op0=ALU.mult, op1=ALU.add))
            def c5(sig_f=sig_f):
                K.emit("vector", lambda e, sig_f=sig_f: e.tensor_scalar(
                    out=sig_f, in0=sig_f, scalar1=MAGIC, scalar2=MAGIC,
                    op0=ALU.add, op1=ALU.subtract))
            def c6(sig_f=sig_f):
                K.emit("vector", lambda e, sig_f=sig_f: e.tensor_scalar(
                    out=sig_f, in0=sig_f, scalar1=65535.0, scalar2=0.0,
                    op0=ALU.min, op1=ALU.max))
            def c7(p=p, g_rd=g_rd, sig_f=sig_f, t_wr=t_wr):
                K.wait("vector", tst[p])
                t_ev = K.emit("vector", lambda e, g_rd=g_rd, sig_f=sig_f, t_wr=t_wr: e.tensor_tensor(
                    out=t_wr, in0=g_rd, in1=sig_f, op=ALU.mult), sV)
                state["last_tev"] = t_ev
                guser[p] = t_ev
            def c8(t_wr=t_wr):
                K.emit("vector", lambda e, t_wr=t_wr: e.reduce_max(
                    out=cmx, in_=t_wr, axis=AX.X, apply_absolute_value=True))
                vgap()
            def c9(p=p, blk=blk, t_wr=t_wr):
                K.emit("vector", lambda e: e.tensor_tensor(
                    out=mx_t, in0=mx_t, in1=cmx, op=ALU.max))
                K.wait("sync", state["last_tev"])
                tst[p] = dma(t_flat[blk:blk + BPG].transpose([1, 0, 2]),
                             t_wr.rearrange("p (b t) -> p b t", b=BPG))
                state["t_st"] = tst[p]
            for f in (c1, c2, c3, c4, c5, c6, c7, c8, c9):
                vfiller.append(f)

    def emit_ar3():
        ar3 = allreduce_max(2, [mx_t], 1)
        K.wait("vector", ar3)
        K.emit("vector", lambda e: e.tensor_tensor(
            out=scals[0:1, 0:1], in0=scals[0:1, 0:1], in1=scals[0:1, 12:13], op=ALU.mult))
        vgap()
        K.emit("vector", lambda e: e.tensor_scalar(
            out=scals[0:1, 13:14], in0=scals[0:1, 0:1], scalar1=1.0 / 32767.0, scalar2=1e-12,
            op0=ALU.mult, op1=ALU.max))
        vgap()
        K.emit("vector", lambda e: e.reciprocal(out=scals[0:1, 5:6], in_=scals[0:1, 13:14]))
        vgap()
        b3 = K.emit("vector", lambda e: e.tensor_tensor(
            out=scals[0:1, 5:6], in0=scals[0:1, 5:6], in1=scals[0:1, 12:13], op=ALU.mult), sV)
        vgap()
        state["mt_ev"] = bcast(mt_b, 5, b3)

    def enqueue_D(g0, g1):
        cmx = cd["cmx"]
        d2u = [None, None]
        ost = [None, None]

        for g in range(g0, g1):
            blk = g * BPG
            p = g % 2
            g_rd, sig_f, t_wr = cd["g_rd"][p], cd["sig_f"][p], cd["t_wr"][p]

            def d0(g=g, p=p, blk=blk, t_wr=t_wr, g_rd=g_rd):
                K.wait("sync", d2u[p], state["t_st"])
                cd[("lev1", p)] = dma(t_wr.rearrange("p (b t) -> p b t", b=BPG),
                                      t_flat[blk:blk + BPG].transpose([1, 0, 2]))
                cd[("lev2", p)] = dma(g_rd.rearrange("p (b t) -> p b t", b=BPG),
                                      up_flat[blk:blk + BPG].transpose([1, 0, 2]))
            vfiller.append(d0)

            def d1(p=p):
                K.wait("vector", cd[("lev1", p)], cd[("lev2", p)], state["mt_ev"],
                       state["mup_ev"])
            vfiller.append(d1)
            for buf, mb in ((t_wr, mt_b), (g_rd, mup_b)):
                vfiller.append(lambda buf=buf, mb=mb: K.emit(
                    "vector", lambda e, buf=buf, mb=mb: e.tensor_scalar(
                        out=buf, in0=buf, scalar1=mb, scalar2=None, op0=ALU.mult)))
                vfiller.append(lambda buf=buf: K.emit(
                    "vector", lambda e, buf=buf: e.tensor_scalar(
                        out=buf, in0=buf, scalar1=MAGIC, scalar2=MAGIC,
                        op0=ALU.add, op1=ALU.subtract)))
                vfiller.append(lambda buf=buf: K.emit(
                    "vector", lambda e, buf=buf: e.tensor_scalar(
                        out=buf, in0=buf, scalar1=32767.0, scalar2=-32768.0,
                        op0=ALU.min, op1=ALU.max)))
            def d2(p=p, g_rd=g_rd, sig_f=sig_f, t_wr=t_wr):
                K.wait("vector", ost[p])
                o_ev = K.emit("vector", lambda e, g_rd=g_rd, sig_f=sig_f, t_wr=t_wr: e.tensor_tensor(
                    out=sig_f, in0=t_wr, in1=g_rd, op=ALU.mult), sV)
                state["last_oev"] = o_ev
                d2u[p] = o_ev
            def d3(sig_f=sig_f):
                K.emit("vector", lambda e, sig_f=sig_f: e.reduce_max(
                    out=cmx, in_=sig_f, axis=AX.X, apply_absolute_value=True))
                vgap()
            def d4(p=p, blk=blk, sig_f=sig_f):
                K.emit("vector", lambda e: e.tensor_tensor(
                    out=mx_o, in0=mx_o, in1=cmx, op=ALU.max))
                K.wait("sync", state["last_oev"])
                ost[p] = dma(o_flat[blk:blk + BPG].transpose([1, 0, 2]),
                             sig_f.rearrange("p (b t) -> p b t", b=BPG))
                state["o_st"] = ost[p]
            for f in (d2, d3, d4):
                vfiller.append(f)

    def emit_ar4():
        ar4 = allreduce_max(3, [mx_o], 1)
        K.wait("vector", ar4)
        K.emit("vector", lambda e: e.tensor_tensor(
            out=scals[0:1, 6:7], in0=scals[0:1, 13:14], in1=scals[0:1, 10:11], op=ALU.mult))
        vgap()
        K.emit("vector", lambda e: e.tensor_tensor(
            out=scals[0:1, 0:1], in0=scals[0:1, 0:1], in1=scals[0:1, 6:7], op=ALU.mult))
        vgap()
        K.emit("vector", lambda e: e.tensor_scalar(
            out=scals[0:1, 14:15], in0=scals[0:1, 0:1], scalar1=1.0 / 32767.0, scalar2=1e-12,
            op0=ALU.mult, op1=ALU.max))
        vgap()
        K.emit("vector", lambda e: e.reciprocal(out=scals[0:1, 7:8], in_=scals[0:1, 14:15]))
        vgap()
        b4 = K.emit("vector", lambda e: e.tensor_tensor(
            out=scals[0:1, 7:8], in0=scals[0:1, 7:8], in1=scals[0:1, 6:7], op=ALU.mult), sV)
        vgap()
        state["mo_ev"] = bcast(mo_b, 7, b4)
        soc = K.emit("vector", lambda e: e.tensor_copy(
            out=scals[0:1, 15:16], in_=scals[0:1, 14:15]), sV)
        vgap()
        state["so_ev"] = bcast(so_b, 15, soc)


    acc_ev = [None] * NT
    K.wait("tensor", *ld_ev, *qkvo_ready, selbf_ev)
    K.wait("scalar", *ld_ev)

    prev = {"x_tm": None, "xn_bf": None, "qkv_mm": None, "qrep": None,
            "scores": None, "dv_mm": None, "wo_mm": None, "att_tr": None,
            "acc_st": None, "dg": None, "qk_seg": None}

    for tile in range(NT):
        cur_tile[0] = tile
        t0 = tile * T
        K.wait("sync", xnb_ev[tile], prev["qkv_mm"])
        lxn = dma(xn_bf.rearrange("p (c t) -> p c t", c=NCH),
                  xnb_s[tile].transpose([1, 0, 2]))
        K.wait("tensor", lxn)
        qkv_ev = {}
        last_qkv_mm = None
        for name, Wsb, bias, dstbf in (("q", Wq, bqs, Qb), ("k", Wk, bks, Kb), ("v", Wv, bvs, Vf)):
            for oc in range(NCH):
                nm, ps = next_mm()
                K.wait("tensor", free[nm])
                if name == "q" and oc == 0:
                    K.wait("tensor", prev["scores"], prev["qk_seg"])
                if name == "v" and oc == 0:
                    K.wait("tensor", prev["wo_mm"])
                    K.wait("vector", prev["att_tr"])
                mev = None
                for kc in range(NCH):
                    mev = K.emit("tensor", lambda e, ps=ps, Wsb=Wsb, kc=kc, oc=oc: e.matmul(
                        ps, Wsb[:, kc * D + oc * 128: kc * D + (oc + 1) * 128],
                        xn_bf[:, kc * 512:(kc + 1) * 512],
                        start=(kc == 0), stop=(kc == NCH - 1)), sT)
                last_qkv_mm = mev
                K.wait("vector", mev)
                if name == "q" and oc == 0:
                    K.wait("vector", prev["qrep"])
                ev = K.emit("vector", lambda e, ps=ps, dstbf=dstbf, bias=bias, oc=oc: e.tensor_scalar(
                    out=dstbf[:, oc * 512:(oc + 1) * 512], in0=ps,
                    scalar1=bias[:, oc:oc + 1], scalar2=None, op0=ALU.add), sV)
                free[nm] = ev
                qkv_ev[(name, oc)] = ev
                pump(1)
        prev["qkv_mm"] = last_qkv_mm
        nm_d, ps_d = next_sc()
        K.wait("tensor", free[nm_d])
        dev = None
        for c in range(NCH):
            b = qk2[c % 2]
            K.wait("vector", qkv_ev[("q", c)], qkv_ev[("k", c)],
                   None if c < 2 else dev)
            qev = K.emit("vector", lambda e, b=b, c=c: e.tensor_tensor(
                out=b, in0=Qb[:, c * 512:(c + 1) * 512], in1=Kb[:, c * 512:(c + 1) * 512],
                op=ALU.mult), sV)
            K.wait("tensor", qev)
            dev = K.emit("tensor", lambda e, ps_d=ps_d, b=b, c=c: e.matmul(
                ps_d[0:16, 0:512], seg[:, c * 16:(c + 1) * 16], b,
                start=(c == 0), stop=(c == NCH - 1)), sT)
        prev["qk_seg"] = dev
        K.wait("scalar", dev, prev["dg"])
        numer_ev = K.emit("scalar", lambda e, ps_d=ps_d: e.activation(
            out=numer, in_=ps_d[0:16, 0:512], func=AF.Exp, scale=EXPSC), sS)
        free[nm_d] = numer_ev
        all_qk = [qkv_ev[(n, oc)] for n in ("q", "k") for oc in range(NCH)]
        last_sc = None
        grevs = [[], []]
        for hl in range(2):
            Qrep = Qrep2[hl]
            K.wait("sync", *all_qk, prev.get(f"qrep_free{hl}"))
            for dhalf in range(2):
                for gpar in range(2):
                    grevs[hl].append(K.emit("sync", lambda e, Qrep=Qrep, dhalf=dhalf, gpar=gpar, hl=hl: e.dma_start(
                        out=Qrep[dhalf * 64:(dhalf + 1) * 64, :]
                            .rearrange("p (k t) -> p k t", t=512)[:, :, gpar * 256:(gpar + 1) * 256],
                        in_=Qb[gpar * 64:(gpar + 1) * 64, :]
                            .rearrange("p (c t) -> p c t", t=512)[:, :, hl * 256:(hl + 1) * 256]),
                        sDma[dma_rr[0] % NSL], 16))
                    dma_rr[0] += 1
        prev["qrep"] = grevs[1][-1]
        for hl in range(2):
            Qrep = Qrep2[hl]
            dnpAll = dnpAll2[hl]
            dnp_free = prev.get(f"dnp_free{hl}")
            grev = grevs[hl]
            K.wait("tensor", *grev)
            first_exp = True
            for r in range(8):
                for v in range(2):
                    m = r * 2 + v
                    lhsA = Kb[0:64,
                              r * 512 + hl * 256 + v * 128: r * 512 + hl * 256 + (v + 1) * 128]
                    lhsB = Kb[64:128,
                              r * 512 + hl * 256 + v * 128: r * 512 + hl * 256 + (v + 1) * 128]
                    for grp in range(4):
                        nmA, psA = next_sc()
                        nmB, psB = next_sc()
                        K.wait("tensor", free[nmA], free[nmB])
                        sevA = sevB = None
                        for half in range(2):
                            a0 = grp * 1024 + half * 512
                            sevA = K.emit("tensor", lambda e, psA=psA, lhsA=lhsA, a0=a0, half=half, Qrep=Qrep: e.matmul(
                                psA[:, half * 512:(half + 1) * 512],
                                lhsA, Qrep[0:64, a0:a0 + 512],
                                start=True, stop=True), sT)
                            sevB = K.emit("tensor", lambda e, psB=psB, lhsB=lhsB, a0=a0, half=half, Qrep=Qrep: e.matmul(
                                psB[:, half * 512:(half + 1) * 512],
                                lhsB, Qrep[64:128, a0:a0 + 512],
                                start=True, stop=True), sT)
                        if first_exp:
                            K.wait("scalar", sevA, dnp_free)
                            first_exp = False
                        else:
                            K.wait("scalar", sevA)
                        eevA = K.emit("scalar", lambda e, psA=psA, m=m, grp=grp, dnpAll=dnpAll: e.activation(
                            out=dummy, in_=psA, func=AF.Exp, scale=EXPSC,
                            accum_out=dnpAll[:, m * 8 + grp: m * 8 + grp + 1]), sS)
                        free[nmA] = eevA
                        K.wait("scalar", sevB)
                        eevB = K.emit("scalar", lambda e, psB=psB, m=m, grp=grp, dnpAll=dnpAll: e.activation(
                            out=dummy, in_=psB, func=AF.Exp, scale=EXPSC,
                            accum_out=dnpAll[:, m * 8 + 4 + grp: m * 8 + 4 + grp + 1]), sS)
                        free[nmB] = eevB
                        pump(1)
                        vpump(1)
                    last_sc = sevB
            K.wait("vector", eevB)
            K.emit("vector", lambda e, hl=hl, dnpAll=dnpAll: e.reduce_sum(
                out=dn[:, hl * 32:(hl + 1) * 32].rearrange("p (r w) -> p r w", w=4)[:, :, 0:2],
                in_=dnpAll.rearrange("p (m g) -> p m g", g=8)[:, :, 0:4], axis=AX.X))
            prev[f"dnp_free{hl}"] = K.emit("vector", lambda e, hl=hl, dnpAll=dnpAll: e.reduce_sum(
                out=dn[:, hl * 32:(hl + 1) * 32].rearrange("p (r w) -> p r w", w=4)[:, :, 2:4],
                in_=dnpAll.rearrange("p (m g) -> p m g", g=8)[:, :, 4:8], axis=AX.X), sV)
            prev[f"qrep_free{hl}"] = last_sc
        prev["scores"] = last_sc
        vgap()
        rc_ev = K.emit("vector", lambda e: e.reciprocal(out=rec, in_=dn), sV)
        for hl in range(2):
            for v in range(2):
                nm, ps = next_ms()
                K.wait("tensor", rc_ev, free[nm])
                tev = K.emit("tensor", lambda e, ps=ps, hl=hl, v=v: e.transpose(
                    ps[0:16, 0:128],
                    rec[:, hl * 32:(hl + 1) * 32].rearrange("p (g w) -> p g w", w=2)[:, :, v],
                    idn), sT)
                K.wait("vector", tev)
                ev = K.emit("vector", lambda e, ps=ps, hl=hl, v=v: e.tensor_copy(
                    out=recD[:, hl * 256 + v * 128: hl * 256 + (v + 1) * 128],
                    in_=ps[0:16, 0:128]), sV)
                free[nm] = ev
        K.wait("vector", numer_ev)
        vgap()
        dg_ev = K.emit("vector", lambda e: e.tensor_tensor(
            out=Dg_bf, in0=numer, in1=recD, op=ALU.mult), sV)
        vgap()
        prev["dg"] = dg_ev
        dv_ev = []
        last_dv_mm = None
        for c in range(NCH):
            nm, ps = next_ms()
            K.wait("tensor", dg_ev, free[nm])
            bev = K.emit("tensor", lambda e, ps=ps, c=c: e.matmul(
                ps, sel_bf[:, c * 128:(c + 1) * 128], Dg_bf, start=True, stop=True), sT)
            last_dv_mm = bev
            K.wait("vector", bev, qkv_ev[("v", c)])
            ev = K.emit("vector", lambda e, ps=ps, c=c: e.tensor_tensor(
                out=DVb[:, c * 512:(c + 1) * 512], in0=ps,
                in1=Vf[:, c * 512:(c + 1) * 512], op=ALU.mult), sV)
            free[nm] = ev
            dv_ev.append(ev)
        prev["dv_mm"] = last_dv_mm
        at_ev = {}
        last_wo = None
        for oc in range(NCH):
            nm, ps = next_mm()
            K.wait("tensor", free[nm], *dv_ev)
            mev = None
            for kc in range(NCH):
                mev = K.emit("tensor", lambda e, ps=ps, kc=kc, oc=oc: e.matmul(
                    ps, Wo[:, kc * D + oc * 128: kc * D + (oc + 1) * 128],
                    DVb[:, kc * 512:(kc + 1) * 512],
                    start=(kc == 0), stop=(kc == NCH - 1)), sT)
            last_wo = mev
            K.wait("vector", mev)
            ev = K.emit("vector", lambda e, ps=ps, oc=oc: e.tensor_scalar(
                out=attn_f[:, oc * 512:(oc + 1) * 512], in0=ps,
                scalar1=bos[:, oc:oc + 1], scalar2=None, op0=ALU.add), sV)
            free[nm] = ev
            at_ev[oc] = ev
            pump(1)
        prev["wo_mm"] = last_wo
        last_tt = None
        acc_last = prev["acc_st"]
        for j in range(4):
            K.wait("sync", acc_last, last_tt)
            lxj = dma(x_j, x_in[t0 + j * 128: t0 + (j + 1) * 128, :])
            jt_ev = []
            for half in range(2):
                nm, ps = next_ms()
                K.wait("tensor", free[nm],
                       *[at_ev[half * 4 + c4] for c4 in range(4)])
                tev = None
                for c4 in range(4):
                    c = half * 4 + c4
                    tev = K.emit("tensor", lambda e, ps=ps, c=c, c4=c4, j=j: e.transpose(
                        ps[:, c4 * 128:(c4 + 1) * 128],
                        attn_f[:, c * 512 + j * 128: c * 512 + (j + 1) * 128], idn), sT)
                K.wait("vector", tev, lxj)
                ev = K.emit("vector", lambda e, ps=ps, half=half: e.tensor_tensor(
                    out=x_j[:, half * 512:(half + 1) * 512],
                    in0=ps, in1=x_j[:, half * 512:(half + 1) * 512],
                    op=ALU.add), sV)
                free[nm] = ev
                jt_ev.append(ev)
            last_tt = jt_ev[-1]
            K.wait("sync", *jt_ev)
            acc_last = dma(acc_s[tile, j], x_j)
            pump(1)
        prev["att_tr"] = last_tt
        acc_ev[tile] = acc_last
        prev["acc_st"] = acc_last
        pump(4)

        if tile == 3:
            pump(len(filler))
            emit_ar2()
            enqueue_C(0, NBLK // BPG)
        if tile == 5:
            vpump(len(vfiller))
            emit_ar3()
            enqueue_D(0, NBLK // BPG)
        if tile == 7:
            vpump(len(vfiller))
            emit_ar4()

    scopeCD.close()
    K.wait("sync", acc_ev[NT - 1], prev["scores"], prev["qk_seg"], prev["wo_mm"],
           prev["dg"], prev["att_tr"], prev["qrep"], prev["dv_mm"], prev["qkv_mm"])
    K.wait("vector", prev["scores"], prev["wo_mm"])
    scopeATT.close()
    prep.close()
    scopeW.close()
    Ae = lambda name, shape, dt: main.enter_context(nc.sbuf_tensor(name, shape, dt)).ap()
    o_ch2 = [Ae(f"o_ch{i}", [128, 2048], F32) for i in range(2)]
    oq_bf = Ae("oq_bf", [128, NFF * 512], BF16)
    WdT = Ae("WdT", [128, NFF * D], BF16)
    acc_rd2 = [Ae(f"acc_rd{i}", [128, 4096], F32) for i in range(2)]

    K.wait("sync", wready[("down", 0)])
    wd_ld = []
    for ic in range(NFF):
        wd_ld.append(dma(WdT[:, ic * D:(ic + 1) * D], wdownT_s[ic]))
    K.wait("vector", state["so_ev"], *wd_ld)
    wds_ev = K.emit("vector", lambda e: e.tensor_scalar(
        out=WdT, in0=WdT, scalar1=so_b, scalar2=None, op0=ALU.mult), sV)

    out_st = []
    bank_ap = [ps_sc[i][:, h * 512:(h + 1) * 512] for i in range(3) for h in range(2)] \
        + [ps_mm[0], ps_mm[1]]
    bank_free = [free["sc0"], free["sc0"], free["sc1"], free["sc1"],
                 free["sc2"], free["sc2"], free["mm0"], free["mm1"]]
    bank_last = [None] * 8
    och_user = [None, None]
    acc_user = [None, None]
    qdq_gate = None
    for tile in range(NT):
        acc_rd = acc_rd2[tile % 2]
        K.wait("sync", acc_ev[tile], acc_user[tile % 2])
        aev = dma(acc_rd.rearrange("p (j f) -> p j f", j=4),
                  acc_s[tile].transpose([1, 0, 2]))
        for ch in range(6):
            o_ch = o_ch2[ch % 2]
            K.wait("sync", state["o_st"], och_user[ch % 2])
            state["o_st"] = None
            lev = dma(o_ch.rearrange("p (c t) -> p c t", c=4),
                      o_s[tile, ch * 4:(ch + 1) * 4].transpose([1, 0, 2]))
            K.wait("vector", lev, state["mo_ev"], qdq_gate if ch == 0 else None)
            state["mo_ev"] = None
            K.emit("vector", lambda e, o_ch=o_ch: e.tensor_scalar(
                out=o_ch, in0=o_ch, scalar1=mo_b, scalar2=None, op0=ALU.mult))
            K.emit("vector", lambda e, o_ch=o_ch: e.tensor_scalar(
                out=o_ch, in0=o_ch, scalar1=MAGIC, scalar2=MAGIC, op0=ALU.add, op1=ALU.subtract))
            ev = K.emit("vector", lambda e, ch=ch, o_ch=o_ch: e.tensor_scalar(
                out=oq_bf[:, ch * 2048:(ch + 1) * 2048], in0=o_ch,
                scalar1=32767.0, scalar2=-32768.0, op0=ALU.min, op1=ALU.max), sV)
            och_user[ch % 2] = ev
            K.wait("tensor", ev, wds_ev if (tile == 0 and ch == 0) else None)
            for b in range(8):
                j, nf = b // 2, b % 2
                if ch == 0:
                    K.wait("tensor", bank_free[b])
                for ic2 in range(4):
                    ic = ch * 4 + ic2
                    bank_last[b] = K.emit("tensor", lambda e, b=b, ic=ic, j=j, nf=nf: e.matmul(
                        bank_ap[b], oq_bf[:, ic * 512 + j * 128: ic * 512 + (j + 1) * 128],
                        WdT[:, ic * D + nf * 512: ic * D + (nf + 1) * 512],
                        start=(ic == 0), stop=(ic == NFF - 1)), sT)
        qdq_gate = bank_last[7]
        ts_ev = []
        for b in range(8):
            j, nf = b // 2, b % 2
            K.wait("vector", bank_last[b], aev if b == 0 else None)
            ev = K.emit("vector", lambda e, b=b, j=j, nf=nf, acc_rd=acc_rd: e.tensor_tensor(
                out=acc_rd[:, j * 1024 + nf * 512: j * 1024 + (nf + 1) * 512],
                in0=bank_ap[b], in1=acc_rd[:, j * 1024 + nf * 512: j * 1024 + (nf + 1) * 512],
                op=ALU.add), sV)
            bank_free[b] = ev
            ts_ev.append(ev)
        K.wait("sync", *ts_ev)
        st = dma(out[tile * T:(tile + 1) * T, :].rearrange("(j p) f -> p j f", p=128),
                 acc_rd.rearrange("p (j f) -> p j f", j=4))
        out_st.append(st)
        acc_user[tile % 2] = st
    main.close()

    K.wait("sync", *out_st)
    K.run()
    es.close()
    return nc


def make_consts():
    ident = np.eye(128, dtype=np.float32)
    seg16 = np.zeros((128, NCH * 16), np.float32)
    for c in range(NCH):
        for p in range(128):
            seg16[p, c * 16 + 2 * c + p // 64] = 1.0
    sel64 = np.zeros((16, NCH * 128), np.float32)
    for c in range(NCH):
        for p in range(128):
            sel64[2 * c + p // 64, c * 128 + p] = 1.0
    ones_row = np.ones((1, 128), np.float32)
    return ident, seg16, sel64, ones_row


_NC_CACHE = {}


def kernel(L0_attn_X_embed, L0_attn_W_q, L0_attn_b_q, L0_attn_W_k, L0_attn_b_k,
           L0_attn_W_v, L0_attn_b_v, L0_attn_W_o, L0_attn_b_o,
           L0_ffn_w_gate, L0_ffn_w_up, L0_ffn_w_down):
    n_cores = 8
    if n_cores not in _NC_CACHE:
        _NC_CACHE[n_cores] = build_nc(n_cores)
    nc = _NC_CACHE[n_cores]
    ident, seg16, sel64, ones_row = make_consts()
    x = np.ascontiguousarray(np.asarray(L0_attn_X_embed, dtype=np.float32))
    common = dict(
        w_q=np.ascontiguousarray(np.asarray(L0_attn_W_q, np.float32)),
        b_q=np.ascontiguousarray(np.asarray(L0_attn_b_q, np.float32)),
        w_k=np.ascontiguousarray(np.asarray(L0_attn_W_k, np.float32)),
        b_k=np.ascontiguousarray(np.asarray(L0_attn_b_k, np.float32)),
        w_v=np.ascontiguousarray(np.asarray(L0_attn_W_v, np.float32)),
        b_v=np.ascontiguousarray(np.asarray(L0_attn_b_v, np.float32)),
        w_o=np.ascontiguousarray(np.asarray(L0_attn_W_o, np.float32)),
        b_o=np.ascontiguousarray(np.asarray(L0_attn_b_o, np.float32)),
        w_gate=np.ascontiguousarray(np.asarray(L0_ffn_w_gate, np.float32)),
        w_up=np.ascontiguousarray(np.asarray(L0_ffn_w_up, np.float32)),
        w_down=np.ascontiguousarray(np.asarray(L0_ffn_w_down, np.float32)),
        ident=ident, seg16=seg16, sel64=sel64, ones_row=ones_row,
    )
    in_maps = [dict(x_in=np.ascontiguousarray(x[i]), **common) for i in range(n_cores)]
    res = run_bass_kernel_spmd(nc, in_maps, core_ids=list(range(n_cores)))
    return np.stack([r["out"] for r in res.results], axis=0)


# version 19
# speedup vs baseline: 1.0534x; 1.0534x over previous
import numpy as np
from collections import deque
from contextlib import ExitStack

import concourse.bass as bass
import concourse.mybir as mybir
from concourse.bass_utils import run_bass_kernel_spmd

F32 = mybir.dt.float32
BF16 = mybir.dt.bfloat16
AF = mybir.ActivationFunctionType
ALU = mybir.AluOpType
AX = mybir.AxisListType

S = 4096
D = 1024
DFF = 3072
T = 512
NT = S // T
NCH = D // 128
NFF = DFF // 128
MAGIC = 12582912.0
EXPSC = float(np.float32(1.0) / np.float32(1024.0))


class Sched:
    ENGS = ("sync", "tensor", "vector", "scalar", "gpsimd")

    def __init__(self, nc):
        self.nc = nc
        self.q = {e: [] for e in self.ENGS}
        self.cnt = {}

    def emit(self, eng, fn, sem=None, k=1):
        ev = None
        if sem is not None:
            self.cnt[sem.name] = self.cnt.get(sem.name, 0) + k
            ev = (sem, self.cnt[sem.name])
            self.q[eng].append(lambda e, fn=fn, sem=sem, k=k: fn(e).then_inc(sem, k))
        else:
            self.q[eng].append(fn)
        return ev

    def wait(self, eng, *evs):
        best = {}
        for e in evs:
            if e is None:
                continue
            sem, val = e
            if best.get(sem.name, (None, -1))[1] < val:
                best[sem.name] = (sem, val)
        for sem, val in best.values():
            self.q[eng].append(lambda e, sem=sem, val=val: e.wait_ge(sem, val))

    def barrier(self, sB):
        self.cnt[sB.name] = self.cnt.get(sB.name, 0) + len(self.ENGS)
        tgt = self.cnt[sB.name]
        for eng in self.ENGS:
            self.q[eng].append(lambda e, sB=sB: e.nop().then_inc(sB, 1))
            self.q[eng].append(lambda e, sB=sB, tgt=tgt: e.wait_ge(sB, tgt))

    def run(self):
        with self.nc.Block() as block:
            block.sync(lambda e: [f(e) for f in self.q["sync"]])
            block.tensor(lambda e: [f(e) for f in self.q["tensor"]])
            block.vector(lambda e: [f(e) for f in self.q["vector"]])
            block.scalar(lambda e: [f(e) for f in self.q["scalar"]])
            block.gpsimd(lambda e: [f(e) for f in self.q["gpsimd"]])


def build_nc(n_cores=8):
    nc = bass.Bass("TRN2", target_bir_lowering=False, detect_race_conditions=False)

    x_in = nc.dram_tensor("x_in", [S, D], F32, kind="ExternalInput").ap()
    w_q = nc.dram_tensor("w_q", [D, D], F32, kind="ExternalInput").ap()
    b_q = nc.dram_tensor("b_q", [D], F32, kind="ExternalInput").ap()
    w_k = nc.dram_tensor("w_k", [D, D], F32, kind="ExternalInput").ap()
    b_k = nc.dram_tensor("b_k", [D], F32, kind="ExternalInput").ap()
    w_v = nc.dram_tensor("w_v", [D, D], F32, kind="ExternalInput").ap()
    b_v = nc.dram_tensor("b_v", [D], F32, kind="ExternalInput").ap()
    w_o = nc.dram_tensor("w_o", [D, D], F32, kind="ExternalInput").ap()
    b_o = nc.dram_tensor("b_o", [D], F32, kind="ExternalInput").ap()
    w_gate = nc.dram_tensor("w_gate", [DFF, D], F32, kind="ExternalInput").ap()
    w_up = nc.dram_tensor("w_up", [DFF, D], F32, kind="ExternalInput").ap()
    w_down = nc.dram_tensor("w_down", [D, DFF], F32, kind="ExternalInput").ap()
    ident = nc.dram_tensor("ident", [128, 128], F32, kind="ExternalInput").ap()
    seg16 = nc.dram_tensor("seg16", [128, NCH * 16], F32, kind="ExternalInput").ap()
    sel64 = nc.dram_tensor("sel64", [16, NCH * 128], F32, kind="ExternalInput").ap()
    ones_row = nc.dram_tensor("ones_row", [1, 128], F32, kind="ExternalInput").ap()

    out = nc.dram_tensor("out", [S, D], F32, kind="ExternalOutput").ap()

    xn_s = nc.dram_tensor("xn_s", [NT, NCH, 128, T], F32, kind="Internal").ap()
    xnb_s = nc.dram_tensor("xnb_s", [NT, NCH, 128, T], BF16, kind="Internal").ap()
    acc_s = nc.dram_tensor("acc_s", [NT, 4, 128, D], F32, kind="Internal").ap()
    up_s = nc.dram_tensor("up_s", [NT, NFF, 128, T], F32, kind="Internal").ap()
    gate_s = nc.dram_tensor("gate_s", [NT, NFF, 128, T], F32, kind="Internal").ap()
    t_s = nc.dram_tensor("t_s", [NT, NFF, 128, T], F32, kind="Internal").ap()
    o_s = nc.dram_tensor("o_s", [NT, NFF, 128, T], F32, kind="Internal").ap()
    wupT_s = nc.dram_tensor("wupT_s", [NFF, NCH, 128, 128], BF16, kind="Internal").ap()
    wgateT_s = nc.dram_tensor("wgateT_s", [NFF, NCH, 128, 128], BF16, kind="Internal").ap()
    wdownT_s = nc.dram_tensor("wdownT_s", [NFF, 128, D], BF16, kind="Internal").ap()
    cc_in = [nc.dram_tensor(f"cc_in{i}", [1, 16], F32, kind="Internal").ap() for i in range(4)]
    cc_out = [nc.dram_tensor(f"cc_out{i}", [1, 16], F32, kind="Internal").ap() for i in range(4)]
    xp_s = nc.dram_tensor("xp_s", [128, 1], F32, kind="Internal").ap()

    ps_sc = [nc.alloc_psum_tensor(f"ps_sc{i}", [128, 1024], F32).ap() for i in range(3)]
    ps_mm = [nc.alloc_psum_tensor(f"ps_mm{i}", [128, 512], F32).ap() for i in range(2)]

    es = ExitStack()
    NSL = 28
    sDma = [es.enter_context(nc.semaphore(f"sD{i}")) for i in range(NSL)]
    sT = es.enter_context(nc.semaphore("sT"))
    sV = es.enter_context(nc.semaphore("sV"))
    sS = es.enter_context(nc.semaphore("sS"))
    sG = es.enter_context(nc.semaphore("sG"))
    sC = es.enter_context(nc.semaphore("sC"))
    sB = es.enter_context(nc.semaphore("sB"))

    K = Sched(nc)
    A = nc.alloc_sbuf_tensor

    idn = A("idn", [128, 128], F32).ap()
    idn_bf = A("idn_bf", [128, 128], BF16).ap()
    seg = A("seg", [128, NCH * 16], F32).ap()
    sel_bf = A("sel_bf", [16, NCH * 128], BF16).ap()
    ones1 = A("ones1", [1, 128], F32).ap()
    bqs = A("bqs", [128, NCH], F32).ap()
    bks = A("bks", [128, NCH], F32).ap()
    bvs = A("bvs", [128, NCH], F32).ap()
    bos = A("bos", [128, NCH], F32).ap()
    mx_xn = A("mx_xn", [128, 1], F32).ap()
    mx_up = A("mx_up", [128, 1], F32).ap()
    mx_gate = A("mx_gate", [128, 1], F32).ap()
    mx_t = A("mx_t", [128, 1], F32).ap()
    mx_o = A("mx_o", [128, 1], F32).ap()
    mxrow = A("mxrow", [1, 128], F32).ap()
    scals = A("scals", [1, 16], F32).ap()
    rx_b = A("rx_b", [128, 1], F32).ap()
    mup_b = A("mup_b", [128, 1], F32).ap()
    mg_b = A("mg_b", [128, 1], F32).ap()
    sgh_b = A("sgh_b", [128, 1], F32).ap()
    mt_b = A("mt_b", [128, 1], F32).ap()
    mo_b = A("mo_b", [128, 1], F32).ap()
    so_b = A("so_b", [128, 1], F32).ap()
    dummy = A("dump_exp", [128, 1024], BF16).ap()
    junkA = A("junkA", [128, 64], F32).ap()
    junkB = A("junkB", [128, 64], F32).ap()
    K.emit("vector", lambda e: e.memset(junkA, 0.0))

    def vgap(n=1):
        for _ in range(n):
            K.emit("vector", lambda e: e.tensor_copy(out=junkB, in_=junkA))

    dma_rr = [0]

    def dma(ap_out, ap_in, eng="sync", slow=False):
        sem = sDma[dma_rr[0] % NSL]
        dma_rr[0] += 1
        prev = K.cnt.get(sem.name, 0)
        if prev:
            K.q[eng].append(lambda e, sem=sem, prev=prev: e.wait_ge(sem, prev))
        return K.emit(eng, lambda e, o=ap_out, i=ap_in, s=slow: e.dma_start(
            out=o, in_=i, allow_slow_non_contiguous=s), sem, 16)

    selbf_ev = K.emit("gpsimd", lambda e: e.dma_start(out=sel_bf, in_=sel64), sG, 16)
    ld_ev = [dma(idn, ident), dma(seg, seg16), dma(ones1, ones_row),
             dma(bqs, b_q.rearrange("(c p) -> p c", p=128), slow=True),
             dma(bks, b_k.rearrange("(c p) -> p c", p=128), slow=True),
             dma(bvs, b_v.rearrange("(c p) -> p c", p=128), slow=True),
             dma(bos, b_o.rearrange("(c p) -> p c", p=128), slow=True)]
    K.emit("scalar", lambda e: e.preload_act_table(AF.Exp))
    for t in (mx_xn, mx_up, mx_gate, mx_t, mx_o, scals):
        K.emit("vector", lambda e, t=t: e.memset(t, 0.0))
    K.wait("vector", *ld_ev)
    idnbf_ev = K.emit("vector", lambda e: e.tensor_copy(out=idn_bf, in_=idn), sV)

    free = {"sc0": None, "sc1": None, "sc2": None, "mm0": None, "mm1": None}
    mmi = [0]
    sci = [0]

    def next_mm():
        i = mmi[0] % 2
        mmi[0] += 1
        return (f"mm{i}", ps_mm[i])

    next_ms = next_mm

    def next_sc():
        i = sci[0] % 3
        sci[0] += 1
        return (f"sc{i}", ps_sc[i])

    scopeW = ExitStack()
    Wq = scopeW.enter_context(nc.sbuf_tensor("Wq", [128, NCH * D], BF16)).ap()
    Wk = scopeW.enter_context(nc.sbuf_tensor("Wk", [128, NCH * D], BF16)).ap()
    Wv = scopeW.enter_context(nc.sbuf_tensor("Wv", [128, NCH * D], BF16)).ap()
    Wo = scopeW.enter_context(nc.sbuf_tensor("Wo", [128, NCH * D], BF16)).ap()

    prep = ExitStack()
    Aw = lambda name, shape, dt: prep.enter_context(nc.sbuf_tensor(name, shape, dt)).ap()
    NPB = 2
    wst = [Aw(f"wst{i}", [128, 1024], F32) for i in range(NPB)]
    wlp = Aw("wlp", [128, 1024], F32)
    bmax = Aw("bmax", [128, 32], F32)
    brec = Aw("brec", [128, 32], F32)
    wbf = [Aw(f"wbf{i}", [128, 1024], BF16) for i in range(NPB)]
    wtrb = [Aw(f"wtrb{i}", [128, 1024], BF16) for i in range(NPB)]

    wst_free = [None] * NPB
    for wi, (wdr, wsb) in enumerate(((w_q, Wq), (w_k, Wk), (w_v, Wv), (w_o, Wo))):
        for kc in range(NCH):
            i2 = (wi * NCH + kc) % NPB
            K.wait("sync", wst_free[i2])
            lev = dma(wst[i2][:, 0:D], wdr[kc * 128:(kc + 1) * 128, :])
            K.wait("vector", lev)
            wst_free[i2] = K.emit(
                "vector", lambda e, i2=i2, wsb=wsb, kc=kc: e.tensor_copy(
                    out=wsb[:, kc * D:(kc + 1) * D], in_=wst[i2][:, 0:D]), sV)
    qkvo_ready = wst_free[:]

    wbf_free = [None] * NPB
    wtrb_free = [None] * NPB
    prep_stores = []

    def lpbq_chunk(wdr, oc, n_in, dst_ap_fn, ci, c0=0):
        i2 = ci % NPB
        nb = n_in // 32
        nbk = n_in // 128
        K.wait("sync", wst_free[i2])
        lev = dma(wst[i2][:, 0:n_in], wdr[oc * 128:(oc + 1) * 128, c0:c0 + n_in])
        K.wait("vector", lev, wbf_free[i2])
        K.emit("vector", lambda e: e.reduce_max(
            out=bmax[:, 0:nb], in_=wst[i2][:, 0:n_in].rearrange("p (b t) -> p b t", b=nb),
            axis=AX.X, apply_absolute_value=True))
        vgap()
        K.emit("vector", lambda e: e.tensor_scalar(
            out=bmax[:, 0:nb], in0=bmax[:, 0:nb], scalar1=1.0 / 7.0, scalar2=1e-12,
            op0=ALU.mult, op1=ALU.max))
        vgap()
        K.emit("vector", lambda e: e.reciprocal(out=brec[:, 0:nb], in_=bmax[:, 0:nb]))
        vgap()
        wst_free[i2] = K.emit("vector", lambda e: e.tensor_tensor(
            out=wlp[:, 0:n_in].rearrange("p (b t) -> p b t", b=nb),
            in0=wst[i2][:, 0:n_in].rearrange("p (b t) -> p b t", b=nb),
            in1=brec[:, 0:nb].rearrange("p (b o) -> p b o", o=1).broadcast_to([128, nb, 32]),
            op=ALU.mult), sV)
        K.emit("vector", lambda e: e.tensor_scalar(
            out=wlp[:, 0:n_in], in0=wlp[:, 0:n_in], scalar1=MAGIC, scalar2=MAGIC,
            op0=ALU.add, op1=ALU.subtract))
        K.emit("vector", lambda e: e.tensor_scalar(
            out=wlp[:, 0:n_in], in0=wlp[:, 0:n_in], scalar1=7.0, scalar2=-8.0,
            op0=ALU.min, op1=ALU.max))
        cast_ev = K.emit("vector", lambda e: e.tensor_tensor(
            out=wbf[i2][:, 0:n_in].rearrange("p (b t) -> p b t", b=nb),
            in0=wlp[:, 0:n_in].rearrange("p (b t) -> p b t", b=nb),
            in1=bmax[:, 0:nb].rearrange("p (b o) -> p b o", o=1).broadcast_to([128, nb, 32]),
            op=ALU.mult), sV)
        cpevs = []
        tev = None
        for ic in range(nbk):
            nm, ps = next_mm()
            K.wait("tensor", cast_ev, free[nm], idnbf_ev)
            tev = K.emit("tensor", lambda e, ps=ps, i2=i2, ic=ic: e.transpose(
                ps.bitcast(BF16)[:, 0:128], wbf[i2][:, ic * 128:(ic + 1) * 128], idn_bf), sT)
            K.wait("vector", tev, wtrb_free[i2] if ic == 0 else None)
            cpev = K.emit("vector", lambda e, ps=ps, i2=i2, ic=ic: e.tensor_copy(
                out=wtrb[i2][:, ic * 128:(ic + 1) * 128], in_=ps.bitcast(BF16)[:, 0:128]), sV)
            free[nm] = cpev
            cpevs.append(cpev)
        wbf_free[i2] = tev
        K.wait("sync", *cpevs)
        sev = dma(dst_ap_fn(oc), wtrb[i2][:, 0:nbk * 128].rearrange("p (b f) -> p b f", f=128))
        wtrb_free[i2] = sev
        prep_stores.append(sev)

    wready = {}
    lpbq_thunks = deque()
    ci = 0
    for oc in range(NFF):
        lpbq_thunks.append(lambda oc=oc, ci=ci: lpbq_chunk(
            w_up, oc, D, lambda oc: wupT_s[oc].transpose([1, 0, 2]), ci,
        )); ci += 1
        lpbq_thunks.append(("mark", ("up", oc)))
    for oc in range(NFF):
        lpbq_thunks.append(lambda oc=oc, ci=ci: lpbq_chunk(
            w_gate, oc, D, lambda oc: wgateT_s[oc].transpose([1, 0, 2]), ci,
        )); ci += 1
        lpbq_thunks.append(("mark", ("gate", oc)))
    for oc in range(NCH):
        for sub in range(3):
            lpbq_thunks.append(lambda oc=oc, ci=ci, sub=sub: lpbq_chunk(
                w_down, oc, D,
                lambda oc, sub=sub: wdownT_s[sub * 8:(sub + 1) * 8, :,
                                             oc * 128:(oc + 1) * 128].transpose([1, 0, 2]),
                ci, c0=sub * D)); ci += 1
    lpbq_thunks.append(("mark", ("down", 0)))

    pp = ExitStack()
    Ap = lambda name, shape, dt: pp.enter_context(nc.sbuf_tensor(name, shape, dt)).ap()
    px2 = [Ap("px0", [128, 4096], F32)] * 2
    pxn = Ap("pxn", [128, 4096], F32)
    pfm = Ap("pfm", [128, 4096], F32)
    psq = pfm
    pbf = Ap("pbf", [128, 4096], BF16)
    stt = Ap("stt", [128, 48], F32)
    sum4 = stt[:, 0:4]; sq4 = stt[:, 4:8]; m4 = stt[:, 8:12]; a4 = stt[:, 12:16]
    y4 = stt[:, 16:20]; t4a = stt[:, 20:24]; e4 = stt[:, 28:32]; xm1 = stt[:, 32:33]

    xns_ev = [None] * NT
    xnb_ev = [None] * NT
    pp_st = []
    px_free = [None, None]
    lx_evs = [None] * NT
    for tile in range(NT):
        t0 = tile * T
        K.wait("sync", px_free[0], *pp_st[-2:])
        lx_evs[tile] = dma(px2[tile % 2].rearrange("p (j f) -> p j f", j=4),
                           x_in[t0:t0 + T, :].rearrange("(j p) f -> p j f", p=128))
        px = px2[tile % 2]
        lx = lx_evs[tile]
        K.wait("vector", lx)
        K.emit("vector", lambda e, px=px: e.reduce_sum(
            out=sum4, in_=px.rearrange("p (j f) -> p j f", j=4), axis=AX.X))
        K.emit("vector", lambda e, px=px: e.tensor_tensor(out=psq, in0=px, in1=px, op=ALU.mult))
        K.emit("vector", lambda e: e.reduce_sum(
            out=sq4, in_=psq.rearrange("p (j f) -> p j f", j=4), axis=AX.X))
        K.emit("vector", lambda e: e.tensor_scalar(
            out=m4, in0=sum4, scalar1=1.0 / D, scalar2=None, op0=ALU.mult))
        vgap()
        K.emit("vector", lambda e: e.tensor_scalar(
            out=a4, in0=sq4, scalar1=1.0 / D, scalar2=None, op0=ALU.mult))
        K.emit("vector", lambda e: e.tensor_tensor(out=e4, in0=m4, in1=m4, op=ALU.mult))
        vgap()
        K.emit("vector", lambda e: e.tensor_tensor(out=a4, in0=a4, in1=e4, op=ALU.subtract))
        vgap()
        K.emit("vector", lambda e: e.tensor_scalar(
            out=a4, in0=a4, scalar1=1e-5, scalar2=None, op0=ALU.add))
        vgap()
        K.emit("vector", lambda e: e.reciprocal(out=y4, in_=a4))
        vgap()
        for _ in range(6):
            K.emit("vector", lambda e: e.tensor_tensor(out=t4a, in0=y4, in1=y4, op=ALU.mult))
            vgap()
            K.emit("vector", lambda e: e.tensor_tensor(out=t4a, in0=t4a, in1=a4, op=ALU.mult))
            vgap()
            K.emit("vector", lambda e: e.tensor_scalar(
                out=t4a, in0=t4a, scalar1=-0.5, scalar2=1.5, op0=ALU.mult, op1=ALU.add))
            vgap()
            K.emit("vector", lambda e: e.tensor_tensor(out=y4, in0=y4, in1=t4a, op=ALU.mult))
            vgap()
        vxn = None
        for j in range(4):
            vxn = K.emit("vector", lambda e, j=j, px=px: e.tensor_scalar(
                out=pxn[:, j * 1024:(j + 1) * 1024], in0=px[:, j * 1024:(j + 1) * 1024],
                scalar1=m4[:, j:j + 1], scalar2=y4[:, j:j + 1],
                op0=ALU.subtract, op1=ALU.mult), sV)
        K.emit("vector", lambda e: e.reduce_max(
            out=xm1, in_=pxn, axis=AX.X, apply_absolute_value=True))
        vgap()
        K.emit("vector", lambda e: e.tensor_tensor(out=mx_xn, in0=mx_xn, in1=xm1, op=ALU.max))
        vfm = []
        for c in range(NCH):
            nm, ps = next_ms()
            K.wait("tensor", vxn, free[nm])
            tev = None
            for j in range(4):
                tev = K.emit("tensor", lambda e, ps=ps, j=j, c=c: e.transpose(
                    ps[:, j * 128:(j + 1) * 128],
                    pxn[:, j * 1024 + c * 128: j * 1024 + (c + 1) * 128], idn), sT)
            K.wait("vector", tev)
            K.emit("vector", lambda e, ps=ps, c=c: e.tensor_copy(
                out=pfm[:, c * 512:(c + 1) * 512], in_=ps))
            ev = K.emit("vector", lambda e, ps=ps, c=c: e.tensor_copy(
                out=pbf[:, c * 512:(c + 1) * 512], in_=ps), sV)
            free[nm] = ev
            vfm.append(ev)
        px_free[0] = vxn
        K.wait("sync", *vfm)
        xns_ev[tile] = dma(xn_s[tile].transpose([1, 0, 2]),
                           pfm.rearrange("p (c t) -> p c t", c=NCH))
        xnb_ev[tile] = dma(xnb_s[tile].transpose([1, 0, 2]),
                           pbf.rearrange("p (c t) -> p c t", c=NCH))
        pp_st = [xns_ev[tile], xnb_ev[tile]]
    pp.close()
    K.barrier(sB)

    def allreduce_max(idx, srcs, n_vals, wait_evs=()):
        pk = []
        for i, src in enumerate(srcs):
            K.wait("sync", *wait_evs)
            ev0 = dma(xp_s, src)
            K.wait("sync", ev0)
            ev = dma(mxrow[0:1, 0:128], xp_s.rearrange("p o -> o p"))
            K.wait("vector", ev)
            pk.append(K.emit("vector", lambda e, i=i: e.reduce_max(
                out=scals[0:1, i:i + 1], in_=mxrow, axis=AX.X), sV))
            vgap()
        z = K.emit("vector", lambda e, n_vals=n_vals: e.memset(scals[0:1, n_vals:8], 0.0), sV)
        K.wait("gpsimd", z, *pk)
        gev = K.emit("gpsimd", lambda e, idx=idx: e.dma_start(
            out=cc_in[idx], in_=scals[0:1, 0:16]), sG, 16)
        K.wait("gpsimd", gev)
        cev = K.emit("gpsimd", lambda e, idx=idx: e.collective_compute(
            "AllReduce", ALU.max, replica_groups=[list(range(n_cores))],
            ins=[cc_in[idx]], outs=[cc_out[idx]]), sC, 1)
        K.wait("gpsimd", cev)
        return K.emit("gpsimd", lambda e, idx=idx: e.dma_start(
            out=scals[0:1, 0:16], in_=cc_out[idx]), sG, 16)

    def bcast(dst, src_col, wait_ev):
        nm, ps = next_ms()
        K.wait("tensor", wait_ev, free[nm])
        mev = K.emit("tensor", lambda e, ps=ps, src_col=src_col: e.matmul(
            ps[:, 0:1], ones1, scals[0:1, src_col:src_col + 1], start=True, stop=True), sT)
        K.wait("vector", mev)
        ev = K.emit("vector", lambda e, ps=ps, dst=dst: e.tensor_copy(out=dst, in_=ps[:, 0:1]), sV)
        vgap()
        free[nm] = ev
        return ev

    ar1 = allreduce_max(0, [mx_xn], 1)
    K.wait("vector", ar1)
    K.emit("vector", lambda e: e.tensor_scalar(
        out=scals[0:1, 8:9], in0=scals[0:1, 0:1], scalar1=1.0 / 32767.0, scalar2=1e-12,
        op0=ALU.mult, op1=ALU.max))
    vgap()
    sx_ev = K.emit("vector", lambda e: e.reciprocal(out=scals[0:1, 9:10], in_=scals[0:1, 8:9]), sV)
    vgap()
    rx_ev = bcast(rx_b, 9, sx_ev)

    main = ExitStack()
    scopeATT = ExitStack()
    Am = lambda name, shape, dt: scopeATT.enter_context(nc.sbuf_tensor(name, shape, dt)).ap()
    x_j = Am("x_j", [128, 1024], F32)
    xn_bf = Am("xn_bf", [128, 4096], BF16)
    Qb = Am("Qb", [128, 4096], BF16)
    Kb = Am("Kb", [128, 4096], BF16)
    Vf = Am("Vf", [128, 4096], F32)
    attn_f = Vf
    Qrep2 = [Am(f"Qrep{i}", [128, 4096], BF16) for i in range(2)]
    qk2 = [Am(f"qk2_{i}", [128, 512], F32) for i in range(2)]
    DVb = Am("DVb", [128, 4096], BF16)
    dn = Am("dn", [128, 64], F32)
    dnpAll2 = [Am(f"dnpAll{i}", [128, 128], F32) for i in range(2)]
    rec = Am("rec", [128, 64], F32)
    numer = Am("numer", [16, 512], F32)
    recD = Am("recD", [16, 512], F32)
    Dg_bf = Am("Dg_bf", [16, 512], BF16)
    scopeB = ExitStack()
    scopeCD = ExitStack()
    Ab = lambda name, shape, dt: scopeB.enter_context(nc.sbuf_tensor(name, shape, dt)).ap()
    xn_rd = Ab("xn_rd", [128, 2048], F32)
    xnq = Ab("xnq", [128, 4096], BF16)
    NWUB = 7
    wub = [Ab(f"wub{i}", [128, NCH * 128], BF16) for i in range(NWUB)]
    spl = [Ab(f"spl{i}", [128, 512], F32) for i in range(3)]
    splmx = Ab("splmx", [128, 1], F32)

    bstate = {"tile": -1, "xnq_ev": None, "xnq_users": [], "xn_rd_user": None}
    wub_free = [None] * NWUB
    wub_ld = [None] * NWUB
    spl_free = [None] * 3
    wubi = [0]
    bdone_evs = []
    bchain_list = []

    def prefetch_wchain():
        i = wubi[0]
        if i >= len(bchain_list):
            return
        wubi[0] += 1
        tile, which, oc = bchain_list[i]
        wsel = i % NWUB
        w_src = wupT_s if which == 0 else wgateT_s
        wkey = ("up" if which == 0 else "gate", oc)
        K.wait("sync", wub_free[wsel], wready[wkey])
        wub_ld[wsel] = dma(wub[wsel].rearrange("p (c f) -> p c f", c=NCH),
                           w_src[oc].transpose([1, 0, 2]))

    def prep_xnq(tile):
        qev = None
        for h in range(2):
            K.wait("sync", xns_ev[tile],
                   *(bstate["xnq_users"] if h == 0 else []), bstate["xn_rd_user"])
            lev = dma(xn_rd.rearrange("p (c t) -> p c t", c=NCH // 2),
                      xn_s[tile, h * 4:(h + 1) * 4].transpose([1, 0, 2]))
            K.wait("vector", lev, rx_ev)
            K.emit("vector", lambda e: e.tensor_scalar(
                out=xn_rd, in0=xn_rd, scalar1=rx_b, scalar2=None, op0=ALU.mult))
            K.emit("vector", lambda e: e.tensor_scalar(
                out=xn_rd, in0=xn_rd, scalar1=MAGIC, scalar2=MAGIC,
                op0=ALU.add, op1=ALU.subtract))
            qev = K.emit("vector", lambda e, h=h: e.tensor_scalar(
                out=xnq[:, h * 2048:(h + 1) * 2048], in0=xn_rd,
                scalar1=32767.0, scalar2=-32768.0,
                op0=ALU.min, op1=ALU.max), sV)
            bstate["xn_rd_user"] = qev
        bstate["tile"] = tile
        bstate["xnq_ev"] = qev
        bstate["xnq_users"] = []

    def b_chain(ci, tile, which, oc):
        if bstate["tile"] != tile:
            prep_xnq(tile)
        while wubi[0] < min(ci + NWUB, len(bchain_list)):
            prefetch_wchain()
        wsel = ci % NWUB
        wev = wub_ld[wsel]
        dst_s = up_s if which == 0 else gate_s
        mxacc = mx_up if which == 0 else mx_gate
        nm, ps = next_mm()
        K.wait("tensor", free[nm], wev, bstate["xnq_ev"])
        mev = None
        for ic in range(NCH):
            mev = K.emit("tensor", lambda e, ps=ps, wsel=wsel, ic=ic: e.matmul(
                ps, wub[wsel][:, ic * 128:(ic + 1) * 128],
                xnq[:, ic * 512:(ic + 1) * 512],
                start=(ic == 0), stop=(ic == NCH - 1)), sT)
        wub_free[wsel] = mev
        bstate["xnq_users"].append(mev)
        ssel = (which * NFF + oc) % 3
        sb = spl[ssel]
        K.wait("vector", mev, spl_free[ssel])
        ev = K.emit("vector", lambda e, ps=ps, sb=sb: e.tensor_copy(out=sb, in_=ps), sV)
        free[nm] = ev
        K.emit("vector", lambda e, sb=sb: e.reduce_max(
            out=splmx, in_=sb, axis=AX.X, apply_absolute_value=True))
        vgap()
        mxev = K.emit("vector", lambda e, mxacc=mxacc: e.tensor_tensor(
            out=mxacc, in0=mxacc, in1=splmx, op=ALU.max), sV)
        K.wait("sync", ev)
        spl_free[ssel] = dma(dst_s[tile, oc], sb)
        bdone_evs.append(mxev)

    for tile in range(NT):
        for oc in range(NFF):
            bchain_list.append((tile, 0, oc))
            bchain_list.append((tile, 1, oc))
    filler = deque()
    for ci, (tile, which, oc) in enumerate(bchain_list):
        filler.append(lambda ci=ci, tile=tile, which=which, oc=oc: b_chain(ci, tile, which, oc))

    cur_tile = [0]

    def pump(n):
        if cur_tile[0] == 0:
            return
        for _ in range(n):
            if filler:
                filler.popleft()()

    cd = {}

    NBLK = NT * NFF
    BPG = 3
    gate_flat = gate_s.rearrange("n c p t -> (n c) p t")
    t_flat = t_s.rearrange("n c p t -> (n c) p t")
    up_flat = up_s.rearrange("n c p t -> (n c) p t")
    o_flat = o_s.rearrange("n c p t -> (n c) p t")

    state = {"t_st": None, "last_tev": None, "o_st": None, "last_oev": None,
             "mup_ev": None, "mg_ev": None, "sgh_ev": None, "mt_ev": None,
             "mo_ev": None, "so_ev": None}

    def emit_ar2():
        scopeB.close()
        Ac = lambda name, shape, dt: scopeCD.enter_context(nc.sbuf_tensor(name, shape, dt)).ap()
        cd["g_rd"] = [Ac(f"g_rd{p}", [128, BPG * 512], F32) for p in range(2)]
        cd["sig_f"] = [Ac(f"sig_f{p}", [128, BPG * 512], F32) for p in range(2)]
        cd["t_wr"] = [Ac(f"t_wr{p}", [128, BPG * 512], F32) for p in range(2)]
        cd["cmx"] = Ac("cmx", [128, 1], F32)
        cd["bdone"] = (bstate["xnq_users"][-1] if bstate["xnq_users"] else None,
                       wub_free[0], wub_free[1], wub_free[2], wub_free[3])
        ar2 = allreduce_max(1, [mx_up, mx_gate], 2, wait_evs=spl_free)
        K.wait("vector", ar2)
        K.emit("vector", lambda e: e.tensor_tensor(
            out=scals[0:1, 0:2], in0=scals[0:1, 0:2],
            in1=scals[0:1, 8:9].broadcast_to([1, 2]), op=ALU.mult))
        vgap()
        K.emit("vector", lambda e: e.tensor_scalar(
            out=scals[0:1, 0:2], in0=scals[0:1, 0:2], scalar1=1.0 / 32767.0, scalar2=1e-12,
            op0=ALU.mult, op1=ALU.max))
        vgap()
        K.emit("vector", lambda e: e.tensor_copy(out=scals[0:1, 10:12], in_=scals[0:1, 0:2]))
        K.emit("vector", lambda e: e.reciprocal(out=scals[0:1, 2:4], in_=scals[0:1, 0:2]))
        vgap()
        K.emit("vector", lambda e: e.tensor_tensor(
            out=scals[0:1, 2:4], in0=scals[0:1, 2:4],
            in1=scals[0:1, 8:9].broadcast_to([1, 2]), op=ALU.mult))
        vgap()
        K.emit("vector", lambda e: e.tensor_scalar(
            out=scals[0:1, 4:5], in0=scals[0:1, 11:12], scalar1=0.5, scalar2=None, op0=ALU.mult))
        vgap()
        b2 = K.emit("vector", lambda e: e.tensor_scalar(
            out=scals[0:1, 12:13], in0=scals[0:1, 11:12], scalar1=1.0 / 65536.0, scalar2=None,
            op0=ALU.mult), sV)
        vgap()
        state["mup_ev"] = bcast(mup_b, 2, b2)
        state["mg_ev"] = bcast(mg_b, 3, state["mup_ev"])
        state["sgh_ev"] = bcast(sgh_b, 4, state["mg_ev"])

    vfiller = deque()

    def vpump(n):
        for _ in range(n):
            while vfiller and isinstance(vfiller[0], tuple):
                _, key = vfiller.popleft()
                wready[key] = prep_stores[-1]
            if vfiller:
                vfiller.popleft()()
        while vfiller and isinstance(vfiller[0], tuple):
            _, key = vfiller.popleft()
            wready[key] = prep_stores[-1]

    vfiller.extend(lpbq_thunks)
    lpbq_thunks = None

    def enqueue_C(g0, g1):
        cmx = cd["cmx"]
        guser = [None, None]
        tst = [None, None]

        for g in range(g0, g1):
            blk = g * BPG
            p = g % 2
            g_rd, sig_f, t_wr = cd["g_rd"][p], cd["sig_f"][p], cd["t_wr"][p]

            def step_load(g=g, p=p, g_rd=g_rd, blk=blk):
                K.wait("sync", guser[p])
                if g == 0:
                    K.wait("tensor", *cd["bdone"])
                    K.wait("sync", *cd["bdone"], *spl_free)
                cd[("lev", p)] = dma(g_rd.rearrange("p (b t) -> p b t", b=BPG),
                                     gate_flat[blk:blk + BPG].transpose([1, 0, 2]))
            vfiller.append(step_load)

            def c1(p=p, g_rd=g_rd):
                K.wait("vector", cd[("lev", p)], state["mg_ev"], state["sgh_ev"])
                K.emit("vector", lambda e, g_rd=g_rd: e.tensor_scalar(
                    out=g_rd, in0=g_rd, scalar1=mg_b, scalar2=None, op0=ALU.mult))
            def c2(g_rd=g_rd):
                K.emit("vector", lambda e, g_rd=g_rd: e.tensor_scalar(
                    out=g_rd, in0=g_rd, scalar1=MAGIC, scalar2=MAGIC, op0=ALU.add, op1=ALU.subtract))
            def c3(p=p, g_rd=g_rd, sig_f=sig_f):
                gq_ev = K.emit("vector", lambda e, g_rd=g_rd: e.tensor_scalar(
                    out=g_rd, in0=g_rd, scalar1=32767.0, scalar2=-32768.0,
                    op0=ALU.min, op1=ALU.max), sV)
                K.wait("scalar", gq_ev, state["sgh_ev"])
                cd[("tn_ev", p)] = K.emit("scalar", lambda e, g_rd=g_rd, sig_f=sig_f: e.activation(
                    out=sig_f, in_=g_rd, func=AF.Tanh, scale=sgh_b), sS)
            def c4(p=p, sig_f=sig_f):
                K.wait("vector", cd[("tn_ev", p)])
                K.emit("vector", lambda e, sig_f=sig_f: e.tensor_scalar(
                    out=sig_f, in0=sig_f, scalar1=32768.0, scalar2=32768.0,
                    op0=ALU.mult, op1=ALU.add))
            def c5(sig_f=sig_f):
                K.emit("vector", lambda e, sig_f=sig_f: e.tensor_scalar(
                    out=sig_f, in0=sig_f, scalar1=MAGIC, scalar2=MAGIC,
                    op0=ALU.add, op1=ALU.subtract))
            def c6(sig_f=sig_f):
                K.emit("vector", lambda e, sig_f=sig_f: e.tensor_scalar(
                    out=sig_f, in0=sig_f, scalar1=65535.0, scalar2=0.0,
                    op0=ALU.min, op1=ALU.max))
            def c7(p=p, g_rd=g_rd, sig_f=sig_f, t_wr=t_wr):
                K.wait("vector", tst[p])
                t_ev = K.emit("vector", lambda e, g_rd=g_rd, sig_f=sig_f, t_wr=t_wr: e.tensor_tensor(
                    out=t_wr, in0=g_rd, in1=sig_f, op=ALU.mult), sV)
                state["last_tev"] = t_ev
                guser[p] = t_ev
            def c8(t_wr=t_wr):
                K.emit("vector", lambda e, t_wr=t_wr: e.reduce_max(
                    out=cmx, in_=t_wr, axis=AX.X, apply_absolute_value=True))
                vgap()
            def c9(p=p, blk=blk, t_wr=t_wr):
                K.emit("vector", lambda e: e.tensor_tensor(
                    out=mx_t, in0=mx_t, in1=cmx, op=ALU.max))
                K.wait("sync", state["last_tev"])
                tst[p] = dma(t_flat[blk:blk + BPG].transpose([1, 0, 2]),
                             t_wr.rearrange("p (b t) -> p b t", b=BPG))
                state["t_st"] = tst[p]
            for f in (c1, c2, c3, c4, c5, c6, c7, c8, c9):
                vfiller.append(f)

    def emit_ar3():
        ar3 = allreduce_max(2, [mx_t], 1)
        K.wait("vector", ar3)
        K.emit("vector", lambda e: e.tensor_tensor(
            out=scals[0:1, 0:1], in0=scals[0:1, 0:1], in1=scals[0:1, 12:13], op=ALU.mult))
        vgap()
        K.emit("vector", lambda e: e.tensor_scalar(
            out=scals[0:1, 13:14], in0=scals[0:1, 0:1], scalar1=1.0 / 32767.0, scalar2=1e-12,
            op0=ALU.mult, op1=ALU.max))
        vgap()
        K.emit("vector", lambda e: e.reciprocal(out=scals[0:1, 5:6], in_=scals[0:1, 13:14]))
        vgap()
        b3 = K.emit("vector", lambda e: e.tensor_tensor(
            out=scals[0:1, 5:6], in0=scals[0:1, 5:6], in1=scals[0:1, 12:13], op=ALU.mult), sV)
        vgap()
        state["mt_ev"] = bcast(mt_b, 5, b3)

    def enqueue_D(g0, g1):
        cmx = cd["cmx"]
        d2u = [None, None]
        ost = [None, None]

        for g in range(g0, g1):
            blk = g * BPG
            p = g % 2
            g_rd, sig_f, t_wr = cd["g_rd"][p], cd["sig_f"][p], cd["t_wr"][p]

            def d0(g=g, p=p, blk=blk, t_wr=t_wr, g_rd=g_rd):
                K.wait("sync", d2u[p], state["t_st"])
                cd[("lev1", p)] = dma(t_wr.rearrange("p (b t) -> p b t", b=BPG),
                                      t_flat[blk:blk + BPG].transpose([1, 0, 2]))
                cd[("lev2", p)] = dma(g_rd.rearrange("p (b t) -> p b t", b=BPG),
                                      up_flat[blk:blk + BPG].transpose([1, 0, 2]))
            vfiller.append(d0)

            def d1(p=p):
                K.wait("vector", cd[("lev1", p)], cd[("lev2", p)], state["mt_ev"],
                       state["mup_ev"])
            vfiller.append(d1)
            for buf, mb in ((t_wr, mt_b), (g_rd, mup_b)):
                vfiller.append(lambda buf=buf, mb=mb: K.emit(
                    "vector", lambda e, buf=buf, mb=mb: e.tensor_scalar(
                        out=buf, in0=buf, scalar1=mb, scalar2=None, op0=ALU.mult)))
                vfiller.append(lambda buf=buf: K.emit(
                    "vector", lambda e, buf=buf: e.tensor_scalar(
                        out=buf, in0=buf, scalar1=MAGIC, scalar2=MAGIC,
                        op0=ALU.add, op1=ALU.subtract)))
                vfiller.append(lambda buf=buf: K.emit(
                    "vector", lambda e, buf=buf: e.tensor_scalar(
                        out=buf, in0=buf, scalar1=32767.0, scalar2=-32768.0,
                        op0=ALU.min, op1=ALU.max)))
            def d2(p=p, g_rd=g_rd, sig_f=sig_f, t_wr=t_wr):
                K.wait("vector", ost[p])
                o_ev = K.emit("vector", lambda e, g_rd=g_rd, sig_f=sig_f, t_wr=t_wr: e.tensor_tensor(
                    out=sig_f, in0=t_wr, in1=g_rd, op=ALU.mult), sV)
                state["last_oev"] = o_ev
                d2u[p] = o_ev
            def d3(sig_f=sig_f):
                K.emit("vector", lambda e, sig_f=sig_f: e.reduce_max(
                    out=cmx, in_=sig_f, axis=AX.X, apply_absolute_value=True))
                vgap()
            def d4(p=p, blk=blk, sig_f=sig_f):
                K.emit("vector", lambda e: e.tensor_tensor(
                    out=mx_o, in0=mx_o, in1=cmx, op=ALU.max))
                K.wait("sync", state["last_oev"])
                ost[p] = dma(o_flat[blk:blk + BPG].transpose([1, 0, 2]),
                             sig_f.rearrange("p (b t) -> p b t", b=BPG))
                state["o_st"] = ost[p]
            for f in (d2, d3, d4):
                vfiller.append(f)

    def emit_ar4():
        ar4 = allreduce_max(3, [mx_o], 1)
        K.wait("vector", ar4)
        K.emit("vector", lambda e: e.tensor_tensor(
            out=scals[0:1, 6:7], in0=scals[0:1, 13:14], in1=scals[0:1, 10:11], op=ALU.mult))
        vgap()
        K.emit("vector", lambda e: e.tensor_tensor(
            out=scals[0:1, 0:1], in0=scals[0:1, 0:1], in1=scals[0:1, 6:7], op=ALU.mult))
        vgap()
        K.emit("vector", lambda e: e.tensor_scalar(
            out=scals[0:1, 14:15], in0=scals[0:1, 0:1], scalar1=1.0 / 32767.0, scalar2=1e-12,
            op0=ALU.mult, op1=ALU.max))
        vgap()
        K.emit("vector", lambda e: e.reciprocal(out=scals[0:1, 7:8], in_=scals[0:1, 14:15]))
        vgap()
        b4 = K.emit("vector", lambda e: e.tensor_tensor(
            out=scals[0:1, 7:8], in0=scals[0:1, 7:8], in1=scals[0:1, 6:7], op=ALU.mult), sV)
        vgap()
        state["mo_ev"] = bcast(mo_b, 7, b4)
        soc = K.emit("vector", lambda e: e.tensor_copy(
            out=scals[0:1, 15:16], in_=scals[0:1, 14:15]), sV)
        vgap()
        state["so_ev"] = bcast(so_b, 15, soc)


    acc_ev = [None] * NT
    K.wait("tensor", *ld_ev, *qkvo_ready, selbf_ev)
    K.wait("scalar", *ld_ev)

    prev = {"x_tm": None, "xn_bf": None, "qkv_mm": None, "qrep": None,
            "scores": None, "dv_mm": None, "wo_mm": None, "att_tr": None,
            "acc_st": None, "dg": None, "qk_seg": None}

    for tile in range(NT):
        cur_tile[0] = tile
        t0 = tile * T
        K.wait("sync", xnb_ev[tile], prev["qkv_mm"])
        lxn = dma(xn_bf.rearrange("p (c t) -> p c t", c=NCH),
                  xnb_s[tile].transpose([1, 0, 2]))
        K.wait("tensor", lxn)
        qkv_ev = {}
        last_qkv_mm = None
        for name, Wsb, bias, dstbf in (("q", Wq, bqs, Qb), ("k", Wk, bks, Kb), ("v", Wv, bvs, Vf)):
            for oc in range(NCH):
                nm, ps = next_mm()
                K.wait("tensor", free[nm])
                if name == "q" and oc == 0:
                    K.wait("tensor", prev["scores"], prev["qk_seg"])
                if name == "v" and oc == 0:
                    K.wait("tensor", prev["wo_mm"])
                    K.wait("vector", prev["att_tr"])
                mev = None
                for kc in range(NCH):
                    mev = K.emit("tensor", lambda e, ps=ps, Wsb=Wsb, kc=kc, oc=oc: e.matmul(
                        ps, Wsb[:, kc * D + oc * 128: kc * D + (oc + 1) * 128],
                        xn_bf[:, kc * 512:(kc + 1) * 512],
                        start=(kc == 0), stop=(kc == NCH - 1)), sT)
                last_qkv_mm = mev
                K.wait("vector", mev)
                if name == "q" and oc == 0:
                    K.wait("vector", prev["qrep"])
                ev = K.emit("vector", lambda e, ps=ps, dstbf=dstbf, bias=bias, oc=oc: e.tensor_scalar(
                    out=dstbf[:, oc * 512:(oc + 1) * 512], in0=ps,
                    scalar1=bias[:, oc:oc + 1], scalar2=None, op0=ALU.add), sV)
                free[nm] = ev
                qkv_ev[(name, oc)] = ev
                pump(1)
        prev["qkv_mm"] = last_qkv_mm
        nm_d, ps_d = next_sc()
        K.wait("tensor", free[nm_d])
        dev = None
        for c in range(NCH):
            b = qk2[c % 2]
            K.wait("vector", qkv_ev[("q", c)], qkv_ev[("k", c)],
                   None if c < 2 else dev)
            qev = K.emit("vector", lambda e, b=b, c=c: e.tensor_tensor(
                out=b, in0=Qb[:, c * 512:(c + 1) * 512], in1=Kb[:, c * 512:(c + 1) * 512],
                op=ALU.mult), sV)
            K.wait("tensor", qev)
            dev = K.emit("tensor", lambda e, ps_d=ps_d, b=b, c=c: e.matmul(
                ps_d[0:16, 0:512], seg[:, c * 16:(c + 1) * 16], b,
                start=(c == 0), stop=(c == NCH - 1)), sT)
        prev["qk_seg"] = dev
        K.wait("scalar", dev, prev["dg"])
        numer_ev = K.emit("scalar", lambda e, ps_d=ps_d: e.activation(
            out=numer, in_=ps_d[0:16, 0:512], func=AF.Exp, scale=EXPSC), sS)
        free[nm_d] = numer_ev
        all_qk = [qkv_ev[(n, oc)] for n in ("q", "k") for oc in range(NCH)]
        last_sc = None
        grevs = [[], []]
        for hl in range(2):
            Qrep = Qrep2[hl]
            K.wait("sync", *all_qk, prev.get(f"qrep_free{hl}"))
            for dhalf in range(2):
                for gpar in range(2):
                    grevs[hl].append(K.emit("sync", lambda e, Qrep=Qrep, dhalf=dhalf, gpar=gpar, hl=hl: e.dma_start(
                        out=Qrep[dhalf * 64:(dhalf + 1) * 64, :]
                            .rearrange("p (k t) -> p k t", t=512)[:, :, gpar * 256:(gpar + 1) * 256],
                        in_=Qb[gpar * 64:(gpar + 1) * 64, :]
                            .rearrange("p (c t) -> p c t", t=512)[:, :, hl * 256:(hl + 1) * 256]),
                        sDma[dma_rr[0] % NSL], 16))
                    dma_rr[0] += 1
        prev["qrep"] = grevs[1][-1]
        for hl in range(2):
            Qrep = Qrep2[hl]
            dnpAll = dnpAll2[hl]
            dnp_free = prev.get(f"dnp_free{hl}")
            grev = grevs[hl]
            K.wait("tensor", *grev)
            first_exp = True
            for r in range(8):
                for v in range(2):
                    m = r * 2 + v
                    lhsA = Kb[0:64,
                              r * 512 + hl * 256 + v * 128: r * 512 + hl * 256 + (v + 1) * 128]
                    lhsB = Kb[64:128,
                              r * 512 + hl * 256 + v * 128: r * 512 + hl * 256 + (v + 1) * 128]
                    for grp in range(4):
                        nmA, psA = next_sc()
                        nmB, psB = next_sc()
                        K.wait("tensor", free[nmA], free[nmB])
                        sevA = sevB = None
                        for half in range(2):
                            a0 = grp * 1024 + half * 512
                            sevA = K.emit("tensor", lambda e, psA=psA, lhsA=lhsA, a0=a0, half=half, Qrep=Qrep: e.matmul(
                                psA[:, half * 512:(half + 1) * 512],
                                lhsA, Qrep[0:64, a0:a0 + 512],
                                start=True, stop=True), sT)
                            sevB = K.emit("tensor", lambda e, psB=psB, lhsB=lhsB, a0=a0, half=half, Qrep=Qrep: e.matmul(
                                psB[:, half * 512:(half + 1) * 512],
                                lhsB, Qrep[64:128, a0:a0 + 512],
                                start=True, stop=True), sT)
                        if first_exp:
                            K.wait("scalar", sevA, dnp_free)
                            first_exp = False
                        else:
                            K.wait("scalar", sevA)
                        eevA = K.emit("scalar", lambda e, psA=psA, m=m, grp=grp, dnpAll=dnpAll: e.activation(
                            out=dummy, in_=psA, func=AF.Exp, scale=EXPSC,
                            accum_out=dnpAll[:, m * 8 + grp: m * 8 + grp + 1]), sS)
                        free[nmA] = eevA
                        K.wait("scalar", sevB)
                        eevB = K.emit("scalar", lambda e, psB=psB, m=m, grp=grp, dnpAll=dnpAll: e.activation(
                            out=dummy, in_=psB, func=AF.Exp, scale=EXPSC,
                            accum_out=dnpAll[:, m * 8 + 4 + grp: m * 8 + 4 + grp + 1]), sS)
                        free[nmB] = eevB
                        pump(1)
                        vpump(2)
                    last_sc = sevB
            K.wait("vector", eevB)
            K.emit("vector", lambda e, hl=hl, dnpAll=dnpAll: e.reduce_sum(
                out=dn[:, hl * 32:(hl + 1) * 32].rearrange("p (r w) -> p r w", w=4)[:, :, 0:2],
                in_=dnpAll.rearrange("p (m g) -> p m g", g=8)[:, :, 0:4], axis=AX.X))
            prev[f"dnp_free{hl}"] = K.emit("vector", lambda e, hl=hl, dnpAll=dnpAll: e.reduce_sum(
                out=dn[:, hl * 32:(hl + 1) * 32].rearrange("p (r w) -> p r w", w=4)[:, :, 2:4],
                in_=dnpAll.rearrange("p (m g) -> p m g", g=8)[:, :, 4:8], axis=AX.X), sV)
            prev[f"qrep_free{hl}"] = last_sc
        prev["scores"] = last_sc
        vgap()
        rc_ev = K.emit("vector", lambda e: e.reciprocal(out=rec, in_=dn), sV)
        for hl in range(2):
            for v in range(2):
                nm, ps = next_ms()
                K.wait("tensor", rc_ev, free[nm])
                tev = K.emit("tensor", lambda e, ps=ps, hl=hl, v=v: e.transpose(
                    ps[0:16, 0:128],
                    rec[:, hl * 32:(hl + 1) * 32].rearrange("p (g w) -> p g w", w=2)[:, :, v],
                    idn), sT)
                K.wait("vector", tev)
                ev = K.emit("vector", lambda e, ps=ps, hl=hl, v=v: e.tensor_copy(
                    out=recD[:, hl * 256 + v * 128: hl * 256 + (v + 1) * 128],
                    in_=ps[0:16, 0:128]), sV)
                free[nm] = ev
        K.wait("vector", numer_ev)
        vgap()
        dg_ev = K.emit("vector", lambda e: e.tensor_tensor(
            out=Dg_bf, in0=numer, in1=recD, op=ALU.mult), sV)
        vgap()
        prev["dg"] = dg_ev
        dv_ev = []
        last_dv_mm = None
        for c in range(NCH):
            nm, ps = next_ms()
            K.wait("tensor", dg_ev, free[nm])
            bev = K.emit("tensor", lambda e, ps=ps, c=c: e.matmul(
                ps, sel_bf[:, c * 128:(c + 1) * 128], Dg_bf, start=True, stop=True), sT)
            last_dv_mm = bev
            K.wait("vector", bev, qkv_ev[("v", c)])
            ev = K.emit("vector", lambda e, ps=ps, c=c: e.tensor_tensor(
                out=DVb[:, c * 512:(c + 1) * 512], in0=ps,
                in1=Vf[:, c * 512:(c + 1) * 512], op=ALU.mult), sV)
            free[nm] = ev
            dv_ev.append(ev)
        prev["dv_mm"] = last_dv_mm
        at_ev = {}
        last_wo = None
        for oc in range(NCH):
            nm, ps = next_mm()
            K.wait("tensor", free[nm], *dv_ev)
            mev = None
            for kc in range(NCH):
                mev = K.emit("tensor", lambda e, ps=ps, kc=kc, oc=oc: e.matmul(
                    ps, Wo[:, kc * D + oc * 128: kc * D + (oc + 1) * 128],
                    DVb[:, kc * 512:(kc + 1) * 512],
                    start=(kc == 0), stop=(kc == NCH - 1)), sT)
            last_wo = mev
            K.wait("vector", mev)
            ev = K.emit("vector", lambda e, ps=ps, oc=oc: e.tensor_scalar(
                out=attn_f[:, oc * 512:(oc + 1) * 512], in0=ps,
                scalar1=bos[:, oc:oc + 1], scalar2=None, op0=ALU.add), sV)
            free[nm] = ev
            at_ev[oc] = ev
            pump(1)
        prev["wo_mm"] = last_wo
        last_tt = None
        acc_last = prev["acc_st"]
        for j in range(4):
            K.wait("sync", acc_last, last_tt)
            lxj = dma(x_j, x_in[t0 + j * 128: t0 + (j + 1) * 128, :])
            jt_ev = []
            for half in range(2):
                nm, ps = next_ms()
                K.wait("tensor", free[nm],
                       *[at_ev[half * 4 + c4] for c4 in range(4)])
                tev = None
                for c4 in range(4):
                    c = half * 4 + c4
                    tev = K.emit("tensor", lambda e, ps=ps, c=c, c4=c4, j=j: e.transpose(
                        ps[:, c4 * 128:(c4 + 1) * 128],
                        attn_f[:, c * 512 + j * 128: c * 512 + (j + 1) * 128], idn), sT)
                K.wait("vector", tev, lxj)
                ev = K.emit("vector", lambda e, ps=ps, half=half: e.tensor_tensor(
                    out=x_j[:, half * 512:(half + 1) * 512],
                    in0=ps, in1=x_j[:, half * 512:(half + 1) * 512],
                    op=ALU.add), sV)
                free[nm] = ev
                jt_ev.append(ev)
            last_tt = jt_ev[-1]
            K.wait("sync", *jt_ev)
            acc_last = dma(acc_s[tile, j], x_j)
            pump(1)
        prev["att_tr"] = last_tt
        acc_ev[tile] = acc_last
        prev["acc_st"] = acc_last
        pump(4)

        if tile == 3:
            pump(len(filler))
            emit_ar2()
            enqueue_C(0, NBLK // BPG)
        if tile == 5:
            vpump(len(vfiller))
            emit_ar3()
            enqueue_D(0, NBLK // BPG)
        if tile == 7:
            vpump(len(vfiller))
            emit_ar4()

    scopeCD.close()
    K.wait("sync", acc_ev[NT - 1], prev["scores"], prev["qk_seg"], prev["wo_mm"],
           prev["dg"], prev["att_tr"], prev["qrep"], prev["dv_mm"], prev["qkv_mm"])
    K.wait("vector", prev["scores"], prev["wo_mm"])
    scopeATT.close()
    prep.close()
    scopeW.close()
    Ae = lambda name, shape, dt: main.enter_context(nc.sbuf_tensor(name, shape, dt)).ap()
    o_ch2 = [Ae(f"o_ch{i}", [128, 2048], F32) for i in range(2)]
    oq_bf = Ae("oq_bf", [128, NFF * 512], BF16)
    WdT = Ae("WdT", [128, NFF * D], BF16)
    acc_rd2 = [Ae(f"acc_rd{i}", [128, 4096], F32) for i in range(2)]

    K.wait("sync", wready[("down", 0)])
    wd_ld = []
    for ic in range(NFF):
        wd_ld.append(dma(WdT[:, ic * D:(ic + 1) * D], wdownT_s[ic]))
    K.wait("vector", state["so_ev"], *wd_ld)
    wds_ev = K.emit("vector", lambda e: e.tensor_scalar(
        out=WdT, in0=WdT, scalar1=so_b, scalar2=None, op0=ALU.mult), sV)

    out_st = []
    bank_ap = [ps_sc[i][:, h * 512:(h + 1) * 512] for i in range(3) for h in range(2)] \
        + [ps_mm[0], ps_mm[1]]
    bank_free = [free["sc0"], free["sc0"], free["sc1"], free["sc1"],
                 free["sc2"], free["sc2"], free["mm0"], free["mm1"]]
    bank_last = [None] * 8
    och_user = [None, None]
    acc_user = [None, None]
    qdq_gate = None
    for tile in range(NT):
        acc_rd = acc_rd2[tile % 2]
        K.wait("sync", acc_ev[tile], acc_user[tile % 2])
        aev = dma(acc_rd.rearrange("p (j f) -> p j f", j=4),
                  acc_s[tile].transpose([1, 0, 2]))
        for ch in range(6):
            o_ch = o_ch2[ch % 2]
            K.wait("sync", state["o_st"], och_user[ch % 2])
            state["o_st"] = None
            lev = dma(o_ch.rearrange("p (c t) -> p c t", c=4),
                      o_s[tile, ch * 4:(ch + 1) * 4].transpose([1, 0, 2]))
            K.wait("vector", lev, state["mo_ev"], qdq_gate if ch == 0 else None)
            state["mo_ev"] = None
            K.emit("vector", lambda e, o_ch=o_ch: e.tensor_scalar(
                out=o_ch, in0=o_ch, scalar1=mo_b, scalar2=None, op0=ALU.mult))
            K.emit("vector", lambda e, o_ch=o_ch: e.tensor_scalar(
                out=o_ch, in0=o_ch, scalar1=MAGIC, scalar2=MAGIC, op0=ALU.add, op1=ALU.subtract))
            ev = K.emit("vector", lambda e, ch=ch, o_ch=o_ch: e.tensor_scalar(
                out=oq_bf[:, ch * 2048:(ch + 1) * 2048], in0=o_ch,
                scalar1=32767.0, scalar2=-32768.0, op0=ALU.min, op1=ALU.max), sV)
            och_user[ch % 2] = ev
            K.wait("tensor", ev, wds_ev if (tile == 0 and ch == 0) else None)
            for b in range(8):
                j, nf = b // 2, b % 2
                if ch == 0:
                    K.wait("tensor", bank_free[b])
                for ic2 in range(4):
                    ic = ch * 4 + ic2
                    bank_last[b] = K.emit("tensor", lambda e, b=b, ic=ic, j=j, nf=nf: e.matmul(
                        bank_ap[b], oq_bf[:, ic * 512 + j * 128: ic * 512 + (j + 1) * 128],
                        WdT[:, ic * D + nf * 512: ic * D + (nf + 1) * 512],
                        start=(ic == 0), stop=(ic == NFF - 1)), sT)
        qdq_gate = bank_last[7]
        ts_ev = []
        for b in range(8):
            j, nf = b // 2, b % 2
            K.wait("vector", bank_last[b], aev if b == 0 else None)
            ev = K.emit("vector", lambda e, b=b, j=j, nf=nf, acc_rd=acc_rd: e.tensor_tensor(
                out=acc_rd[:, j * 1024 + nf * 512: j * 1024 + (nf + 1) * 512],
                in0=bank_ap[b], in1=acc_rd[:, j * 1024 + nf * 512: j * 1024 + (nf + 1) * 512],
                op=ALU.add), sV)
            bank_free[b] = ev
            ts_ev.append(ev)
        K.wait("sync", *ts_ev)
        st = dma(out[tile * T:(tile + 1) * T, :].rearrange("(j p) f -> p j f", p=128),
                 acc_rd.rearrange("p (j f) -> p j f", j=4))
        out_st.append(st)
        acc_user[tile % 2] = st
    main.close()

    K.wait("sync", *out_st)
    K.run()
    es.close()
    return nc


def make_consts():
    ident = np.eye(128, dtype=np.float32)
    seg16 = np.zeros((128, NCH * 16), np.float32)
    for c in range(NCH):
        for p in range(128):
            seg16[p, c * 16 + 2 * c + p // 64] = 1.0
    sel64 = np.zeros((16, NCH * 128), np.float32)
    for c in range(NCH):
        for p in range(128):
            sel64[2 * c + p // 64, c * 128 + p] = 1.0
    ones_row = np.ones((1, 128), np.float32)
    return ident, seg16, sel64, ones_row


_NC_CACHE = {}


def kernel(L0_attn_X_embed, L0_attn_W_q, L0_attn_b_q, L0_attn_W_k, L0_attn_b_k,
           L0_attn_W_v, L0_attn_b_v, L0_attn_W_o, L0_attn_b_o,
           L0_ffn_w_gate, L0_ffn_w_up, L0_ffn_w_down):
    n_cores = 8
    if n_cores not in _NC_CACHE:
        _NC_CACHE[n_cores] = build_nc(n_cores)
    nc = _NC_CACHE[n_cores]
    ident, seg16, sel64, ones_row = make_consts()
    x = np.ascontiguousarray(np.asarray(L0_attn_X_embed, dtype=np.float32))
    common = dict(
        w_q=np.ascontiguousarray(np.asarray(L0_attn_W_q, np.float32)),
        b_q=np.ascontiguousarray(np.asarray(L0_attn_b_q, np.float32)),
        w_k=np.ascontiguousarray(np.asarray(L0_attn_W_k, np.float32)),
        b_k=np.ascontiguousarray(np.asarray(L0_attn_b_k, np.float32)),
        w_v=np.ascontiguousarray(np.asarray(L0_attn_W_v, np.float32)),
        b_v=np.ascontiguousarray(np.asarray(L0_attn_b_v, np.float32)),
        w_o=np.ascontiguousarray(np.asarray(L0_attn_W_o, np.float32)),
        b_o=np.ascontiguousarray(np.asarray(L0_attn_b_o, np.float32)),
        w_gate=np.ascontiguousarray(np.asarray(L0_ffn_w_gate, np.float32)),
        w_up=np.ascontiguousarray(np.asarray(L0_ffn_w_up, np.float32)),
        w_down=np.ascontiguousarray(np.asarray(L0_ffn_w_down, np.float32)),
        ident=ident, seg16=seg16, sel64=sel64, ones_row=ones_row,
    )
    in_maps = [dict(x_in=np.ascontiguousarray(x[i]), **common) for i in range(n_cores)]
    res = run_bass_kernel_spmd(nc, in_maps, core_ids=list(range(n_cores)))
    return np.stack([r["out"] for r in res.results], axis=0)


# version 21
# speedup vs baseline: 1.0784x; 1.0237x over previous
import numpy as np
from collections import deque
from contextlib import ExitStack

import concourse.bass as bass
import concourse.mybir as mybir
from concourse.bass_utils import run_bass_kernel_spmd

F32 = mybir.dt.float32
BF16 = mybir.dt.bfloat16
AF = mybir.ActivationFunctionType
ALU = mybir.AluOpType
AX = mybir.AxisListType

S = 4096
D = 1024
DFF = 3072
T = 512
NT = S // T
NCH = D // 128
NFF = DFF // 128
MAGIC = 12582912.0
EXPSC = float(np.float32(1.0) / np.float32(1024.0))


class Sched:
    ENGS = ("sync", "tensor", "vector", "scalar", "gpsimd")

    def __init__(self, nc):
        self.nc = nc
        self.q = {e: [] for e in self.ENGS}
        self.cnt = {}

    def emit(self, eng, fn, sem=None, k=1):
        ev = None
        if sem is not None:
            self.cnt[sem.name] = self.cnt.get(sem.name, 0) + k
            ev = (sem, self.cnt[sem.name])
            self.q[eng].append(lambda e, fn=fn, sem=sem, k=k: fn(e).then_inc(sem, k))
        else:
            self.q[eng].append(fn)
        return ev

    def wait(self, eng, *evs):
        best = {}
        for e in evs:
            if e is None:
                continue
            sem, val = e
            if best.get(sem.name, (None, -1))[1] < val:
                best[sem.name] = (sem, val)
        for sem, val in best.values():
            self.q[eng].append(lambda e, sem=sem, val=val: e.wait_ge(sem, val))

    def barrier(self, sB):
        self.cnt[sB.name] = self.cnt.get(sB.name, 0) + len(self.ENGS)
        tgt = self.cnt[sB.name]
        for eng in self.ENGS:
            self.q[eng].append(lambda e, sB=sB: e.nop().then_inc(sB, 1))
            self.q[eng].append(lambda e, sB=sB, tgt=tgt: e.wait_ge(sB, tgt))

    def run(self):
        with self.nc.Block() as block:
            block.sync(lambda e: [f(e) for f in self.q["sync"]])
            block.tensor(lambda e: [f(e) for f in self.q["tensor"]])
            block.vector(lambda e: [f(e) for f in self.q["vector"]])
            block.scalar(lambda e: [f(e) for f in self.q["scalar"]])
            block.gpsimd(lambda e: [f(e) for f in self.q["gpsimd"]])


def build_nc(n_cores=8):
    nc = bass.Bass("TRN2", target_bir_lowering=False, detect_race_conditions=False)

    x_in = nc.dram_tensor("x_in", [S, D], F32, kind="ExternalInput").ap()
    w_q = nc.dram_tensor("w_q", [D, D], F32, kind="ExternalInput").ap()
    b_q = nc.dram_tensor("b_q", [D], F32, kind="ExternalInput").ap()
    w_k = nc.dram_tensor("w_k", [D, D], F32, kind="ExternalInput").ap()
    b_k = nc.dram_tensor("b_k", [D], F32, kind="ExternalInput").ap()
    w_v = nc.dram_tensor("w_v", [D, D], F32, kind="ExternalInput").ap()
    b_v = nc.dram_tensor("b_v", [D], F32, kind="ExternalInput").ap()
    w_o = nc.dram_tensor("w_o", [D, D], F32, kind="ExternalInput").ap()
    b_o = nc.dram_tensor("b_o", [D], F32, kind="ExternalInput").ap()
    w_gate = nc.dram_tensor("w_gate", [DFF, D], F32, kind="ExternalInput").ap()
    w_up = nc.dram_tensor("w_up", [DFF, D], F32, kind="ExternalInput").ap()
    w_down = nc.dram_tensor("w_down", [D, DFF], F32, kind="ExternalInput").ap()
    ident = nc.dram_tensor("ident", [128, 128], F32, kind="ExternalInput").ap()
    seg16 = nc.dram_tensor("seg16", [128, NCH * 16], F32, kind="ExternalInput").ap()
    sel64 = nc.dram_tensor("sel64", [16, NCH * 128], F32, kind="ExternalInput").ap()
    ones_row = nc.dram_tensor("ones_row", [1, 128], F32, kind="ExternalInput").ap()

    out = nc.dram_tensor("out", [S, D], F32, kind="ExternalOutput").ap()

    xn_s = nc.dram_tensor("xn_s", [NT, NCH, 128, T], F32, kind="Internal").ap()
    xnb_s = nc.dram_tensor("xnb_s", [NT, NCH, 128, T], BF16, kind="Internal").ap()
    acc_s = nc.dram_tensor("acc_s", [NT, 4, 128, D], F32, kind="Internal").ap()
    up_s = nc.dram_tensor("up_s", [NT, NFF, 128, T], F32, kind="Internal").ap()
    gate_s = nc.dram_tensor("gate_s", [NT, NFF, 128, T], F32, kind="Internal").ap()
    t_s = nc.dram_tensor("t_s", [NT, NFF, 128, T], F32, kind="Internal").ap()
    o_s = nc.dram_tensor("o_s", [NT, NFF, 128, T], F32, kind="Internal").ap()
    wupT_s = nc.dram_tensor("wupT_s", [NFF, NCH, 128, 128], BF16, kind="Internal").ap()
    wgateT_s = nc.dram_tensor("wgateT_s", [NFF, NCH, 128, 128], BF16, kind="Internal").ap()
    wdownT_s = nc.dram_tensor("wdownT_s", [NFF, 128, D], BF16, kind="Internal").ap()
    cc_in = [nc.dram_tensor(f"cc_in{i}", [1, 16], F32, kind="Internal").ap() for i in range(4)]
    cc_out = [nc.dram_tensor(f"cc_out{i}", [1, 16], F32, kind="Internal").ap() for i in range(4)]
    xp_s = nc.dram_tensor("xp_s", [128, 1], F32, kind="Internal").ap()

    ps_sc = [nc.alloc_psum_tensor(f"ps_sc{i}", [128, 1024], F32).ap() for i in range(3)]
    ps_mm = [nc.alloc_psum_tensor(f"ps_mm{i}", [128, 512], F32).ap() for i in range(2)]

    es = ExitStack()
    NSL = 28
    sDma = [es.enter_context(nc.semaphore(f"sD{i}")) for i in range(NSL)]
    sT = es.enter_context(nc.semaphore("sT"))
    sV = es.enter_context(nc.semaphore("sV"))
    sS = es.enter_context(nc.semaphore("sS"))
    sG = es.enter_context(nc.semaphore("sG"))
    sC = es.enter_context(nc.semaphore("sC"))
    sB = es.enter_context(nc.semaphore("sB"))

    K = Sched(nc)
    A = nc.alloc_sbuf_tensor

    idn = A("idn", [128, 128], F32).ap()
    idn_bf = A("idn_bf", [128, 128], BF16).ap()
    seg = A("seg", [128, NCH * 16], F32).ap()
    sel_bf = A("sel_bf", [16, NCH * 128], BF16).ap()
    ones1 = A("ones1", [1, 128], F32).ap()
    bqs = A("bqs", [128, NCH], F32).ap()
    bks = A("bks", [128, NCH], F32).ap()
    bvs = A("bvs", [128, NCH], F32).ap()
    bos = A("bos", [128, NCH], F32).ap()
    mx_xn = A("mx_xn", [128, 1], F32).ap()
    mx_up = A("mx_up", [128, 1], F32).ap()
    mx_gate = A("mx_gate", [128, 1], F32).ap()
    mx_t = A("mx_t", [128, 1], F32).ap()
    mx_o = A("mx_o", [128, 1], F32).ap()
    mxrow = A("mxrow", [1, 128], F32).ap()
    scals = A("scals", [1, 16], F32).ap()
    rx_b = A("rx_b", [128, 1], F32).ap()
    mup_b = A("mup_b", [128, 1], F32).ap()
    mg_b = A("mg_b", [128, 1], F32).ap()
    sgh_b = A("sgh_b", [128, 1], F32).ap()
    mt_b = A("mt_b", [128, 1], F32).ap()
    mo_b = A("mo_b", [128, 1], F32).ap()
    so_b = A("so_b", [128, 1], F32).ap()
    dummy = A("dump_exp", [128, 1024], BF16).ap()
    junkA = A("junkA", [128, 64], F32).ap()
    junkB = A("junkB", [128, 64], F32).ap()
    K.emit("vector", lambda e: e.memset(junkA, 0.0))

    def vgap(n=1):
        for _ in range(n):
            K.emit("vector", lambda e: e.tensor_copy(out=junkB, in_=junkA))

    dma_rr = [0]

    def dma(ap_out, ap_in, eng="sync", slow=False):
        sem = sDma[dma_rr[0] % NSL]
        dma_rr[0] += 1
        prev = K.cnt.get(sem.name, 0)
        if prev:
            K.q[eng].append(lambda e, sem=sem, prev=prev: e.wait_ge(sem, prev))
        return K.emit(eng, lambda e, o=ap_out, i=ap_in, s=slow: e.dma_start(
            out=o, in_=i, allow_slow_non_contiguous=s), sem, 16)

    selbf_ev = K.emit("gpsimd", lambda e: e.dma_start(out=sel_bf, in_=sel64), sG, 16)
    ld_ev = [dma(idn, ident), dma(seg, seg16), dma(ones1, ones_row),
             dma(bqs, b_q.rearrange("(c p) -> p c", p=128), slow=True),
             dma(bks, b_k.rearrange("(c p) -> p c", p=128), slow=True),
             dma(bvs, b_v.rearrange("(c p) -> p c", p=128), slow=True),
             dma(bos, b_o.rearrange("(c p) -> p c", p=128), slow=True)]
    K.emit("scalar", lambda e: e.preload_act_table(AF.Exp))
    for t in (mx_xn, mx_up, mx_gate, mx_t, mx_o, scals):
        K.emit("vector", lambda e, t=t: e.memset(t, 0.0))
    K.wait("vector", *ld_ev)
    idnbf_ev = K.emit("vector", lambda e: e.tensor_copy(out=idn_bf, in_=idn), sV)

    free = {"sc0": None, "sc1": None, "sc2": None, "mm0": None, "mm1": None}
    mmi = [0]
    sci = [0]

    def next_mm():
        i = mmi[0] % 2
        mmi[0] += 1
        return (f"mm{i}", ps_mm[i])

    next_ms = next_mm

    def next_sc():
        i = sci[0] % 3
        sci[0] += 1
        return (f"sc{i}", ps_sc[i])

    scopeW = ExitStack()
    Wq = scopeW.enter_context(nc.sbuf_tensor("Wq", [128, NCH * D], BF16)).ap()
    Wk = scopeW.enter_context(nc.sbuf_tensor("Wk", [128, NCH * D], BF16)).ap()
    Wv = scopeW.enter_context(nc.sbuf_tensor("Wv", [128, NCH * D], BF16)).ap()
    Wo = scopeW.enter_context(nc.sbuf_tensor("Wo", [128, NCH * D], BF16)).ap()

    prep = ExitStack()
    Aw = lambda name, shape, dt: prep.enter_context(nc.sbuf_tensor(name, shape, dt)).ap()
    NPB = 2
    wst = [Aw(f"wst{i}", [128, 1024], F32) for i in range(NPB)]
    wlp = Aw("wlp", [128, 1024], F32)
    bmax = Aw("bmax", [128, 32], F32)
    brec = Aw("brec", [128, 32], F32)
    wbf = [Aw(f"wbf{i}", [128, 1024], BF16) for i in range(NPB)]
    wtrb = [Aw(f"wtrb{i}", [128, 1024], BF16) for i in range(NPB)]

    wst_free = [None] * NPB
    for wi, (wdr, wsb) in enumerate(((w_q, Wq), (w_k, Wk), (w_v, Wv), (w_o, Wo))):
        for kc in range(NCH):
            i2 = (wi * NCH + kc) % NPB
            K.wait("sync", wst_free[i2])
            lev = dma(wst[i2][:, 0:D], wdr[kc * 128:(kc + 1) * 128, :])
            K.wait("vector", lev)
            wst_free[i2] = K.emit(
                "vector", lambda e, i2=i2, wsb=wsb, kc=kc: e.tensor_copy(
                    out=wsb[:, kc * D:(kc + 1) * D], in_=wst[i2][:, 0:D]), sV)
    qkvo_ready = wst_free[:]

    wbf_free = [None] * NPB
    wtrb_free = [None] * NPB
    prep_stores = []

    def lpbq_chunk(wdr, oc, n_in, dst_ap_fn, ci, c0=0):
        i2 = ci % NPB
        nb = n_in // 32
        nbk = n_in // 128
        K.wait("sync", wst_free[i2])
        lev = dma(wst[i2][:, 0:n_in], wdr[oc * 128:(oc + 1) * 128, c0:c0 + n_in])
        K.wait("vector", lev, wbf_free[i2])
        K.emit("vector", lambda e: e.reduce_max(
            out=bmax[:, 0:nb], in_=wst[i2][:, 0:n_in].rearrange("p (b t) -> p b t", b=nb),
            axis=AX.X, apply_absolute_value=True))
        vgap()
        K.emit("vector", lambda e: e.tensor_scalar(
            out=bmax[:, 0:nb], in0=bmax[:, 0:nb], scalar1=1.0 / 7.0, scalar2=1e-12,
            op0=ALU.mult, op1=ALU.max))
        vgap()
        K.emit("vector", lambda e: e.reciprocal(out=brec[:, 0:nb], in_=bmax[:, 0:nb]))
        vgap()
        wst_free[i2] = K.emit("vector", lambda e: e.tensor_tensor(
            out=wlp[:, 0:n_in].rearrange("p (b t) -> p b t", b=nb),
            in0=wst[i2][:, 0:n_in].rearrange("p (b t) -> p b t", b=nb),
            in1=brec[:, 0:nb].rearrange("p (b o) -> p b o", o=1).broadcast_to([128, nb, 32]),
            op=ALU.mult), sV)
        K.emit("vector", lambda e: e.tensor_scalar(
            out=wlp[:, 0:n_in], in0=wlp[:, 0:n_in], scalar1=MAGIC, scalar2=MAGIC,
            op0=ALU.add, op1=ALU.subtract))
        K.emit("vector", lambda e: e.tensor_scalar(
            out=wlp[:, 0:n_in], in0=wlp[:, 0:n_in], scalar1=7.0, scalar2=-8.0,
            op0=ALU.min, op1=ALU.max))
        cast_ev = K.emit("vector", lambda e: e.tensor_tensor(
            out=wbf[i2][:, 0:n_in].rearrange("p (b t) -> p b t", b=nb),
            in0=wlp[:, 0:n_in].rearrange("p (b t) -> p b t", b=nb),
            in1=bmax[:, 0:nb].rearrange("p (b o) -> p b o", o=1).broadcast_to([128, nb, 32]),
            op=ALU.mult), sV)
        cpevs = []
        tev = None
        for ic in range(nbk):
            nm, ps = next_mm()
            K.wait("tensor", cast_ev, free[nm], idnbf_ev)
            tev = K.emit("tensor", lambda e, ps=ps, i2=i2, ic=ic: e.transpose(
                ps.bitcast(BF16)[:, 0:128], wbf[i2][:, ic * 128:(ic + 1) * 128], idn_bf), sT)
            K.wait("vector", tev, wtrb_free[i2] if ic == 0 else None)
            cpev = K.emit("vector", lambda e, ps=ps, i2=i2, ic=ic: e.tensor_copy(
                out=wtrb[i2][:, ic * 128:(ic + 1) * 128], in_=ps.bitcast(BF16)[:, 0:128]), sV)
            free[nm] = cpev
            cpevs.append(cpev)
        wbf_free[i2] = tev
        K.wait("sync", *cpevs)
        sev = dma(dst_ap_fn(oc), wtrb[i2][:, 0:nbk * 128].rearrange("p (b f) -> p b f", f=128))
        wtrb_free[i2] = sev
        prep_stores.append(sev)

    wready = {}
    lpbq_thunks = deque()
    ci = 0
    for oc in range(NFF):
        lpbq_thunks.append(lambda oc=oc, ci=ci: lpbq_chunk(
            w_up, oc, D, lambda oc: wupT_s[oc].transpose([1, 0, 2]), ci,
        )); ci += 1
        lpbq_thunks.append(("mark", ("up", oc)))
    for oc in range(NFF):
        lpbq_thunks.append(lambda oc=oc, ci=ci: lpbq_chunk(
            w_gate, oc, D, lambda oc: wgateT_s[oc].transpose([1, 0, 2]), ci,
        )); ci += 1
        lpbq_thunks.append(("mark", ("gate", oc)))
    for oc in range(NCH):
        for sub in range(3):
            lpbq_thunks.append(lambda oc=oc, ci=ci, sub=sub: lpbq_chunk(
                w_down, oc, D,
                lambda oc, sub=sub: wdownT_s[sub * 8:(sub + 1) * 8, :,
                                             oc * 128:(oc + 1) * 128].transpose([1, 0, 2]),
                ci, c0=sub * D)); ci += 1
    lpbq_thunks.append(("mark", ("down", 0)))

    pp = ExitStack()
    Ap = lambda name, shape, dt: pp.enter_context(nc.sbuf_tensor(name, shape, dt)).ap()
    px2 = [Ap("px0", [128, 4096], F32)] * 2
    pxn = Ap("pxn", [128, 4096], F32)
    pfm = Ap("pfm", [128, 4096], F32)
    psq = pfm
    pbf = Ap("pbf", [128, 4096], BF16)
    stt = Ap("stt", [128, 48], F32)
    sum4 = stt[:, 0:4]; sq4 = stt[:, 4:8]; m4 = stt[:, 8:12]; a4 = stt[:, 12:16]
    y4 = stt[:, 16:20]; t4a = stt[:, 20:24]; e4 = stt[:, 28:32]; xm1 = stt[:, 32:33]

    xns_ev = [None] * NT
    xnb_ev = [None] * NT
    pp_st = []
    px_free = [None, None]
    lx_evs = [None] * NT
    for tile in range(NT):
        t0 = tile * T
        K.wait("sync", px_free[0], *pp_st[-2:])
        lx_evs[tile] = dma(px2[tile % 2].rearrange("p (j f) -> p j f", j=4),
                           x_in[t0:t0 + T, :].rearrange("(j p) f -> p j f", p=128))
        px = px2[tile % 2]
        lx = lx_evs[tile]
        K.wait("vector", lx)
        K.emit("vector", lambda e, px=px: e.reduce_sum(
            out=sum4, in_=px.rearrange("p (j f) -> p j f", j=4), axis=AX.X))
        K.emit("vector", lambda e, px=px: e.tensor_tensor(out=psq, in0=px, in1=px, op=ALU.mult))
        K.emit("vector", lambda e: e.reduce_sum(
            out=sq4, in_=psq.rearrange("p (j f) -> p j f", j=4), axis=AX.X))
        K.emit("vector", lambda e: e.tensor_scalar(
            out=m4, in0=sum4, scalar1=1.0 / D, scalar2=None, op0=ALU.mult))
        vgap()
        K.emit("vector", lambda e: e.tensor_scalar(
            out=a4, in0=sq4, scalar1=1.0 / D, scalar2=None, op0=ALU.mult))
        K.emit("vector", lambda e: e.tensor_tensor(out=e4, in0=m4, in1=m4, op=ALU.mult))
        vgap()
        K.emit("vector", lambda e: e.tensor_tensor(out=a4, in0=a4, in1=e4, op=ALU.subtract))
        vgap()
        K.emit("vector", lambda e: e.tensor_scalar(
            out=a4, in0=a4, scalar1=1e-5, scalar2=None, op0=ALU.add))
        vgap()
        K.emit("vector", lambda e: e.reciprocal(out=y4, in_=a4))
        vgap()
        for _ in range(6):
            K.emit("vector", lambda e: e.tensor_tensor(out=t4a, in0=y4, in1=y4, op=ALU.mult))
            vgap()
            K.emit("vector", lambda e: e.tensor_tensor(out=t4a, in0=t4a, in1=a4, op=ALU.mult))
            vgap()
            K.emit("vector", lambda e: e.tensor_scalar(
                out=t4a, in0=t4a, scalar1=-0.5, scalar2=1.5, op0=ALU.mult, op1=ALU.add))
            vgap()
            K.emit("vector", lambda e: e.tensor_tensor(out=y4, in0=y4, in1=t4a, op=ALU.mult))
            vgap()
        vxn = None
        for j in range(4):
            vxn = K.emit("vector", lambda e, j=j, px=px: e.tensor_scalar(
                out=pxn[:, j * 1024:(j + 1) * 1024], in0=px[:, j * 1024:(j + 1) * 1024],
                scalar1=m4[:, j:j + 1], scalar2=y4[:, j:j + 1],
                op0=ALU.subtract, op1=ALU.mult), sV)
        K.emit("vector", lambda e: e.reduce_max(
            out=xm1, in_=pxn, axis=AX.X, apply_absolute_value=True))
        vgap()
        K.emit("vector", lambda e: e.tensor_tensor(out=mx_xn, in0=mx_xn, in1=xm1, op=ALU.max))
        vfm = []
        for c in range(NCH):
            nm, ps = next_ms()
            K.wait("tensor", vxn, free[nm])
            tev = None
            for j in range(4):
                tev = K.emit("tensor", lambda e, ps=ps, j=j, c=c: e.transpose(
                    ps[:, j * 128:(j + 1) * 128],
                    pxn[:, j * 1024 + c * 128: j * 1024 + (c + 1) * 128], idn), sT)
            K.wait("vector", tev)
            K.emit("vector", lambda e, ps=ps, c=c: e.tensor_copy(
                out=pfm[:, c * 512:(c + 1) * 512], in_=ps))
            ev = K.emit("vector", lambda e, ps=ps, c=c: e.tensor_copy(
                out=pbf[:, c * 512:(c + 1) * 512], in_=ps), sV)
            free[nm] = ev
            vfm.append(ev)
        px_free[0] = vxn
        K.wait("sync", *vfm)
        xns_ev[tile] = dma(xn_s[tile].transpose([1, 0, 2]),
                           pfm.rearrange("p (c t) -> p c t", c=NCH))
        xnb_ev[tile] = dma(xnb_s[tile].transpose([1, 0, 2]),
                           pbf.rearrange("p (c t) -> p c t", c=NCH))
        pp_st = [xns_ev[tile], xnb_ev[tile]]
    pp.close()
    K.barrier(sB)

    def allreduce_max(idx, srcs, n_vals, wait_evs=()):
        pk = []
        for i, src in enumerate(srcs):
            K.wait("sync", *wait_evs)
            ev0 = dma(xp_s, src)
            K.wait("sync", ev0)
            ev = dma(mxrow[0:1, 0:128], xp_s.rearrange("p o -> o p"))
            K.wait("vector", ev)
            pk.append(K.emit("vector", lambda e, i=i: e.reduce_max(
                out=scals[0:1, i:i + 1], in_=mxrow, axis=AX.X), sV))
            vgap()
        z = K.emit("vector", lambda e, n_vals=n_vals: e.memset(scals[0:1, n_vals:8], 0.0), sV)
        K.wait("gpsimd", z, *pk)
        gev = K.emit("gpsimd", lambda e, idx=idx: e.dma_start(
            out=cc_in[idx], in_=scals[0:1, 0:16]), sG, 16)
        K.wait("gpsimd", gev)
        cev = K.emit("gpsimd", lambda e, idx=idx: e.collective_compute(
            "AllReduce", ALU.max, replica_groups=[list(range(n_cores))],
            ins=[cc_in[idx]], outs=[cc_out[idx]]), sC, 1)
        K.wait("gpsimd", cev)
        return K.emit("gpsimd", lambda e, idx=idx: e.dma_start(
            out=scals[0:1, 0:16], in_=cc_out[idx]), sG, 16)

    def bcast(dst, src_col, wait_ev):
        nm, ps = next_ms()
        K.wait("tensor", wait_ev, free[nm])
        mev = K.emit("tensor", lambda e, ps=ps, src_col=src_col: e.matmul(
            ps[:, 0:1], ones1, scals[0:1, src_col:src_col + 1], start=True, stop=True), sT)
        K.wait("vector", mev)
        ev = K.emit("vector", lambda e, ps=ps, dst=dst: e.tensor_copy(out=dst, in_=ps[:, 0:1]), sV)
        vgap()
        free[nm] = ev
        return ev

    ar1 = allreduce_max(0, [mx_xn], 1)
    K.wait("vector", ar1)
    K.emit("vector", lambda e: e.tensor_scalar(
        out=scals[0:1, 8:9], in0=scals[0:1, 0:1], scalar1=1.0 / 32767.0, scalar2=1e-12,
        op0=ALU.mult, op1=ALU.max))
    vgap()
    sx_ev = K.emit("vector", lambda e: e.reciprocal(out=scals[0:1, 9:10], in_=scals[0:1, 8:9]), sV)
    vgap()
    rx_ev = bcast(rx_b, 9, sx_ev)

    main = ExitStack()
    scopeATT = ExitStack()
    Am = lambda name, shape, dt: scopeATT.enter_context(nc.sbuf_tensor(name, shape, dt)).ap()
    x_j = Am("x_j", [128, 1024], F32)
    xn_bf = Am("xn_bf", [128, 4096], BF16)
    Qb = Am("Qb", [128, 4096], BF16)
    Kb = Am("Kb", [128, 4096], BF16)
    Vf = Am("Vf", [128, 4096], F32)
    attn_f = Vf
    Qrep2 = [Am(f"Qrep{i}", [128, 4096], BF16) for i in range(2)]
    qk2 = [Am(f"qk2_{i}", [128, 512], F32) for i in range(2)]
    DVb = Am("DVb", [128, 4096], BF16)
    dn = Am("dn", [128, 64], F32)
    dnpAll2 = [Am(f"dnpAll{i}", [128, 128], F32) for i in range(2)]
    rec = Am("rec", [128, 64], F32)
    numer = Am("numer", [16, 512], F32)
    recD = Am("recD", [16, 512], F32)
    Dg_bf = Am("Dg_bf", [16, 512], BF16)
    scopeB = ExitStack()
    scopeCD = ExitStack()
    Ab = lambda name, shape, dt: scopeB.enter_context(nc.sbuf_tensor(name, shape, dt)).ap()
    xn_rd = Ab("xn_rd", [128, 2048], F32)
    xnq = Ab("xnq", [128, 4096], BF16)
    NWUB = 7
    wub = [Ab(f"wub{i}", [128, NCH * 128], BF16) for i in range(NWUB)]
    spl = [Ab(f"spl{i}", [128, 512], F32) for i in range(3)]
    splmx = Ab("splmx", [128, 1], F32)

    bstate = {"tile": -1, "xnq_ev": None, "xnq_users": [], "xn_rd_user": None}
    wub_free = [None] * NWUB
    wub_ld = [None] * NWUB
    spl_free = [None] * 3
    wubi = [0]
    bdone_evs = []
    bchain_list = []

    def prefetch_wchain():
        i = wubi[0]
        if i >= len(bchain_list):
            return False
        tile, which, oc = bchain_list[i]
        wkey = ("up" if which == 0 else "gate", oc)
        if wkey not in wready:
            return False
        wubi[0] += 1
        wsel = i % NWUB
        w_src = wupT_s if which == 0 else wgateT_s
        K.wait("sync", wub_free[wsel], wready[wkey])
        wub_ld[wsel] = dma(wub[wsel].rearrange("p (c f) -> p c f", c=NCH),
                           w_src[oc].transpose([1, 0, 2]))
        return True

    def prep_xnq(tile):
        qev = None
        for h in range(2):
            K.wait("sync", xns_ev[tile],
                   *(bstate["xnq_users"] if h == 0 else []), bstate["xn_rd_user"])
            lev = dma(xn_rd.rearrange("p (c t) -> p c t", c=NCH // 2),
                      xn_s[tile, h * 4:(h + 1) * 4].transpose([1, 0, 2]))
            K.wait("vector", lev, rx_ev)
            K.emit("vector", lambda e: e.tensor_scalar(
                out=xn_rd, in0=xn_rd, scalar1=rx_b, scalar2=None, op0=ALU.mult))
            K.emit("vector", lambda e: e.tensor_scalar(
                out=xn_rd, in0=xn_rd, scalar1=MAGIC, scalar2=MAGIC,
                op0=ALU.add, op1=ALU.subtract))
            qev = K.emit("vector", lambda e, h=h: e.tensor_scalar(
                out=xnq[:, h * 2048:(h + 1) * 2048], in0=xn_rd,
                scalar1=32767.0, scalar2=-32768.0,
                op0=ALU.min, op1=ALU.max), sV)
            bstate["xn_rd_user"] = qev
        bstate["tile"] = tile
        bstate["xnq_ev"] = qev
        bstate["xnq_users"] = []

    def b_chain(ci, tile, which, oc):
        if bstate["tile"] != tile:
            prep_xnq(tile)
        while wubi[0] < min(ci + NWUB, len(bchain_list)):
            if not prefetch_wchain():
                break
        wsel = ci % NWUB
        wev = wub_ld[wsel]
        dst_s = up_s if which == 0 else gate_s
        mxacc = mx_up if which == 0 else mx_gate
        nm, ps = next_mm()
        K.wait("tensor", free[nm], wev, bstate["xnq_ev"])
        mev = None
        for ic in range(NCH):
            mev = K.emit("tensor", lambda e, ps=ps, wsel=wsel, ic=ic: e.matmul(
                ps, wub[wsel][:, ic * 128:(ic + 1) * 128],
                xnq[:, ic * 512:(ic + 1) * 512],
                start=(ic == 0), stop=(ic == NCH - 1)), sT)
        wub_free[wsel] = mev
        bstate["xnq_users"].append(mev)
        ssel = (which * NFF + oc) % 3
        sb = spl[ssel]
        K.wait("vector", mev, spl_free[ssel])
        ev = K.emit("vector", lambda e, ps=ps, sb=sb: e.tensor_copy(out=sb, in_=ps), sV)
        free[nm] = ev
        K.emit("vector", lambda e, sb=sb: e.reduce_max(
            out=splmx, in_=sb, axis=AX.X, apply_absolute_value=True))
        vgap()
        mxev = K.emit("vector", lambda e, mxacc=mxacc: e.tensor_tensor(
            out=mxacc, in0=mxacc, in1=splmx, op=ALU.max), sV)
        K.wait("sync", ev)
        spl_free[ssel] = dma(dst_s[tile, oc], sb)
        bdone_evs.append(mxev)

    for tile in range(NT):
        for oc in range(NFF):
            bchain_list.append((tile, 0, oc))
            bchain_list.append((tile, 1, oc))
    filler = deque()
    for ci, (tile, which, oc) in enumerate(bchain_list):
        filler.append(lambda ci=ci, tile=tile, which=which, oc=oc: b_chain(ci, tile, which, oc))

    cur_tile = [0]

    bpumped = [0]

    def pump(n):
        for _ in range(n):
            if not filler:
                return
            ci = bpumped[0]
            tile, which, oc = bchain_list[ci]
            wkey = ("up" if which == 0 else "gate", oc)
            if wubi[0] <= ci and wkey not in wready:
                return
            bpumped[0] += 1
            filler.popleft()()

    cd = {}

    NBLK = NT * NFF
    BPG = 6
    gate_flat = gate_s.rearrange("n c p t -> (n c) p t")
    t_flat = t_s.rearrange("n c p t -> (n c) p t")
    up_flat = up_s.rearrange("n c p t -> (n c) p t")
    o_flat = o_s.rearrange("n c p t -> (n c) p t")

    state = {"t_st": None, "last_tev": None, "o_st": None, "last_oev": None,
             "mup_ev": None, "mg_ev": None, "sgh_ev": None, "mt_ev": None,
             "mo_ev": None, "so_ev": None}

    def emit_ar2():
        scopeB.close()
        Ac = lambda name, shape, dt: scopeCD.enter_context(nc.sbuf_tensor(name, shape, dt)).ap()
        cd["g_rd"] = Ac("g_rd", [128, 3072], F32)
        cd["sig_f"] = Ac("sig_f", [128, 3072], F32)
        cd["t_wr"] = Ac("t_wr", [128, 3072], F32)
        cd["cmx"] = Ac("cmx", [128, 1], F32)
        cd["bdone"] = (bstate["xnq_users"][-1] if bstate["xnq_users"] else None,
                       wub_free[0], wub_free[1], wub_free[2], wub_free[3])
        ar2 = allreduce_max(1, [mx_up, mx_gate], 2, wait_evs=spl_free)
        K.wait("vector", ar2)
        K.emit("vector", lambda e: e.tensor_tensor(
            out=scals[0:1, 0:2], in0=scals[0:1, 0:2],
            in1=scals[0:1, 8:9].broadcast_to([1, 2]), op=ALU.mult))
        vgap()
        K.emit("vector", lambda e: e.tensor_scalar(
            out=scals[0:1, 0:2], in0=scals[0:1, 0:2], scalar1=1.0 / 32767.0, scalar2=1e-12,
            op0=ALU.mult, op1=ALU.max))
        vgap()
        K.emit("vector", lambda e: e.tensor_copy(out=scals[0:1, 10:12], in_=scals[0:1, 0:2]))
        K.emit("vector", lambda e: e.reciprocal(out=scals[0:1, 2:4], in_=scals[0:1, 0:2]))
        vgap()
        K.emit("vector", lambda e: e.tensor_tensor(
            out=scals[0:1, 2:4], in0=scals[0:1, 2:4],
            in1=scals[0:1, 8:9].broadcast_to([1, 2]), op=ALU.mult))
        vgap()
        K.emit("vector", lambda e: e.tensor_scalar(
            out=scals[0:1, 4:5], in0=scals[0:1, 11:12], scalar1=0.5, scalar2=None, op0=ALU.mult))
        vgap()
        b2 = K.emit("vector", lambda e: e.tensor_scalar(
            out=scals[0:1, 12:13], in0=scals[0:1, 11:12], scalar1=1.0 / 65536.0, scalar2=None,
            op0=ALU.mult), sV)
        vgap()
        state["mup_ev"] = bcast(mup_b, 2, b2)
        state["mg_ev"] = bcast(mg_b, 3, state["mup_ev"])
        state["sgh_ev"] = bcast(sgh_b, 4, state["mg_ev"])

    vfiller = deque()

    def vpump(n):
        for _ in range(n):
            while vfiller and isinstance(vfiller[0], tuple):
                _, key = vfiller.popleft()
                wready[key] = prep_stores[-1]
            if vfiller:
                vfiller.popleft()()
        while vfiller and isinstance(vfiller[0], tuple):
            _, key = vfiller.popleft()
            wready[key] = prep_stores[-1]

    vfiller.extend(lpbq_thunks)
    lpbq_thunks = None

    def enqueue_C(g0, g1):
        g_rd, sig_f, t_wr, cmx = cd["g_rd"], cd["sig_f"], cd["t_wr"], cd["cmx"]

        def step_load(g):
            blk = g * BPG
            K.wait("sync", *spl_free, state["last_tev"])
            if g == 0:
                K.wait("tensor", *cd["bdone"])
                K.wait("sync", *cd["bdone"])
            cd["lev"] = dma(g_rd.rearrange("p (b t) -> p b t", b=BPG),
                            gate_flat[blk:blk + BPG].transpose([1, 0, 2]))

        def step(fn, *waits, sem=None):
            def run():
                K.wait("vector", *[state.get(w) if isinstance(w, str) else w for w in waits])
                ev = K.emit("vector", fn, sem)
                return ev
            return run

        for g in range(g0, g1):
            blk = g * BPG
            vfiller.append(lambda g=g: step_load(g))
            def c1(g=g):
                K.wait("vector", cd["lev"], state["mg_ev"], state["sgh_ev"], state["t_st"])
                K.emit("vector", lambda e: e.tensor_scalar(
                    out=g_rd, in0=g_rd, scalar1=mg_b, scalar2=None, op0=ALU.mult))
            def c2():
                K.emit("vector", lambda e: e.tensor_scalar(
                    out=g_rd, in0=g_rd, scalar1=MAGIC, scalar2=MAGIC, op0=ALU.add, op1=ALU.subtract))
            def c3():
                gq_ev = K.emit("vector", lambda e: e.tensor_scalar(
                    out=g_rd, in0=g_rd, scalar1=32767.0, scalar2=-32768.0,
                    op0=ALU.min, op1=ALU.max), sV)
                K.wait("scalar", gq_ev, state["sgh_ev"])
                cd["tn_ev"] = K.emit("scalar", lambda e: e.activation(
                    out=sig_f, in_=g_rd, func=AF.Tanh, scale=sgh_b), sS)
            def c4():
                K.wait("vector", cd["tn_ev"])
                K.emit("vector", lambda e: e.tensor_scalar(
                    out=sig_f, in0=sig_f, scalar1=32768.0, scalar2=32768.0,
                    op0=ALU.mult, op1=ALU.add))
            def c5():
                K.emit("vector", lambda e: e.tensor_scalar(
                    out=sig_f, in0=sig_f, scalar1=MAGIC, scalar2=MAGIC,
                    op0=ALU.add, op1=ALU.subtract))
            def c6():
                K.emit("vector", lambda e: e.tensor_scalar(
                    out=sig_f, in0=sig_f, scalar1=65535.0, scalar2=0.0,
                    op0=ALU.min, op1=ALU.max))
            def c7(blk=blk):
                t_ev = K.emit("vector", lambda e: e.tensor_tensor(
                    out=t_wr, in0=g_rd, in1=sig_f, op=ALU.mult), sV)
                state["last_tev"] = t_ev
            def c8():
                K.emit("vector", lambda e: e.reduce_max(
                    out=cmx, in_=t_wr, axis=AX.X, apply_absolute_value=True))
                vgap()
            def c9(blk=blk):
                K.emit("vector", lambda e: e.tensor_tensor(
                    out=mx_t, in0=mx_t, in1=cmx, op=ALU.max))
                K.wait("sync", state["last_tev"])
                state["t_st"] = dma(t_flat[blk:blk + BPG].transpose([1, 0, 2]),
                                    t_wr.rearrange("p (b t) -> p b t", b=BPG))
            for f in (c1, c2, c3, c4, c5, c6, c7, c8, c9):
                vfiller.append(f)

    def emit_ar3():
        ar3 = allreduce_max(2, [mx_t], 1)
        K.wait("vector", ar3)
        K.emit("vector", lambda e: e.tensor_tensor(
            out=scals[0:1, 0:1], in0=scals[0:1, 0:1], in1=scals[0:1, 12:13], op=ALU.mult))
        vgap()
        K.emit("vector", lambda e: e.tensor_scalar(
            out=scals[0:1, 13:14], in0=scals[0:1, 0:1], scalar1=1.0 / 32767.0, scalar2=1e-12,
            op0=ALU.mult, op1=ALU.max))
        vgap()
        K.emit("vector", lambda e: e.reciprocal(out=scals[0:1, 5:6], in_=scals[0:1, 13:14]))
        vgap()
        b3 = K.emit("vector", lambda e: e.tensor_tensor(
            out=scals[0:1, 5:6], in0=scals[0:1, 5:6], in1=scals[0:1, 12:13], op=ALU.mult), sV)
        vgap()
        state["mt_ev"] = bcast(mt_b, 5, b3)

    def enqueue_D(g0, g1):
        g_rd, sig_f, t_wr, cmx = cd["g_rd"], cd["sig_f"], cd["t_wr"], cd["cmx"]

        def d0(g):
            blk = g * BPG
            K.wait("sync", state["t_st"], state["last_oev"])
            cd["lev1"] = dma(t_wr.rearrange("p (b t) -> p b t", b=BPG),
                             t_flat[blk:blk + BPG].transpose([1, 0, 2]))
            cd["lev2"] = dma(g_rd.rearrange("p (b t) -> p b t", b=BPG),
                             up_flat[blk:blk + BPG].transpose([1, 0, 2]))

        def d1():
            K.wait("vector", cd["lev1"], cd["lev2"], state["mt_ev"], state["mup_ev"],
                   state["o_st"])

        for g in range(g0, g1):
            blk = g * BPG
            vfiller.append(lambda g=g: d0(g))
            vfiller.append(d1)
            for buf, mb in ((t_wr, mt_b), (g_rd, mup_b)):
                vfiller.append(lambda buf=buf, mb=mb: K.emit(
                    "vector", lambda e, buf=buf, mb=mb: e.tensor_scalar(
                        out=buf, in0=buf, scalar1=mb, scalar2=None, op0=ALU.mult)))
                vfiller.append(lambda buf=buf: K.emit(
                    "vector", lambda e, buf=buf: e.tensor_scalar(
                        out=buf, in0=buf, scalar1=MAGIC, scalar2=MAGIC,
                        op0=ALU.add, op1=ALU.subtract)))
                vfiller.append(lambda buf=buf: K.emit(
                    "vector", lambda e, buf=buf: e.tensor_scalar(
                        out=buf, in0=buf, scalar1=32767.0, scalar2=-32768.0,
                        op0=ALU.min, op1=ALU.max)))
            def d2():
                o_ev = K.emit("vector", lambda e: e.tensor_tensor(
                    out=sig_f, in0=t_wr, in1=g_rd, op=ALU.mult), sV)
                state["last_oev"] = o_ev
            def d3():
                K.emit("vector", lambda e: e.reduce_max(
                    out=cmx, in_=sig_f, axis=AX.X, apply_absolute_value=True))
                vgap()
            def d4(blk=blk):
                K.emit("vector", lambda e: e.tensor_tensor(
                    out=mx_o, in0=mx_o, in1=cmx, op=ALU.max))
                K.wait("sync", state["last_oev"])
                state["o_st"] = dma(o_flat[blk:blk + BPG].transpose([1, 0, 2]),
                                    sig_f.rearrange("p (b t) -> p b t", b=BPG))
            for f in (d2, d3, d4):
                vfiller.append(f)

    def emit_ar4():
        ar4 = allreduce_max(3, [mx_o], 1)
        K.wait("vector", ar4)
        K.emit("vector", lambda e: e.tensor_tensor(
            out=scals[0:1, 6:7], in0=scals[0:1, 13:14], in1=scals[0:1, 10:11], op=ALU.mult))
        vgap()
        K.emit("vector", lambda e: e.tensor_tensor(
            out=scals[0:1, 0:1], in0=scals[0:1, 0:1], in1=scals[0:1, 6:7], op=ALU.mult))
        vgap()
        K.emit("vector", lambda e: e.tensor_scalar(
            out=scals[0:1, 14:15], in0=scals[0:1, 0:1], scalar1=1.0 / 32767.0, scalar2=1e-12,
            op0=ALU.mult, op1=ALU.max))
        vgap()
        K.emit("vector", lambda e: e.reciprocal(out=scals[0:1, 7:8], in_=scals[0:1, 14:15]))
        vgap()
        b4 = K.emit("vector", lambda e: e.tensor_tensor(
            out=scals[0:1, 7:8], in0=scals[0:1, 7:8], in1=scals[0:1, 6:7], op=ALU.mult), sV)
        vgap()
        state["mo_ev"] = bcast(mo_b, 7, b4)
        soc = K.emit("vector", lambda e: e.tensor_copy(
            out=scals[0:1, 15:16], in_=scals[0:1, 14:15]), sV)
        vgap()
        state["so_ev"] = bcast(so_b, 15, soc)


    acc_ev = [None] * NT
    K.wait("tensor", *ld_ev, *qkvo_ready, selbf_ev)
    K.wait("scalar", *ld_ev)

    prev = {"x_tm": None, "xn_bf": None, "qkv_mm": None, "qrep": None,
            "scores": None, "dv_mm": None, "wo_mm": None, "att_tr": None,
            "acc_st": None, "dg": None, "qk_seg": None}

    for tile in range(NT):
        cur_tile[0] = tile
        t0 = tile * T
        K.wait("sync", xnb_ev[tile], prev["qkv_mm"])
        lxn = dma(xn_bf.rearrange("p (c t) -> p c t", c=NCH),
                  xnb_s[tile].transpose([1, 0, 2]))
        K.wait("tensor", lxn)
        qkv_ev = {}
        last_qkv_mm = None
        for name, Wsb, bias, dstbf in (("q", Wq, bqs, Qb), ("k", Wk, bks, Kb), ("v", Wv, bvs, Vf)):
            for oc in range(NCH):
                nm, ps = next_mm()
                K.wait("tensor", free[nm])
                if name == "q" and oc == 0:
                    K.wait("tensor", prev["scores"], prev["qk_seg"])
                if name == "v" and oc == 0:
                    K.wait("tensor", prev["wo_mm"])
                    K.wait("vector", prev["att_tr"])
                mev = None
                for kc in range(NCH):
                    mev = K.emit("tensor", lambda e, ps=ps, Wsb=Wsb, kc=kc, oc=oc: e.matmul(
                        ps, Wsb[:, kc * D + oc * 128: kc * D + (oc + 1) * 128],
                        xn_bf[:, kc * 512:(kc + 1) * 512],
                        start=(kc == 0), stop=(kc == NCH - 1)), sT)
                last_qkv_mm = mev
                K.wait("vector", mev)
                if name == "q" and oc == 0:
                    K.wait("vector", prev["qrep"])
                ev = K.emit("vector", lambda e, ps=ps, dstbf=dstbf, bias=bias, oc=oc: e.tensor_scalar(
                    out=dstbf[:, oc * 512:(oc + 1) * 512], in0=ps,
                    scalar1=bias[:, oc:oc + 1], scalar2=None, op0=ALU.add), sV)
                free[nm] = ev
                qkv_ev[(name, oc)] = ev
                pump(1)
        prev["qkv_mm"] = last_qkv_mm
        nm_d, ps_d = next_sc()
        K.wait("tensor", free[nm_d])
        dev = None
        for c in range(NCH):
            b = qk2[c % 2]
            K.wait("vector", qkv_ev[("q", c)], qkv_ev[("k", c)],
                   None if c < 2 else dev)
            qev = K.emit("vector", lambda e, b=b, c=c: e.tensor_tensor(
                out=b, in0=Qb[:, c * 512:(c + 1) * 512], in1=Kb[:, c * 512:(c + 1) * 512],
                op=ALU.mult), sV)
            K.wait("tensor", qev)
            dev = K.emit("tensor", lambda e, ps_d=ps_d, b=b, c=c: e.matmul(
                ps_d[0:16, 0:512], seg[:, c * 16:(c + 1) * 16], b,
                start=(c == 0), stop=(c == NCH - 1)), sT)
        prev["qk_seg"] = dev
        K.wait("scalar", dev, prev["dg"])
        numer_ev = K.emit("scalar", lambda e, ps_d=ps_d: e.activation(
            out=numer, in_=ps_d[0:16, 0:512], func=AF.Exp, scale=EXPSC), sS)
        free[nm_d] = numer_ev
        all_qk = [qkv_ev[(n, oc)] for n in ("q", "k") for oc in range(NCH)]
        last_sc = None
        grevs = [[], []]
        for hl in range(2):
            Qrep = Qrep2[hl]
            K.wait("sync", *all_qk, prev.get(f"qrep_free{hl}"))
            for dhalf in range(2):
                for gpar in range(2):
                    grevs[hl].append(K.emit("sync", lambda e, Qrep=Qrep, dhalf=dhalf, gpar=gpar, hl=hl: e.dma_start(
                        out=Qrep[dhalf * 64:(dhalf + 1) * 64, :]
                            .rearrange("p (k t) -> p k t", t=512)[:, :, gpar * 256:(gpar + 1) * 256],
                        in_=Qb[gpar * 64:(gpar + 1) * 64, :]
                            .rearrange("p (c t) -> p c t", t=512)[:, :, hl * 256:(hl + 1) * 256]),
                        sDma[dma_rr[0] % NSL], 16))
                    dma_rr[0] += 1
        prev["qrep"] = grevs[1][-1]
        for hl in range(2):
            Qrep = Qrep2[hl]
            dnpAll = dnpAll2[hl]
            dnp_free = prev.get(f"dnp_free{hl}")
            grev = grevs[hl]
            K.wait("tensor", *grev)
            first_exp = True
            for r in range(8):
                for v in range(2):
                    m = r * 2 + v
                    lhsA = Kb[0:64,
                              r * 512 + hl * 256 + v * 128: r * 512 + hl * 256 + (v + 1) * 128]
                    lhsB = Kb[64:128,
                              r * 512 + hl * 256 + v * 128: r * 512 + hl * 256 + (v + 1) * 128]
                    for grp in range(4):
                        nmA, psA = next_sc()
                        nmB, psB = next_sc()
                        K.wait("tensor", free[nmA], free[nmB])
                        sevA = sevB = None
                        for half in range(2):
                            a0 = grp * 1024 + half * 512
                            sevA = K.emit("tensor", lambda e, psA=psA, lhsA=lhsA, a0=a0, half=half, Qrep=Qrep: e.matmul(
                                psA[:, half * 512:(half + 1) * 512],
                                lhsA, Qrep[0:64, a0:a0 + 512],
                                start=True, stop=True), sT)
                            sevB = K.emit("tensor", lambda e, psB=psB, lhsB=lhsB, a0=a0, half=half, Qrep=Qrep: e.matmul(
                                psB[:, half * 512:(half + 1) * 512],
                                lhsB, Qrep[64:128, a0:a0 + 512],
                                start=True, stop=True), sT)
                        if first_exp:
                            K.wait("scalar", sevA, dnp_free)
                            first_exp = False
                        else:
                            K.wait("scalar", sevA)
                        eevA = K.emit("scalar", lambda e, psA=psA, m=m, grp=grp, dnpAll=dnpAll: e.activation(
                            out=dummy, in_=psA, func=AF.Exp, scale=EXPSC,
                            accum_out=dnpAll[:, m * 8 + grp: m * 8 + grp + 1]), sS)
                        free[nmA] = eevA
                        K.wait("scalar", sevB)
                        eevB = K.emit("scalar", lambda e, psB=psB, m=m, grp=grp, dnpAll=dnpAll: e.activation(
                            out=dummy, in_=psB, func=AF.Exp, scale=EXPSC,
                            accum_out=dnpAll[:, m * 8 + 4 + grp: m * 8 + 4 + grp + 1]), sS)
                        free[nmB] = eevB
                        pump(1)
                        vpump(1)
                    last_sc = sevB
            K.wait("vector", eevB)
            K.emit("vector", lambda e, hl=hl, dnpAll=dnpAll: e.reduce_sum(
                out=dn[:, hl * 32:(hl + 1) * 32].rearrange("p (r w) -> p r w", w=4)[:, :, 0:2],
                in_=dnpAll.rearrange("p (m g) -> p m g", g=8)[:, :, 0:4], axis=AX.X))
            prev[f"dnp_free{hl}"] = K.emit("vector", lambda e, hl=hl, dnpAll=dnpAll: e.reduce_sum(
                out=dn[:, hl * 32:(hl + 1) * 32].rearrange("p (r w) -> p r w", w=4)[:, :, 2:4],
                in_=dnpAll.rearrange("p (m g) -> p m g", g=8)[:, :, 4:8], axis=AX.X), sV)
            prev[f"qrep_free{hl}"] = last_sc
        prev["scores"] = last_sc
        vgap()
        rc_ev = K.emit("vector", lambda e: e.reciprocal(out=rec, in_=dn), sV)
        for hl in range(2):
            for v in range(2):
                nm, ps = next_ms()
                K.wait("tensor", rc_ev, free[nm])
                tev = K.emit("tensor", lambda e, ps=ps, hl=hl, v=v: e.transpose(
                    ps[0:16, 0:128],
                    rec[:, hl * 32:(hl + 1) * 32].rearrange("p (g w) -> p g w", w=2)[:, :, v],
                    idn), sT)
                K.wait("vector", tev)
                ev = K.emit("vector", lambda e, ps=ps, hl=hl, v=v: e.tensor_copy(
                    out=recD[:, hl * 256 + v * 128: hl * 256 + (v + 1) * 128],
                    in_=ps[0:16, 0:128]), sV)
                free[nm] = ev
        K.wait("vector", numer_ev)
        vgap()
        dg_ev = K.emit("vector", lambda e: e.tensor_tensor(
            out=Dg_bf, in0=numer, in1=recD, op=ALU.mult), sV)
        vgap()
        prev["dg"] = dg_ev
        dv_ev = []
        last_dv_mm = None
        for c in range(NCH):
            nm, ps = next_ms()
            K.wait("tensor", dg_ev, free[nm])
            bev = K.emit("tensor", lambda e, ps=ps, c=c: e.matmul(
                ps, sel_bf[:, c * 128:(c + 1) * 128], Dg_bf, start=True, stop=True), sT)
            last_dv_mm = bev
            K.wait("vector", bev, qkv_ev[("v", c)])
            ev = K.emit("vector", lambda e, ps=ps, c=c: e.tensor_tensor(
                out=DVb[:, c * 512:(c + 1) * 512], in0=ps,
                in1=Vf[:, c * 512:(c + 1) * 512], op=ALU.mult), sV)
            free[nm] = ev
            dv_ev.append(ev)
        prev["dv_mm"] = last_dv_mm
        at_ev = {}
        last_wo = None
        for oc in range(NCH):
            nm, ps = next_mm()
            K.wait("tensor", free[nm], *dv_ev)
            mev = None
            for kc in range(NCH):
                mev = K.emit("tensor", lambda e, ps=ps, kc=kc, oc=oc: e.matmul(
                    ps, Wo[:, kc * D + oc * 128: kc * D + (oc + 1) * 128],
                    DVb[:, kc * 512:(kc + 1) * 512],
                    start=(kc == 0), stop=(kc == NCH - 1)), sT)
            last_wo = mev
            K.wait("vector", mev)
            ev = K.emit("vector", lambda e, ps=ps, oc=oc: e.tensor_scalar(
                out=attn_f[:, oc * 512:(oc + 1) * 512], in0=ps,
                scalar1=bos[:, oc:oc + 1], scalar2=None, op0=ALU.add), sV)
            free[nm] = ev
            at_ev[oc] = ev
            pump(1)
        prev["wo_mm"] = last_wo
        last_tt = None
        acc_last = prev["acc_st"]
        for j in range(4):
            K.wait("sync", acc_last, last_tt)
            lxj = dma(x_j, x_in[t0 + j * 128: t0 + (j + 1) * 128, :])
            jt_ev = []
            for half in range(2):
                nm, ps = next_ms()
                K.wait("tensor", free[nm],
                       *[at_ev[half * 4 + c4] for c4 in range(4)])
                tev = None
                for c4 in range(4):
                    c = half * 4 + c4
                    tev = K.emit("tensor", lambda e, ps=ps, c=c, c4=c4, j=j: e.transpose(
                        ps[:, c4 * 128:(c4 + 1) * 128],
                        attn_f[:, c * 512 + j * 128: c * 512 + (j + 1) * 128], idn), sT)
                K.wait("vector", tev, lxj)
                ev = K.emit("vector", lambda e, ps=ps, half=half: e.tensor_tensor(
                    out=x_j[:, half * 512:(half + 1) * 512],
                    in0=ps, in1=x_j[:, half * 512:(half + 1) * 512],
                    op=ALU.add), sV)
                free[nm] = ev
                jt_ev.append(ev)
            last_tt = jt_ev[-1]
            K.wait("sync", *jt_ev)
            acc_last = dma(acc_s[tile, j], x_j)
            pump(1)
        prev["att_tr"] = last_tt
        acc_ev[tile] = acc_last
        prev["acc_st"] = acc_last
        pump(4)

        if tile == 3:
            pump(len(filler))
            emit_ar2()
            enqueue_C(0, NBLK // BPG)
        if tile == 5:
            vpump(len(vfiller))
            emit_ar3()
            enqueue_D(0, NBLK // BPG)
        if tile == 7:
            vpump(len(vfiller))
            emit_ar4()

    scopeCD.close()
    K.wait("sync", acc_ev[NT - 1], prev["scores"], prev["qk_seg"], prev["wo_mm"],
           prev["dg"], prev["att_tr"], prev["qrep"], prev["dv_mm"], prev["qkv_mm"])
    K.wait("vector", prev["scores"], prev["wo_mm"])
    scopeATT.close()
    prep.close()
    scopeW.close()
    Ae = lambda name, shape, dt: main.enter_context(nc.sbuf_tensor(name, shape, dt)).ap()
    o_ch2 = [Ae(f"o_ch{i}", [128, 2048], F32) for i in range(2)]
    oq_bf = Ae("oq_bf", [128, NFF * 512], BF16)
    WdT = Ae("WdT", [128, NFF * D], BF16)
    acc_rd2 = [Ae(f"acc_rd{i}", [128, 4096], F32) for i in range(2)]

    K.wait("sync", wready[("down", 0)])
    wd_ld = []
    for ic in range(NFF):
        wd_ld.append(dma(WdT[:, ic * D:(ic + 1) * D], wdownT_s[ic]))
    K.wait("vector", state["so_ev"], *wd_ld)
    wds_ev = K.emit("vector", lambda e: e.tensor_scalar(
        out=WdT, in0=WdT, scalar1=so_b, scalar2=None, op0=ALU.mult), sV)

    out_st = []
    bank_ap = [ps_sc[i][:, h * 512:(h + 1) * 512] for i in range(3) for h in range(2)] \
        + [ps_mm[0], ps_mm[1]]
    bank_free = [free["sc0"], free["sc0"], free["sc1"], free["sc1"],
                 free["sc2"], free["sc2"], free["mm0"], free["mm1"]]
    bank_last = [None] * 8
    och_user = [None, None]
    acc_user = [None, None]
    qdq_gate = None
    for tile in range(NT):
        acc_rd = acc_rd2[tile % 2]
        K.wait("sync", acc_ev[tile], acc_user[tile % 2])
        aev = dma(acc_rd.rearrange("p (j f) -> p j f", j=4),
                  acc_s[tile].transpose([1, 0, 2]))
        for ch in range(6):
            o_ch = o_ch2[ch % 2]
            K.wait("sync", state["o_st"], och_user[ch % 2])
            state["o_st"] = None
            lev = dma(o_ch.rearrange("p (c t) -> p c t", c=4),
                      o_s[tile, ch * 4:(ch + 1) * 4].transpose([1, 0, 2]))
            K.wait("vector", lev, state["mo_ev"], qdq_gate if ch == 0 else None)
            state["mo_ev"] = None
            K.emit("vector", lambda e, o_ch=o_ch: e.tensor_scalar(
                out=o_ch, in0=o_ch, scalar1=mo_b, scalar2=None, op0=ALU.mult))
            K.emit("vector", lambda e, o_ch=o_ch: e.tensor_scalar(
                out=o_ch, in0=o_ch, scalar1=MAGIC, scalar2=MAGIC, op0=ALU.add, op1=ALU.subtract))
            ev = K.emit("vector", lambda e, ch=ch, o_ch=o_ch: e.tensor_scalar(
                out=oq_bf[:, ch * 2048:(ch + 1) * 2048], in0=o_ch,
                scalar1=32767.0, scalar2=-32768.0, op0=ALU.min, op1=ALU.max), sV)
            och_user[ch % 2] = ev
            K.wait("tensor", ev, wds_ev if (tile == 0 and ch == 0) else None)
            for b in range(8):
                j, nf = b // 2, b % 2
                if ch == 0:
                    K.wait("tensor", bank_free[b])
                for ic2 in range(4):
                    ic = ch * 4 + ic2
                    bank_last[b] = K.emit("tensor", lambda e, b=b, ic=ic, j=j, nf=nf: e.matmul(
                        bank_ap[b], oq_bf[:, ic * 512 + j * 128: ic * 512 + (j + 1) * 128],
                        WdT[:, ic * D + nf * 512: ic * D + (nf + 1) * 512],
                        start=(ic == 0), stop=(ic == NFF - 1)), sT)
        qdq_gate = bank_last[7]
        ts_ev = []
        for b in range(8):
            j, nf = b // 2, b % 2
            K.wait("vector", bank_last[b], aev if b == 0 else None)
            ev = K.emit("vector", lambda e, b=b, j=j, nf=nf, acc_rd=acc_rd: e.tensor_tensor(
                out=acc_rd[:, j * 1024 + nf * 512: j * 1024 + (nf + 1) * 512],
                in0=bank_ap[b], in1=acc_rd[:, j * 1024 + nf * 512: j * 1024 + (nf + 1) * 512],
                op=ALU.add), sV)
            bank_free[b] = ev
            ts_ev.append(ev)
        K.wait("sync", *ts_ev)
        st = dma(out[tile * T:(tile + 1) * T, :].rearrange("(j p) f -> p j f", p=128),
                 acc_rd.rearrange("p (j f) -> p j f", j=4))
        out_st.append(st)
        acc_user[tile % 2] = st
    main.close()

    K.wait("sync", *out_st)
    K.run()
    es.close()
    return nc


def make_consts():
    ident = np.eye(128, dtype=np.float32)
    seg16 = np.zeros((128, NCH * 16), np.float32)
    for c in range(NCH):
        for p in range(128):
            seg16[p, c * 16 + 2 * c + p // 64] = 1.0
    sel64 = np.zeros((16, NCH * 128), np.float32)
    for c in range(NCH):
        for p in range(128):
            sel64[2 * c + p // 64, c * 128 + p] = 1.0
    ones_row = np.ones((1, 128), np.float32)
    return ident, seg16, sel64, ones_row


_NC_CACHE = {}


def kernel(L0_attn_X_embed, L0_attn_W_q, L0_attn_b_q, L0_attn_W_k, L0_attn_b_k,
           L0_attn_W_v, L0_attn_b_v, L0_attn_W_o, L0_attn_b_o,
           L0_ffn_w_gate, L0_ffn_w_up, L0_ffn_w_down):
    n_cores = 8
    if n_cores not in _NC_CACHE:
        _NC_CACHE[n_cores] = build_nc(n_cores)
    nc = _NC_CACHE[n_cores]
    ident, seg16, sel64, ones_row = make_consts()
    x = np.ascontiguousarray(np.asarray(L0_attn_X_embed, dtype=np.float32))
    common = dict(
        w_q=np.ascontiguousarray(np.asarray(L0_attn_W_q, np.float32)),
        b_q=np.ascontiguousarray(np.asarray(L0_attn_b_q, np.float32)),
        w_k=np.ascontiguousarray(np.asarray(L0_attn_W_k, np.float32)),
        b_k=np.ascontiguousarray(np.asarray(L0_attn_b_k, np.float32)),
        w_v=np.ascontiguousarray(np.asarray(L0_attn_W_v, np.float32)),
        b_v=np.ascontiguousarray(np.asarray(L0_attn_b_v, np.float32)),
        w_o=np.ascontiguousarray(np.asarray(L0_attn_W_o, np.float32)),
        b_o=np.ascontiguousarray(np.asarray(L0_attn_b_o, np.float32)),
        w_gate=np.ascontiguousarray(np.asarray(L0_ffn_w_gate, np.float32)),
        w_up=np.ascontiguousarray(np.asarray(L0_ffn_w_up, np.float32)),
        w_down=np.ascontiguousarray(np.asarray(L0_ffn_w_down, np.float32)),
        ident=ident, seg16=seg16, sel64=sel64, ones_row=ones_row,
    )
    in_maps = [dict(x_in=np.ascontiguousarray(x[i]), **common) for i in range(n_cores)]
    res = run_bass_kernel_spmd(nc, in_maps, core_ids=list(range(n_cores)))
    return np.stack([r["out"] for r in res.results], axis=0)


# version 22
# speedup vs baseline: 1.0929x; 1.0135x over previous
import numpy as np
from collections import deque
from contextlib import ExitStack

import concourse.bass as bass
import concourse.mybir as mybir
from concourse.bass_utils import run_bass_kernel_spmd

F32 = mybir.dt.float32
BF16 = mybir.dt.bfloat16
AF = mybir.ActivationFunctionType
ALU = mybir.AluOpType
AX = mybir.AxisListType

S = 4096
D = 1024
DFF = 3072
T = 512
NT = S // T
NCH = D // 128
NFF = DFF // 128
MAGIC = 12582912.0
EXPSC = float(np.float32(1.0) / np.float32(1024.0))


class Sched:
    ENGS = ("sync", "tensor", "vector", "scalar", "gpsimd")

    def __init__(self, nc):
        self.nc = nc
        self.q = {e: [] for e in self.ENGS}
        self.cnt = {}

    def emit(self, eng, fn, sem=None, k=1):
        ev = None
        if sem is not None:
            self.cnt[sem.name] = self.cnt.get(sem.name, 0) + k
            ev = (sem, self.cnt[sem.name])
            self.q[eng].append(lambda e, fn=fn, sem=sem, k=k: fn(e).then_inc(sem, k))
        else:
            self.q[eng].append(fn)
        return ev

    def wait(self, eng, *evs):
        best = {}
        for e in evs:
            if e is None:
                continue
            sem, val = e
            if best.get(sem.name, (None, -1))[1] < val:
                best[sem.name] = (sem, val)
        for sem, val in best.values():
            self.q[eng].append(lambda e, sem=sem, val=val: e.wait_ge(sem, val))

    def barrier(self, sB):
        self.cnt[sB.name] = self.cnt.get(sB.name, 0) + len(self.ENGS)
        tgt = self.cnt[sB.name]
        for eng in self.ENGS:
            self.q[eng].append(lambda e, sB=sB: e.nop().then_inc(sB, 1))
            self.q[eng].append(lambda e, sB=sB, tgt=tgt: e.wait_ge(sB, tgt))

    def run(self):
        with self.nc.Block() as block:
            block.sync(lambda e: [f(e) for f in self.q["sync"]])
            block.tensor(lambda e: [f(e) for f in self.q["tensor"]])
            block.vector(lambda e: [f(e) for f in self.q["vector"]])
            block.scalar(lambda e: [f(e) for f in self.q["scalar"]])
            block.gpsimd(lambda e: [f(e) for f in self.q["gpsimd"]])


def build_nc(n_cores=8):
    nc = bass.Bass("TRN2", target_bir_lowering=False, detect_race_conditions=False)

    x_in = nc.dram_tensor("x_in", [S, D], F32, kind="ExternalInput").ap()
    w_q = nc.dram_tensor("w_q", [D, D], F32, kind="ExternalInput").ap()
    b_q = nc.dram_tensor("b_q", [D], F32, kind="ExternalInput").ap()
    w_k = nc.dram_tensor("w_k", [D, D], F32, kind="ExternalInput").ap()
    b_k = nc.dram_tensor("b_k", [D], F32, kind="ExternalInput").ap()
    w_v = nc.dram_tensor("w_v", [D, D], F32, kind="ExternalInput").ap()
    b_v = nc.dram_tensor("b_v", [D], F32, kind="ExternalInput").ap()
    w_o = nc.dram_tensor("w_o", [D, D], F32, kind="ExternalInput").ap()
    b_o = nc.dram_tensor("b_o", [D], F32, kind="ExternalInput").ap()
    w_gate = nc.dram_tensor("w_gate", [DFF, D], F32, kind="ExternalInput").ap()
    w_up = nc.dram_tensor("w_up", [DFF, D], F32, kind="ExternalInput").ap()
    w_down = nc.dram_tensor("w_down", [D, DFF], F32, kind="ExternalInput").ap()
    ident = nc.dram_tensor("ident", [128, 128], F32, kind="ExternalInput").ap()
    seg16 = nc.dram_tensor("seg16", [128, NCH * 16], F32, kind="ExternalInput").ap()
    sel64 = nc.dram_tensor("sel64", [16, NCH * 128], F32, kind="ExternalInput").ap()
    ones_row = nc.dram_tensor("ones_row", [1, 128], F32, kind="ExternalInput").ap()

    out = nc.dram_tensor("out", [S, D], F32, kind="ExternalOutput").ap()

    xn_s = nc.dram_tensor("xn_s", [NT, NCH, 128, T], F32, kind="Internal").ap()
    xnb_s = nc.dram_tensor("xnb_s", [NT, NCH, 128, T], BF16, kind="Internal").ap()
    acc_s = nc.dram_tensor("acc_s", [NT, 4, 128, D], F32, kind="Internal").ap()
    up_s = nc.dram_tensor("up_s", [NT, NFF, 128, T], F32, kind="Internal").ap()
    gate_s = nc.dram_tensor("gate_s", [NT, NFF, 128, T], F32, kind="Internal").ap()
    t_s = nc.dram_tensor("t_s", [NT, NFF, 128, T], F32, kind="Internal").ap()
    o_s = nc.dram_tensor("o_s", [NT, NFF, 128, T], F32, kind="Internal").ap()
    wupT_s = nc.dram_tensor("wupT_s", [NFF, NCH, 128, 128], BF16, kind="Internal").ap()
    wgateT_s = nc.dram_tensor("wgateT_s", [NFF, NCH, 128, 128], BF16, kind="Internal").ap()
    wdownT_s = nc.dram_tensor("wdownT_s", [NFF, 128, D], BF16, kind="Internal").ap()
    cc_in = [nc.dram_tensor(f"cc_in{i}", [1, 16], F32, kind="Internal").ap() for i in range(4)]
    cc_out = [nc.dram_tensor(f"cc_out{i}", [1, 16], F32, kind="Internal").ap() for i in range(4)]
    xp_s = nc.dram_tensor("xp_s", [128, 1], F32, kind="Internal").ap()

    ps_sc = [nc.alloc_psum_tensor(f"ps_sc{i}", [128, 1024], F32).ap() for i in range(3)]
    ps_mm = [nc.alloc_psum_tensor(f"ps_mm{i}", [128, 512], F32).ap() for i in range(2)]

    es = ExitStack()
    NSL = 28
    sDma = [es.enter_context(nc.semaphore(f"sD{i}")) for i in range(NSL)]
    sT = es.enter_context(nc.semaphore("sT"))
    sV = es.enter_context(nc.semaphore("sV"))
    sS = es.enter_context(nc.semaphore("sS"))
    sG = es.enter_context(nc.semaphore("sG"))
    sC = es.enter_context(nc.semaphore("sC"))
    sB = es.enter_context(nc.semaphore("sB"))

    K = Sched(nc)
    A = nc.alloc_sbuf_tensor

    idn = A("idn", [128, 128], F32).ap()
    idn_bf = A("idn_bf", [128, 128], BF16).ap()
    seg = A("seg", [128, NCH * 16], F32).ap()
    sel_bf = A("sel_bf", [16, NCH * 128], BF16).ap()
    ones1 = A("ones1", [1, 128], F32).ap()
    bqs = A("bqs", [128, NCH], F32).ap()
    bks = A("bks", [128, NCH], F32).ap()
    bvs = A("bvs", [128, NCH], F32).ap()
    bos = A("bos", [128, NCH], F32).ap()
    mx_xn = A("mx_xn", [128, 1], F32).ap()
    mx_up = A("mx_up", [128, 1], F32).ap()
    mx_gate = A("mx_gate", [128, 1], F32).ap()
    mx_t = A("mx_t", [128, 1], F32).ap()
    mx_o = A("mx_o", [128, 1], F32).ap()
    mxrow = A("mxrow", [1, 128], F32).ap()
    scals = A("scals", [1, 16], F32).ap()
    rx_b = A("rx_b", [128, 1], F32).ap()
    mup_b = A("mup_b", [128, 1], F32).ap()
    mg_b = A("mg_b", [128, 1], F32).ap()
    sgh_b = A("sgh_b", [128, 1], F32).ap()
    mt_b = A("mt_b", [128, 1], F32).ap()
    mo_b = A("mo_b", [128, 1], F32).ap()
    so_b = A("so_b", [128, 1], F32).ap()
    dummy = A("dump_exp", [128, 1024], BF16).ap()
    junkA = A("junkA", [128, 64], F32).ap()
    junkB = A("junkB", [128, 64], F32).ap()
    K.emit("vector", lambda e: e.memset(junkA, 0.0))

    def vgap(n=1):
        for _ in range(n):
            K.emit("vector", lambda e: e.tensor_copy(out=junkB, in_=junkA))

    dma_rr = [0]

    def dma(ap_out, ap_in, eng="sync", slow=False):
        sem = sDma[dma_rr[0] % NSL]
        dma_rr[0] += 1
        prev = K.cnt.get(sem.name, 0)
        if prev:
            K.q[eng].append(lambda e, sem=sem, prev=prev: e.wait_ge(sem, prev))
        return K.emit(eng, lambda e, o=ap_out, i=ap_in, s=slow: e.dma_start(
            out=o, in_=i, allow_slow_non_contiguous=s), sem, 16)

    selbf_ev = K.emit("gpsimd", lambda e: e.dma_start(out=sel_bf, in_=sel64), sG, 16)
    ld_ev = [dma(idn, ident), dma(seg, seg16), dma(ones1, ones_row),
             dma(bqs, b_q.rearrange("(c p) -> p c", p=128), slow=True),
             dma(bks, b_k.rearrange("(c p) -> p c", p=128), slow=True),
             dma(bvs, b_v.rearrange("(c p) -> p c", p=128), slow=True),
             dma(bos, b_o.rearrange("(c p) -> p c", p=128), slow=True)]
    K.emit("scalar", lambda e: e.preload_act_table(AF.Exp))
    for t in (mx_xn, mx_up, mx_gate, mx_t, mx_o, scals):
        K.emit("vector", lambda e, t=t: e.memset(t, 0.0))
    K.wait("vector", *ld_ev)
    idnbf_ev = K.emit("vector", lambda e: e.tensor_copy(out=idn_bf, in_=idn), sV)

    free = {"sc0": None, "sc1": None, "sc2": None, "mm0": None, "mm1": None}
    mmi = [0]
    sci = [0]

    def next_mm():
        i = mmi[0] % 2
        mmi[0] += 1
        return (f"mm{i}", ps_mm[i])

    next_ms = next_mm

    def next_sc():
        i = sci[0] % 3
        sci[0] += 1
        return (f"sc{i}", ps_sc[i])

    scopeW = ExitStack()
    Wq = scopeW.enter_context(nc.sbuf_tensor("Wq", [128, NCH * D], BF16)).ap()
    Wk = scopeW.enter_context(nc.sbuf_tensor("Wk", [128, NCH * D], BF16)).ap()
    Wv = scopeW.enter_context(nc.sbuf_tensor("Wv", [128, NCH * D], BF16)).ap()
    Wo = scopeW.enter_context(nc.sbuf_tensor("Wo", [128, NCH * D], BF16)).ap()

    prep = ExitStack()
    Aw = lambda name, shape, dt: prep.enter_context(nc.sbuf_tensor(name, shape, dt)).ap()
    NPB = 2
    wst = [Aw(f"wst{i}", [128, 1024], F32) for i in range(NPB)]
    wlp = Aw("wlp", [128, 1024], F32)
    bmax = Aw("bmax", [128, 32], F32)
    brec = Aw("brec", [128, 32], F32)
    wbf = [Aw(f"wbf{i}", [128, 1024], BF16) for i in range(NPB)]
    wtrb = [Aw(f"wtrb{i}", [128, 1024], BF16) for i in range(NPB)]

    wst_free = [None] * NPB
    for wi, (wdr, wsb) in enumerate(((w_q, Wq), (w_k, Wk), (w_v, Wv), (w_o, Wo))):
        for kc in range(NCH):
            i2 = (wi * NCH + kc) % NPB
            K.wait("sync", wst_free[i2])
            lev = dma(wst[i2][:, 0:D], wdr[kc * 128:(kc + 1) * 128, :])
            K.wait("vector", lev)
            wst_free[i2] = K.emit(
                "vector", lambda e, i2=i2, wsb=wsb, kc=kc: e.tensor_copy(
                    out=wsb[:, kc * D:(kc + 1) * D], in_=wst[i2][:, 0:D]), sV)
    qkvo_ready = wst_free[:]

    wbf_free = [None] * NPB
    wtrb_free = [None] * NPB
    prep_stores = []

    def lpbq_chunk(wdr, oc, n_in, dst_ap_fn, ci, c0=0):
        i2 = ci % NPB
        nb = n_in // 32
        nbk = n_in // 128
        K.wait("sync", wst_free[i2])
        lev = dma(wst[i2][:, 0:n_in], wdr[oc * 128:(oc + 1) * 128, c0:c0 + n_in])
        K.wait("vector", lev, wbf_free[i2])
        K.emit("vector", lambda e: e.reduce_max(
            out=bmax[:, 0:nb], in_=wst[i2][:, 0:n_in].rearrange("p (b t) -> p b t", b=nb),
            axis=AX.X, apply_absolute_value=True))
        vgap()
        K.emit("vector", lambda e: e.tensor_scalar(
            out=bmax[:, 0:nb], in0=bmax[:, 0:nb], scalar1=1.0 / 7.0, scalar2=1e-12,
            op0=ALU.mult, op1=ALU.max))
        vgap()
        K.emit("vector", lambda e: e.reciprocal(out=brec[:, 0:nb], in_=bmax[:, 0:nb]))
        vgap()
        wst_free[i2] = K.emit("vector", lambda e: e.tensor_tensor(
            out=wlp[:, 0:n_in].rearrange("p (b t) -> p b t", b=nb),
            in0=wst[i2][:, 0:n_in].rearrange("p (b t) -> p b t", b=nb),
            in1=brec[:, 0:nb].rearrange("p (b o) -> p b o", o=1).broadcast_to([128, nb, 32]),
            op=ALU.mult), sV)
        K.emit("vector", lambda e: e.tensor_scalar(
            out=wlp[:, 0:n_in], in0=wlp[:, 0:n_in], scalar1=MAGIC, scalar2=MAGIC,
            op0=ALU.add, op1=ALU.subtract))
        K.emit("vector", lambda e: e.tensor_scalar(
            out=wlp[:, 0:n_in], in0=wlp[:, 0:n_in], scalar1=7.0, scalar2=-8.0,
            op0=ALU.min, op1=ALU.max))
        cast_ev = K.emit("vector", lambda e: e.tensor_tensor(
            out=wbf[i2][:, 0:n_in].rearrange("p (b t) -> p b t", b=nb),
            in0=wlp[:, 0:n_in].rearrange("p (b t) -> p b t", b=nb),
            in1=bmax[:, 0:nb].rearrange("p (b o) -> p b o", o=1).broadcast_to([128, nb, 32]),
            op=ALU.mult), sV)
        cpevs = []
        tev = None
        for ic in range(nbk):
            nm, ps = next_mm()
            K.wait("tensor", cast_ev, free[nm], idnbf_ev)
            tev = K.emit("tensor", lambda e, ps=ps, i2=i2, ic=ic: e.transpose(
                ps.bitcast(BF16)[:, 0:128], wbf[i2][:, ic * 128:(ic + 1) * 128], idn_bf), sT)
            K.wait("vector", tev, wtrb_free[i2] if ic == 0 else None)
            cpev = K.emit("vector", lambda e, ps=ps, i2=i2, ic=ic: e.tensor_copy(
                out=wtrb[i2][:, ic * 128:(ic + 1) * 128], in_=ps.bitcast(BF16)[:, 0:128]), sV)
            free[nm] = cpev
            cpevs.append(cpev)
        wbf_free[i2] = tev
        K.wait("sync", *cpevs)
        sev = dma(dst_ap_fn(oc), wtrb[i2][:, 0:nbk * 128].rearrange("p (b f) -> p b f", f=128))
        wtrb_free[i2] = sev
        prep_stores.append(sev)

    wready = {}
    lpbq_thunks = deque()
    ci = 0
    for oc in range(NFF):
        lpbq_thunks.append(lambda oc=oc, ci=ci: lpbq_chunk(
            w_up, oc, D, lambda oc: wupT_s[oc].transpose([1, 0, 2]), ci,
        )); ci += 1
        lpbq_thunks.append(("mark", ("up", oc)))
    for oc in range(NFF):
        lpbq_thunks.append(lambda oc=oc, ci=ci: lpbq_chunk(
            w_gate, oc, D, lambda oc: wgateT_s[oc].transpose([1, 0, 2]), ci,
        )); ci += 1
        lpbq_thunks.append(("mark", ("gate", oc)))
    for oc in range(NCH):
        for sub in range(3):
            lpbq_thunks.append(lambda oc=oc, ci=ci, sub=sub: lpbq_chunk(
                w_down, oc, D,
                lambda oc, sub=sub: wdownT_s[sub * 8:(sub + 1) * 8, :,
                                             oc * 128:(oc + 1) * 128].transpose([1, 0, 2]),
                ci, c0=sub * D)); ci += 1
    lpbq_thunks.append(("mark", ("down", 0)))

    pp = ExitStack()
    Ap = lambda name, shape, dt: pp.enter_context(nc.sbuf_tensor(name, shape, dt)).ap()
    px2 = [Ap("px0", [128, 4096], F32)] * 2
    pxn = Ap("pxn", [128, 4096], F32)
    pfm = Ap("pfm", [128, 4096], F32)
    psq = pfm
    pbf = Ap("pbf", [128, 4096], BF16)
    stt = Ap("stt", [128, 48], F32)
    sum4 = stt[:, 0:4]; sq4 = stt[:, 4:8]; m4 = stt[:, 8:12]; a4 = stt[:, 12:16]
    y4 = stt[:, 16:20]; t4a = stt[:, 20:24]; e4 = stt[:, 28:32]; xm1 = stt[:, 32:33]

    xns_ev = [None] * NT
    xnb_ev = [None] * NT
    pp_st = []
    px_free = [None, None]
    lx_evs = [None] * NT
    for tile in range(NT):
        t0 = tile * T
        K.wait("sync", px_free[0], *pp_st[-2:])
        lx_evs[tile] = dma(px2[tile % 2].rearrange("p (j f) -> p j f", j=4),
                           x_in[t0:t0 + T, :].rearrange("(j p) f -> p j f", p=128))
        px = px2[tile % 2]
        lx = lx_evs[tile]
        K.wait("vector", lx)
        K.emit("vector", lambda e, px=px: e.reduce_sum(
            out=sum4, in_=px.rearrange("p (j f) -> p j f", j=4), axis=AX.X))
        K.emit("vector", lambda e, px=px: e.tensor_tensor(out=psq, in0=px, in1=px, op=ALU.mult))
        K.emit("vector", lambda e: e.reduce_sum(
            out=sq4, in_=psq.rearrange("p (j f) -> p j f", j=4), axis=AX.X))
        K.emit("vector", lambda e: e.tensor_scalar(
            out=m4, in0=sum4, scalar1=1.0 / D, scalar2=None, op0=ALU.mult))
        vgap()
        K.emit("vector", lambda e: e.tensor_scalar(
            out=a4, in0=sq4, scalar1=1.0 / D, scalar2=None, op0=ALU.mult))
        K.emit("vector", lambda e: e.tensor_tensor(out=e4, in0=m4, in1=m4, op=ALU.mult))
        vgap()
        K.emit("vector", lambda e: e.tensor_tensor(out=a4, in0=a4, in1=e4, op=ALU.subtract))
        vgap()
        K.emit("vector", lambda e: e.tensor_scalar(
            out=a4, in0=a4, scalar1=1e-5, scalar2=None, op0=ALU.add))
        vgap()
        K.emit("vector", lambda e: e.reciprocal(out=y4, in_=a4))
        vgap()
        for _ in range(6):
            K.emit("vector", lambda e: e.tensor_tensor(out=t4a, in0=y4, in1=y4, op=ALU.mult))
            vgap()
            K.emit("vector", lambda e: e.tensor_tensor(out=t4a, in0=t4a, in1=a4, op=ALU.mult))
            vgap()
            K.emit("vector", lambda e: e.tensor_scalar(
                out=t4a, in0=t4a, scalar1=-0.5, scalar2=1.5, op0=ALU.mult, op1=ALU.add))
            vgap()
            K.emit("vector", lambda e: e.tensor_tensor(out=y4, in0=y4, in1=t4a, op=ALU.mult))
            vgap()
        vxn = None
        for j in range(4):
            vxn = K.emit("vector", lambda e, j=j, px=px: e.tensor_scalar(
                out=pxn[:, j * 1024:(j + 1) * 1024], in0=px[:, j * 1024:(j + 1) * 1024],
                scalar1=m4[:, j:j + 1], scalar2=y4[:, j:j + 1],
                op0=ALU.subtract, op1=ALU.mult), sV)
        K.emit("vector", lambda e: e.reduce_max(
            out=xm1, in_=pxn, axis=AX.X, apply_absolute_value=True))
        vgap()
        K.emit("vector", lambda e: e.tensor_tensor(out=mx_xn, in0=mx_xn, in1=xm1, op=ALU.max))
        vfm = []
        for c in range(NCH):
            nm, ps = next_ms()
            K.wait("tensor", vxn, free[nm])
            tev = None
            for j in range(4):
                tev = K.emit("tensor", lambda e, ps=ps, j=j, c=c: e.transpose(
                    ps[:, j * 128:(j + 1) * 128],
                    pxn[:, j * 1024 + c * 128: j * 1024 + (c + 1) * 128], idn), sT)
            K.wait("vector", tev)
            K.emit("vector", lambda e, ps=ps, c=c: e.tensor_copy(
                out=pfm[:, c * 512:(c + 1) * 512], in_=ps))
            ev = K.emit("vector", lambda e, ps=ps, c=c: e.tensor_copy(
                out=pbf[:, c * 512:(c + 1) * 512], in_=ps), sV)
            free[nm] = ev
            vfm.append(ev)
        px_free[0] = vxn
        K.wait("sync", *vfm)
        xns_ev[tile] = dma(xn_s[tile].transpose([1, 0, 2]),
                           pfm.rearrange("p (c t) -> p c t", c=NCH))
        xnb_ev[tile] = dma(xnb_s[tile].transpose([1, 0, 2]),
                           pbf.rearrange("p (c t) -> p c t", c=NCH))
        pp_st = [xns_ev[tile], xnb_ev[tile]]
    pp.close()
    K.barrier(sB)

    def allreduce_max(idx, srcs, n_vals, wait_evs=()):
        pk = []
        for i, src in enumerate(srcs):
            K.wait("sync", *wait_evs)
            ev0 = dma(xp_s, src)
            K.wait("sync", ev0)
            ev = dma(mxrow[0:1, 0:128], xp_s.rearrange("p o -> o p"))
            K.wait("vector", ev)
            pk.append(K.emit("vector", lambda e, i=i: e.reduce_max(
                out=scals[0:1, i:i + 1], in_=mxrow, axis=AX.X), sV))
            vgap()
        z = K.emit("vector", lambda e, n_vals=n_vals: e.memset(scals[0:1, n_vals:8], 0.0), sV)
        K.wait("gpsimd", z, *pk)
        gev = K.emit("gpsimd", lambda e, idx=idx: e.dma_start(
            out=cc_in[idx], in_=scals[0:1, 0:16]), sG, 16)
        K.wait("gpsimd", gev)
        cev = K.emit("gpsimd", lambda e, idx=idx: e.collective_compute(
            "AllReduce", ALU.max, replica_groups=[list(range(n_cores))],
            ins=[cc_in[idx]], outs=[cc_out[idx]]), sC, 1)
        K.wait("gpsimd", cev)
        return K.emit("gpsimd", lambda e, idx=idx: e.dma_start(
            out=scals[0:1, 0:16], in_=cc_out[idx]), sG, 16)

    def bcast(dst, src_col, wait_ev):
        nm, ps = next_ms()
        K.wait("tensor", wait_ev, free[nm])
        mev = K.emit("tensor", lambda e, ps=ps, src_col=src_col: e.matmul(
            ps[:, 0:1], ones1, scals[0:1, src_col:src_col + 1], start=True, stop=True), sT)
        K.wait("vector", mev)
        ev = K.emit("vector", lambda e, ps=ps, dst=dst: e.tensor_copy(out=dst, in_=ps[:, 0:1]), sV)
        vgap()
        free[nm] = ev
        return ev

    ar1 = allreduce_max(0, [mx_xn], 1)
    K.wait("vector", ar1)
    K.emit("vector", lambda e: e.tensor_scalar(
        out=scals[0:1, 8:9], in0=scals[0:1, 0:1], scalar1=1.0 / 32767.0, scalar2=1e-12,
        op0=ALU.mult, op1=ALU.max))
    vgap()
    sx_ev = K.emit("vector", lambda e: e.reciprocal(out=scals[0:1, 9:10], in_=scals[0:1, 8:9]), sV)
    vgap()
    rx_ev = bcast(rx_b, 9, sx_ev)

    main = ExitStack()
    scopeATT = ExitStack()
    Am = lambda name, shape, dt: scopeATT.enter_context(nc.sbuf_tensor(name, shape, dt)).ap()
    x_j = Am("x_j", [128, 1024], F32)
    xn_bf = Am("xn_bf", [128, 4096], BF16)
    Qb = Am("Qb", [128, 4096], BF16)
    Kb = Am("Kb", [128, 4096], BF16)
    Vf = Am("Vf", [128, 4096], F32)
    attn_f = Vf
    Qrep2 = [Am(f"Qrep{i}", [128, 4096], BF16) for i in range(2)]
    qk2 = [Am(f"qk2_{i}", [128, 512], F32) for i in range(2)]
    DVb = Am("DVb", [128, 4096], BF16)
    dn = Am("dn", [128, 64], F32)
    dnpAll2 = [Am(f"dnpAll{i}", [128, 128], F32) for i in range(2)]
    rec = Am("rec", [128, 64], F32)
    numer = Am("numer", [16, 512], F32)
    recD = Am("recD", [16, 512], F32)
    Dg_bf = Am("Dg_bf", [16, 512], BF16)
    scopeB = ExitStack()
    scopeCD = ExitStack()
    Ab = lambda name, shape, dt: scopeB.enter_context(nc.sbuf_tensor(name, shape, dt)).ap()
    xn_rd = Ab("xn_rd", [128, 2048], F32)
    xnq = Ab("xnq", [128, 4096], BF16)
    NWUB = 7
    wub = [Ab(f"wub{i}", [128, NCH * 128], BF16) for i in range(NWUB)]
    spl = [Ab(f"spl{i}", [128, 512], F32) for i in range(3)]
    splmx = Ab("splmx", [128, 1], F32)

    bstate = {"tile": -1, "xnq_ev": None, "xnq_users": [], "xn_rd_user": None}
    wub_free = [None] * NWUB
    wub_ld = [None] * NWUB
    spl_free = [None] * 3
    wubi = [0]
    bdone_evs = []
    bchain_list = []

    def prefetch_wchain():
        i = wubi[0]
        if i >= len(bchain_list):
            return
        wubi[0] += 1
        tile, which, oc = bchain_list[i]
        wsel = i % NWUB
        w_src = wupT_s if which == 0 else wgateT_s
        wkey = ("up" if which == 0 else "gate", oc)
        K.wait("sync", wub_free[wsel], wready[wkey])
        wub_ld[wsel] = dma(wub[wsel].rearrange("p (c f) -> p c f", c=NCH),
                           w_src[oc].transpose([1, 0, 2]))

    def prep_xnq(tile):
        qev = None
        for h in range(2):
            K.wait("sync", xns_ev[tile],
                   *(bstate["xnq_users"] if h == 0 else []), bstate["xn_rd_user"])
            lev = dma(xn_rd.rearrange("p (c t) -> p c t", c=NCH // 2),
                      xn_s[tile, h * 4:(h + 1) * 4].transpose([1, 0, 2]))
            K.wait("vector", lev, rx_ev)
            K.emit("vector", lambda e: e.tensor_scalar(
                out=xn_rd, in0=xn_rd, scalar1=rx_b, scalar2=None, op0=ALU.mult))
            K.emit("vector", lambda e: e.tensor_scalar(
                out=xn_rd, in0=xn_rd, scalar1=MAGIC, scalar2=MAGIC,
                op0=ALU.add, op1=ALU.subtract))
            qev = K.emit("vector", lambda e, h=h: e.tensor_scalar(
                out=xnq[:, h * 2048:(h + 1) * 2048], in0=xn_rd,
                scalar1=32767.0, scalar2=-32768.0,
                op0=ALU.min, op1=ALU.max), sV)
            bstate["xn_rd_user"] = qev
        bstate["tile"] = tile
        bstate["xnq_ev"] = qev
        bstate["xnq_users"] = []

    def b_chain(ci, tile, which, oc):
        if bstate["tile"] != tile:
            prep_xnq(tile)
        while wubi[0] < min(ci + NWUB, len(bchain_list)):
            prefetch_wchain()
        wsel = ci % NWUB
        wev = wub_ld[wsel]
        dst_s = up_s if which == 0 else gate_s
        mxacc = mx_up if which == 0 else mx_gate
        nm, ps = next_mm()
        K.wait("tensor", free[nm], wev, bstate["xnq_ev"])
        mev = None
        for ic in range(NCH):
            mev = K.emit("tensor", lambda e, ps=ps, wsel=wsel, ic=ic: e.matmul(
                ps, wub[wsel][:, ic * 128:(ic + 1) * 128],
                xnq[:, ic * 512:(ic + 1) * 512],
                start=(ic == 0), stop=(ic == NCH - 1)), sT)
        wub_free[wsel] = mev
        bstate["xnq_users"].append(mev)
        ssel = (which * NFF + oc) % 3
        sb = spl[ssel]
        K.wait("vector", mev, spl_free[ssel])
        ev = K.emit("vector", lambda e, ps=ps, sb=sb: e.tensor_copy(out=sb, in_=ps), sV)
        free[nm] = ev
        K.emit("vector", lambda e, sb=sb: e.reduce_max(
            out=splmx, in_=sb, axis=AX.X, apply_absolute_value=True))
        vgap()
        mxev = K.emit("vector", lambda e, mxacc=mxacc: e.tensor_tensor(
            out=mxacc, in0=mxacc, in1=splmx, op=ALU.max), sV)
        K.wait("sync", ev)
        spl_free[ssel] = dma(dst_s[tile, oc], sb)
        bdone_evs.append(mxev)

    for tile in range(NT):
        for oc in range(NFF):
            bchain_list.append((tile, 0, oc))
            bchain_list.append((tile, 1, oc))
    filler = deque()
    for ci, (tile, which, oc) in enumerate(bchain_list):
        filler.append(lambda ci=ci, tile=tile, which=which, oc=oc: b_chain(ci, tile, which, oc))

    cur_tile = [0]

    def pump(n):
        if cur_tile[0] == 0:
            return
        for _ in range(n):
            if filler:
                filler.popleft()()

    cd = {}

    NBLK = NT * NFF
    BPG = 6
    gate_flat = gate_s.rearrange("n c p t -> (n c) p t")
    t_flat = t_s.rearrange("n c p t -> (n c) p t")
    up_flat = up_s.rearrange("n c p t -> (n c) p t")
    o_flat = o_s.rearrange("n c p t -> (n c) p t")

    state = {"t_st": None, "last_tev": None, "o_st": None, "last_oev": None,
             "mup_ev": None, "mg_ev": None, "sgh_ev": None, "mt_ev": None,
             "mo_ev": None, "so_ev": None}

    def emit_ar2():
        scopeB.close()
        Ac = lambda name, shape, dt: scopeCD.enter_context(nc.sbuf_tensor(name, shape, dt)).ap()
        cd["g_rd"] = Ac("g_rd", [128, 3072], F32)
        cd["sig_f"] = Ac("sig_f", [128, 3072], F32)
        cd["t_wr"] = Ac("t_wr", [128, 3072], F32)
        cd["cmx"] = Ac("cmx", [128, 1], F32)
        cd["bdone"] = (bstate["xnq_users"][-1] if bstate["xnq_users"] else None,
                       wub_free[0], wub_free[1], wub_free[2], wub_free[3])
        ar2 = allreduce_max(1, [mx_up, mx_gate], 2, wait_evs=spl_free)
        K.wait("vector", ar2)
        K.emit("vector", lambda e: e.tensor_tensor(
            out=scals[0:1, 0:2], in0=scals[0:1, 0:2],
            in1=scals[0:1, 8:9].broadcast_to([1, 2]), op=ALU.mult))
        vgap()
        K.emit("vector", lambda e: e.tensor_scalar(
            out=scals[0:1, 0:2], in0=scals[0:1, 0:2], scalar1=1.0 / 32767.0, scalar2=1e-12,
            op0=ALU.mult, op1=ALU.max))
        vgap()
        K.emit("vector", lambda e: e.tensor_copy(out=scals[0:1, 10:12], in_=scals[0:1, 0:2]))
        K.emit("vector", lambda e: e.reciprocal(out=scals[0:1, 2:4], in_=scals[0:1, 0:2]))
        vgap()
        K.emit("vector", lambda e: e.tensor_tensor(
            out=scals[0:1, 2:4], in0=scals[0:1, 2:4],
            in1=scals[0:1, 8:9].broadcast_to([1, 2]), op=ALU.mult))
        vgap()
        K.emit("vector", lambda e: e.tensor_scalar(
            out=scals[0:1, 4:5], in0=scals[0:1, 11:12], scalar1=0.5, scalar2=None, op0=ALU.mult))
        vgap()
        b2 = K.emit("vector", lambda e: e.tensor_scalar(
            out=scals[0:1, 12:13], in0=scals[0:1, 11:12], scalar1=1.0 / 65536.0, scalar2=None,
            op0=ALU.mult), sV)
        vgap()
        state["mup_ev"] = bcast(mup_b, 2, b2)
        state["mg_ev"] = bcast(mg_b, 3, state["mup_ev"])
        state["sgh_ev"] = bcast(sgh_b, 4, state["mg_ev"])

    vfiller = deque()

    def vpump(n):
        for _ in range(n):
            while vfiller and isinstance(vfiller[0], tuple):
                _, key = vfiller.popleft()
                wready[key] = prep_stores[-1]
            if vfiller:
                vfiller.popleft()()
        while vfiller and isinstance(vfiller[0], tuple):
            _, key = vfiller.popleft()
            wready[key] = prep_stores[-1]

    vfiller.extend(lpbq_thunks)
    lpbq_thunks = None

    def enqueue_C(g0, g1):
        g_rd, sig_f, t_wr, cmx = cd["g_rd"], cd["sig_f"], cd["t_wr"], cd["cmx"]

        def step_load(g):
            blk = g * BPG
            K.wait("sync", *spl_free, state["last_tev"])
            if g == 0:
                K.wait("tensor", *cd["bdone"])
                K.wait("sync", *cd["bdone"])
            cd["lev"] = dma(g_rd.rearrange("p (b t) -> p b t", b=BPG),
                            gate_flat[blk:blk + BPG].transpose([1, 0, 2]))

        def step(fn, *waits, sem=None):
            def run():
                K.wait("vector", *[state.get(w) if isinstance(w, str) else w for w in waits])
                ev = K.emit("vector", fn, sem)
                return ev
            return run

        for g in range(g0, g1):
            blk = g * BPG
            vfiller.append(lambda g=g: step_load(g))
            def c1(g=g):
                K.wait("vector", cd["lev"], state["mg_ev"], state["sgh_ev"], state["t_st"])
                K.emit("vector", lambda e: e.tensor_scalar(
                    out=g_rd, in0=g_rd, scalar1=mg_b, scalar2=None, op0=ALU.mult))
            def c2():
                K.emit("vector", lambda e: e.tensor_scalar(
                    out=g_rd, in0=g_rd, scalar1=MAGIC, scalar2=MAGIC, op0=ALU.add, op1=ALU.subtract))
            def c3():
                gq_ev = K.emit("vector", lambda e: e.tensor_scalar(
                    out=g_rd, in0=g_rd, scalar1=32767.0, scalar2=-32768.0,
                    op0=ALU.min, op1=ALU.max), sV)
                K.wait("scalar", gq_ev, state["sgh_ev"])
                cd["tn_ev"] = K.emit("scalar", lambda e: e.activation(
                    out=sig_f, in_=g_rd, func=AF.Tanh, scale=sgh_b), sS)
            def c4():
                K.wait("vector", cd["tn_ev"])
                K.emit("vector", lambda e: e.tensor_scalar(
                    out=sig_f, in0=sig_f, scalar1=32768.0, scalar2=32768.0,
                    op0=ALU.mult, op1=ALU.add))
            def c5():
                K.emit("vector", lambda e: e.tensor_scalar(
                    out=sig_f, in0=sig_f, scalar1=MAGIC, scalar2=MAGIC,
                    op0=ALU.add, op1=ALU.subtract))
            def c6():
                K.emit("vector", lambda e: e.tensor_scalar(
                    out=sig_f, in0=sig_f, scalar1=65535.0, scalar2=0.0,
                    op0=ALU.min, op1=ALU.max))
            def c7(blk=blk):
                t_ev = K.emit("vector", lambda e: e.tensor_tensor(
                    out=t_wr, in0=g_rd, in1=sig_f, op=ALU.mult), sV)
                state["last_tev"] = t_ev
            def c8():
                K.emit("vector", lambda e: e.reduce_max(
                    out=cmx, in_=t_wr, axis=AX.X, apply_absolute_value=True))
                vgap()
            def c9(blk=blk):
                K.emit("vector", lambda e: e.tensor_tensor(
                    out=mx_t, in0=mx_t, in1=cmx, op=ALU.max))
                K.wait("sync", state["last_tev"])
                state["t_st"] = dma(t_flat[blk:blk + BPG].transpose([1, 0, 2]),
                                    t_wr.rearrange("p (b t) -> p b t", b=BPG))
            for f in (c1, c2, c3, c4, c5, c6, c7, c8, c9):
                vfiller.append(f)

    def emit_ar3():
        ar3 = allreduce_max(2, [mx_t], 1)
        K.wait("vector", ar3)
        K.emit("vector", lambda e: e.tensor_tensor(
            out=scals[0:1, 0:1], in0=scals[0:1, 0:1], in1=scals[0:1, 12:13], op=ALU.mult))
        vgap()
        K.emit("vector", lambda e: e.tensor_scalar(
            out=scals[0:1, 13:14], in0=scals[0:1, 0:1], scalar1=1.0 / 32767.0, scalar2=1e-12,
            op0=ALU.mult, op1=ALU.max))
        vgap()
        K.emit("vector", lambda e: e.reciprocal(out=scals[0:1, 5:6], in_=scals[0:1, 13:14]))
        vgap()
        b3 = K.emit("vector", lambda e: e.tensor_tensor(
            out=scals[0:1, 5:6], in0=scals[0:1, 5:6], in1=scals[0:1, 12:13], op=ALU.mult), sV)
        vgap()
        state["mt_ev"] = bcast(mt_b, 5, b3)

    def enqueue_D(g0, g1):
        g_rd, sig_f, t_wr, cmx = cd["g_rd"], cd["sig_f"], cd["t_wr"], cd["cmx"]

        def d0(g):
            blk = g * BPG
            K.wait("sync", state["t_st"], state["last_oev"])
            cd["lev1"] = dma(t_wr.rearrange("p (b t) -> p b t", b=BPG),
                             t_flat[blk:blk + BPG].transpose([1, 0, 2]))
            cd["lev2"] = dma(g_rd.rearrange("p (b t) -> p b t", b=BPG),
                             up_flat[blk:blk + BPG].transpose([1, 0, 2]))

        def d1():
            K.wait("vector", cd["lev1"], cd["lev2"], state["mt_ev"], state["mup_ev"],
                   state["o_st"])

        for g in range(g0, g1):
            blk = g * BPG
            vfiller.append(lambda g=g: d0(g))
            vfiller.append(d1)
            for buf, mb in ((t_wr, mt_b), (g_rd, mup_b)):
                vfiller.append(lambda buf=buf, mb=mb: K.emit(
                    "vector", lambda e, buf=buf, mb=mb: e.tensor_scalar(
                        out=buf, in0=buf, scalar1=mb, scalar2=None, op0=ALU.mult)))
                vfiller.append(lambda buf=buf: K.emit(
                    "vector", lambda e, buf=buf: e.tensor_scalar(
                        out=buf, in0=buf, scalar1=MAGIC, scalar2=MAGIC,
                        op0=ALU.add, op1=ALU.subtract)))
                vfiller.append(lambda buf=buf: K.emit(
                    "vector", lambda e, buf=buf: e.tensor_scalar(
                        out=buf, in0=buf, scalar1=32767.0, scalar2=-32768.0,
                        op0=ALU.min, op1=ALU.max)))
            def d2():
                o_ev = K.emit("vector", lambda e: e.tensor_tensor(
                    out=sig_f, in0=t_wr, in1=g_rd, op=ALU.mult), sV)
                state["last_oev"] = o_ev
            def d3():
                K.emit("vector", lambda e: e.reduce_max(
                    out=cmx, in_=sig_f, axis=AX.X, apply_absolute_value=True))
                vgap()
            def d4(blk=blk):
                K.emit("vector", lambda e: e.tensor_tensor(
                    out=mx_o, in0=mx_o, in1=cmx, op=ALU.max))
                K.wait("sync", state["last_oev"])
                state["o_st"] = dma(o_flat[blk:blk + BPG].transpose([1, 0, 2]),
                                    sig_f.rearrange("p (b t) -> p b t", b=BPG))
            for f in (d2, d3, d4):
                vfiller.append(f)

    def emit_ar4():
        ar4 = allreduce_max(3, [mx_o], 1)
        K.wait("vector", ar4)
        K.emit("vector", lambda e: e.tensor_tensor(
            out=scals[0:1, 6:7], in0=scals[0:1, 13:14], in1=scals[0:1, 10:11], op=ALU.mult))
        vgap()
        K.emit("vector", lambda e: e.tensor_tensor(
            out=scals[0:1, 0:1], in0=scals[0:1, 0:1], in1=scals[0:1, 6:7], op=ALU.mult))
        vgap()
        K.emit("vector", lambda e: e.tensor_scalar(
            out=scals[0:1, 14:15], in0=scals[0:1, 0:1], scalar1=1.0 / 32767.0, scalar2=1e-12,
            op0=ALU.mult, op1=ALU.max))
        vgap()
        K.emit("vector", lambda e: e.reciprocal(out=scals[0:1, 7:8], in_=scals[0:1, 14:15]))
        vgap()
        b4 = K.emit("vector", lambda e: e.tensor_tensor(
            out=scals[0:1, 7:8], in0=scals[0:1, 7:8], in1=scals[0:1, 6:7], op=ALU.mult), sV)
        vgap()
        state["mo_ev"] = bcast(mo_b, 7, b4)
        soc = K.emit("vector", lambda e: e.tensor_copy(
            out=scals[0:1, 15:16], in_=scals[0:1, 14:15]), sV)
        vgap()
        state["so_ev"] = bcast(so_b, 15, soc)


    acc_ev = [None] * NT
    K.wait("tensor", *ld_ev, *qkvo_ready, selbf_ev)
    K.wait("scalar", *ld_ev)

    prev = {"x_tm": None, "xn_bf": None, "qkv_mm": None, "qrep": None,
            "scores": None, "dv_mm": None, "wo_mm": None, "att_tr": None,
            "acc_st": None, "dg": None, "qk_seg": None}

    for tile in range(NT):
        cur_tile[0] = tile
        t0 = tile * T
        K.wait("sync", xnb_ev[tile], prev["qkv_mm"])
        lxn = dma(xn_bf.rearrange("p (c t) -> p c t", c=NCH),
                  xnb_s[tile].transpose([1, 0, 2]))
        K.wait("tensor", lxn)
        qkv_ev = {}
        last_qkv_mm = None
        for name, Wsb, bias, dstbf in (("q", Wq, bqs, Qb), ("k", Wk, bks, Kb), ("v", Wv, bvs, Vf)):
            for oc in range(NCH):
                nm, ps = next_mm()
                K.wait("tensor", free[nm])
                if name == "q" and oc == 0:
                    K.wait("tensor", prev["scores"], prev["qk_seg"])
                if name == "v" and oc == 0:
                    K.wait("tensor", prev["wo_mm"])
                    K.wait("vector", prev["att_tr"])
                mev = None
                for kc in range(NCH):
                    mev = K.emit("tensor", lambda e, ps=ps, Wsb=Wsb, kc=kc, oc=oc: e.matmul(
                        ps, Wsb[:, kc * D + oc * 128: kc * D + (oc + 1) * 128],
                        xn_bf[:, kc * 512:(kc + 1) * 512],
                        start=(kc == 0), stop=(kc == NCH - 1)), sT)
                last_qkv_mm = mev
                K.wait("vector", mev)
                if name == "q" and oc == 0:
                    K.wait("vector", prev["qrep"])
                ev = K.emit("vector", lambda e, ps=ps, dstbf=dstbf, bias=bias, oc=oc: e.tensor_scalar(
                    out=dstbf[:, oc * 512:(oc + 1) * 512], in0=ps,
                    scalar1=bias[:, oc:oc + 1], scalar2=None, op0=ALU.add), sV)
                free[nm] = ev
                qkv_ev[(name, oc)] = ev
                pump(1)
        prev["qkv_mm"] = last_qkv_mm
        nm_d, ps_d = next_sc()
        K.wait("tensor", free[nm_d])
        dev = None
        for c in range(NCH):
            b = qk2[c % 2]
            K.wait("vector", qkv_ev[("q", c)], qkv_ev[("k", c)],
                   None if c < 2 else dev)
            qev = K.emit("vector", lambda e, b=b, c=c: e.tensor_tensor(
                out=b, in0=Qb[:, c * 512:(c + 1) * 512], in1=Kb[:, c * 512:(c + 1) * 512],
                op=ALU.mult), sV)
            K.wait("tensor", qev)
            dev = K.emit("tensor", lambda e, ps_d=ps_d, b=b, c=c: e.matmul(
                ps_d[0:16, 0:512], seg[:, c * 16:(c + 1) * 16], b,
                start=(c == 0), stop=(c == NCH - 1)), sT)
        prev["qk_seg"] = dev
        K.wait("scalar", dev, prev["dg"])
        numer_ev = K.emit("scalar", lambda e, ps_d=ps_d: e.activation(
            out=numer, in_=ps_d[0:16, 0:512], func=AF.Exp, scale=EXPSC), sS)
        free[nm_d] = numer_ev
        all_qk = [qkv_ev[(n, oc)] for n in ("q", "k") for oc in range(NCH)]
        last_sc = None
        grevs = [[], []]
        for hl in range(2):
            Qrep = Qrep2[hl]
            K.wait("sync", *all_qk, prev.get(f"qrep_free{hl}"))
            for dhalf in range(2):
                for gpar in range(2):
                    grevs[hl].append(K.emit("sync", lambda e, Qrep=Qrep, dhalf=dhalf, gpar=gpar, hl=hl: e.dma_start(
                        out=Qrep[dhalf * 64:(dhalf + 1) * 64, :]
                            .rearrange("p (k t) -> p k t", t=512)[:, :, gpar * 256:(gpar + 1) * 256],
                        in_=Qb[gpar * 64:(gpar + 1) * 64, :]
                            .rearrange("p (c t) -> p c t", t=512)[:, :, hl * 256:(hl + 1) * 256]),
                        sDma[dma_rr[0] % NSL], 16))
                    dma_rr[0] += 1
        prev["qrep"] = grevs[1][-1]
        for hl in range(2):
            Qrep = Qrep2[hl]
            dnpAll = dnpAll2[hl]
            dnp_free = prev.get(f"dnp_free{hl}")
            grev = grevs[hl]
            K.wait("tensor", *grev)
            first_exp = True
            for r in range(8):
                for v in range(2):
                    m = r * 2 + v
                    lhsA = Kb[0:64,
                              r * 512 + hl * 256 + v * 128: r * 512 + hl * 256 + (v + 1) * 128]
                    lhsB = Kb[64:128,
                              r * 512 + hl * 256 + v * 128: r * 512 + hl * 256 + (v + 1) * 128]
                    for grp in range(4):
                        nmA, psA = next_sc()
                        nmB, psB = next_sc()
                        K.wait("tensor", free[nmA], free[nmB])
                        sevA = sevB = None
                        for half in range(2):
                            a0 = grp * 1024 + half * 512
                            sevA = K.emit("tensor", lambda e, psA=psA, lhsA=lhsA, a0=a0, half=half, Qrep=Qrep: e.matmul(
                                psA[:, half * 512:(half + 1) * 512],
                                lhsA, Qrep[0:64, a0:a0 + 512],
                                start=True, stop=True), sT)
                            sevB = K.emit("tensor", lambda e, psB=psB, lhsB=lhsB, a0=a0, half=half, Qrep=Qrep: e.matmul(
                                psB[:, half * 512:(half + 1) * 512],
                                lhsB, Qrep[64:128, a0:a0 + 512],
                                start=True, stop=True), sT)
                        if first_exp:
                            K.wait("scalar", sevA, dnp_free)
                            first_exp = False
                        else:
                            K.wait("scalar", sevA)
                        eevA = K.emit("scalar", lambda e, psA=psA, m=m, grp=grp, dnpAll=dnpAll: e.activation(
                            out=dummy, in_=psA, func=AF.Exp, scale=EXPSC,
                            accum_out=dnpAll[:, m * 8 + grp: m * 8 + grp + 1]), sS)
                        free[nmA] = eevA
                        K.wait("scalar", sevB)
                        eevB = K.emit("scalar", lambda e, psB=psB, m=m, grp=grp, dnpAll=dnpAll: e.activation(
                            out=dummy, in_=psB, func=AF.Exp, scale=EXPSC,
                            accum_out=dnpAll[:, m * 8 + 4 + grp: m * 8 + 4 + grp + 1]), sS)
                        free[nmB] = eevB
                        pump(1)
                        vpump(1)
                    last_sc = sevB
            K.wait("vector", eevB)
            K.emit("vector", lambda e, hl=hl, dnpAll=dnpAll: e.reduce_sum(
                out=dn[:, hl * 32:(hl + 1) * 32].rearrange("p (r w) -> p r w", w=4)[:, :, 0:2],
                in_=dnpAll.rearrange("p (m g) -> p m g", g=8)[:, :, 0:4], axis=AX.X))
            prev[f"dnp_free{hl}"] = K.emit("vector", lambda e, hl=hl, dnpAll=dnpAll: e.reduce_sum(
                out=dn[:, hl * 32:(hl + 1) * 32].rearrange("p (r w) -> p r w", w=4)[:, :, 2:4],
                in_=dnpAll.rearrange("p (m g) -> p m g", g=8)[:, :, 4:8], axis=AX.X), sV)
            prev[f"qrep_free{hl}"] = last_sc
        prev["scores"] = last_sc
        vgap()
        rc_ev = K.emit("vector", lambda e: e.reciprocal(out=rec, in_=dn), sV)
        for hl in range(2):
            for v in range(2):
                nm, ps = next_ms()
                K.wait("tensor", rc_ev, free[nm])
                tev = K.emit("tensor", lambda e, ps=ps, hl=hl, v=v: e.transpose(
                    ps[0:16, 0:128],
                    rec[:, hl * 32:(hl + 1) * 32].rearrange("p (g w) -> p g w", w=2)[:, :, v],
                    idn), sT)
                K.wait("vector", tev)
                ev = K.emit("vector", lambda e, ps=ps, hl=hl, v=v: e.tensor_copy(
                    out=recD[:, hl * 256 + v * 128: hl * 256 + (v + 1) * 128],
                    in_=ps[0:16, 0:128]), sV)
                free[nm] = ev
        K.wait("vector", numer_ev)
        vgap()
        dg_ev = K.emit("vector", lambda e: e.tensor_tensor(
            out=Dg_bf, in0=numer, in1=recD, op=ALU.mult), sV)
        vgap()
        prev["dg"] = dg_ev
        dv_ev = []
        last_dv_mm = None
        for c in range(NCH):
            nm, ps = next_ms()
            K.wait("tensor", dg_ev, free[nm])
            bev = K.emit("tensor", lambda e, ps=ps, c=c: e.matmul(
                ps, sel_bf[:, c * 128:(c + 1) * 128], Dg_bf, start=True, stop=True), sT)
            last_dv_mm = bev
            K.wait("vector", bev, qkv_ev[("v", c)])
            ev = K.emit("vector", lambda e, ps=ps, c=c: e.tensor_tensor(
                out=DVb[:, c * 512:(c + 1) * 512], in0=ps,
                in1=Vf[:, c * 512:(c + 1) * 512], op=ALU.mult), sV)
            free[nm] = ev
            dv_ev.append(ev)
        prev["dv_mm"] = last_dv_mm
        at_ev = {}
        last_wo = None
        for oc in range(NCH):
            nm, ps = next_mm()
            K.wait("tensor", free[nm], *dv_ev)
            mev = None
            for kc in range(NCH):
                mev = K.emit("tensor", lambda e, ps=ps, kc=kc, oc=oc: e.matmul(
                    ps, Wo[:, kc * D + oc * 128: kc * D + (oc + 1) * 128],
                    DVb[:, kc * 512:(kc + 1) * 512],
                    start=(kc == 0), stop=(kc == NCH - 1)), sT)
            last_wo = mev
            K.wait("vector", mev)
            ev = K.emit("vector", lambda e, ps=ps, oc=oc: e.tensor_scalar(
                out=attn_f[:, oc * 512:(oc + 1) * 512], in0=ps,
                scalar1=bos[:, oc:oc + 1], scalar2=None, op0=ALU.add), sV)
            free[nm] = ev
            at_ev[oc] = ev
            pump(1)
        prev["wo_mm"] = last_wo
        last_tt = None
        acc_last = prev["acc_st"]
        for j in range(4):
            K.wait("sync", acc_last, last_tt)
            lxj = dma(x_j, x_in[t0 + j * 128: t0 + (j + 1) * 128, :])
            jt_ev = []
            for half in range(2):
                nm, ps = next_ms()
                K.wait("tensor", free[nm],
                       *[at_ev[half * 4 + c4] for c4 in range(4)])
                tev = None
                for c4 in range(4):
                    c = half * 4 + c4
                    tev = K.emit("tensor", lambda e, ps=ps, c=c, c4=c4, j=j: e.transpose(
                        ps[:, c4 * 128:(c4 + 1) * 128],
                        attn_f[:, c * 512 + j * 128: c * 512 + (j + 1) * 128], idn), sT)
                K.wait("vector", tev, lxj)
                ev = K.emit("vector", lambda e, ps=ps, half=half: e.tensor_tensor(
                    out=x_j[:, half * 512:(half + 1) * 512],
                    in0=ps, in1=x_j[:, half * 512:(half + 1) * 512],
                    op=ALU.add), sV)
                free[nm] = ev
                jt_ev.append(ev)
            last_tt = jt_ev[-1]
            K.wait("sync", *jt_ev)
            acc_last = dma(acc_s[tile, j], x_j)
            pump(1)
        prev["att_tr"] = last_tt
        acc_ev[tile] = acc_last
        prev["acc_st"] = acc_last
        pump(4)

        if tile == 3:
            pump(len(filler))
            emit_ar2()
            enqueue_C(0, NBLK // BPG)
        if tile == 5:
            vpump(len(vfiller))
            emit_ar3()
            enqueue_D(0, NBLK // BPG)
        if tile == 7:
            vpump(len(vfiller))
            emit_ar4()

    scopeCD.close()
    K.wait("sync", acc_ev[NT - 1], prev["scores"], prev["qk_seg"], prev["wo_mm"],
           prev["dg"], prev["att_tr"], prev["qrep"], prev["dv_mm"], prev["qkv_mm"])
    K.wait("vector", prev["scores"], prev["wo_mm"])
    scopeATT.close()
    prep.close()
    scopeW.close()
    Ae = lambda name, shape, dt: main.enter_context(nc.sbuf_tensor(name, shape, dt)).ap()
    o_ch2 = [Ae(f"o_ch{i}", [128, 2048], F32) for i in range(2)]
    oq_bf2 = [Ae(f"oq_bf{i}", [128, NFF * 512], BF16) for i in range(2)]
    WdT = Ae("WdT", [128, NFF * D], BF16)
    acc_rd2 = [Ae(f"acc_rd{i}", [128, 4096], F32) for i in range(2)]

    K.wait("sync", wready[("down", 0)])
    wd_ld = []
    for ic in range(NFF):
        wd_ld.append(dma(WdT[:, ic * D:(ic + 1) * D], wdownT_s[ic]))
    K.wait("vector", state["so_ev"], *wd_ld)
    wds_ev = K.emit("vector", lambda e: e.tensor_scalar(
        out=WdT, in0=WdT, scalar1=so_b, scalar2=None, op0=ALU.mult), sV)

    out_st = []
    bank_ap = [ps_sc[i][:, h * 512:(h + 1) * 512] for i in range(3) for h in range(2)] \
        + [ps_mm[0], ps_mm[1]]
    bank_free = [free["sc0"], free["sc0"], free["sc1"], free["sc1"],
                 free["sc2"], free["sc2"], free["mm0"], free["mm1"]]
    bank_last = [None] * 8
    och_user = [None, None]
    acc_user = [None, None]
    qdq_gate = [None, None]
    for tile in range(NT):
        oq_bf = oq_bf2[tile % 2]
        acc_rd = acc_rd2[tile % 2]
        K.wait("sync", acc_ev[tile], acc_user[tile % 2])
        aev = dma(acc_rd.rearrange("p (j f) -> p j f", j=4),
                  acc_s[tile].transpose([1, 0, 2]))
        for ch in range(6):
            o_ch = o_ch2[ch % 2]
            K.wait("sync", state["o_st"], och_user[ch % 2])
            state["o_st"] = None
            lev = dma(o_ch.rearrange("p (c t) -> p c t", c=4),
                      o_s[tile, ch * 4:(ch + 1) * 4].transpose([1, 0, 2]))
            K.wait("vector", lev, state["mo_ev"], qdq_gate[tile % 2] if ch == 0 else None)
            state["mo_ev"] = None
            K.emit("vector", lambda e, o_ch=o_ch: e.tensor_scalar(
                out=o_ch, in0=o_ch, scalar1=mo_b, scalar2=None, op0=ALU.mult))
            K.emit("vector", lambda e, o_ch=o_ch: e.tensor_scalar(
                out=o_ch, in0=o_ch, scalar1=MAGIC, scalar2=MAGIC, op0=ALU.add, op1=ALU.subtract))
            ev = K.emit("vector", lambda e, ch=ch, o_ch=o_ch, oq_bf=oq_bf: e.tensor_scalar(
                out=oq_bf[:, ch * 2048:(ch + 1) * 2048], in0=o_ch,
                scalar1=32767.0, scalar2=-32768.0, op0=ALU.min, op1=ALU.max), sV)
            och_user[ch % 2] = ev
            K.wait("tensor", ev, wds_ev if (tile == 0 and ch == 0) else None)
            for b in range(8):
                j, nf = b // 2, b % 2
                if ch == 0:
                    K.wait("tensor", bank_free[b])
                for ic2 in range(4):
                    ic = ch * 4 + ic2
                    bank_last[b] = K.emit("tensor", lambda e, b=b, ic=ic, j=j, nf=nf, oq_bf=oq_bf: e.matmul(
                        bank_ap[b], oq_bf[:, ic * 512 + j * 128: ic * 512 + (j + 1) * 128],
                        WdT[:, ic * D + nf * 512: ic * D + (nf + 1) * 512],
                        start=(ic == 0), stop=(ic == NFF - 1)), sT)
        qdq_gate[tile % 2] = bank_last[7]
        ts_ev = []
        for b in range(8):
            j, nf = b // 2, b % 2
            K.wait("vector", bank_last[b], aev if b == 0 else None)
            ev = K.emit("vector", lambda e, b=b, j=j, nf=nf, acc_rd=acc_rd: e.tensor_tensor(
                out=acc_rd[:, j * 1024 + nf * 512: j * 1024 + (nf + 1) * 512],
                in0=bank_ap[b], in1=acc_rd[:, j * 1024 + nf * 512: j * 1024 + (nf + 1) * 512],
                op=ALU.add), sV)
            bank_free[b] = ev
            ts_ev.append(ev)
        K.wait("sync", *ts_ev)
        st = dma(out[tile * T:(tile + 1) * T, :].rearrange("(j p) f -> p j f", p=128),
                 acc_rd.rearrange("p (j f) -> p j f", j=4))
        out_st.append(st)
        acc_user[tile % 2] = st
    main.close()

    K.wait("sync", *out_st)
    K.run()
    es.close()
    return nc


def make_consts():
    ident = np.eye(128, dtype=np.float32)
    seg16 = np.zeros((128, NCH * 16), np.float32)
    for c in range(NCH):
        for p in range(128):
            seg16[p, c * 16 + 2 * c + p // 64] = 1.0
    sel64 = np.zeros((16, NCH * 128), np.float32)
    for c in range(NCH):
        for p in range(128):
            sel64[2 * c + p // 64, c * 128 + p] = 1.0
    ones_row = np.ones((1, 128), np.float32)
    return ident, seg16, sel64, ones_row


_NC_CACHE = {}


def kernel(L0_attn_X_embed, L0_attn_W_q, L0_attn_b_q, L0_attn_W_k, L0_attn_b_k,
           L0_attn_W_v, L0_attn_b_v, L0_attn_W_o, L0_attn_b_o,
           L0_ffn_w_gate, L0_ffn_w_up, L0_ffn_w_down):
    n_cores = 8
    if n_cores not in _NC_CACHE:
        _NC_CACHE[n_cores] = build_nc(n_cores)
    nc = _NC_CACHE[n_cores]
    ident, seg16, sel64, ones_row = make_consts()
    x = np.ascontiguousarray(np.asarray(L0_attn_X_embed, dtype=np.float32))
    common = dict(
        w_q=np.ascontiguousarray(np.asarray(L0_attn_W_q, np.float32)),
        b_q=np.ascontiguousarray(np.asarray(L0_attn_b_q, np.float32)),
        w_k=np.ascontiguousarray(np.asarray(L0_attn_W_k, np.float32)),
        b_k=np.ascontiguousarray(np.asarray(L0_attn_b_k, np.float32)),
        w_v=np.ascontiguousarray(np.asarray(L0_attn_W_v, np.float32)),
        b_v=np.ascontiguousarray(np.asarray(L0_attn_b_v, np.float32)),
        w_o=np.ascontiguousarray(np.asarray(L0_attn_W_o, np.float32)),
        b_o=np.ascontiguousarray(np.asarray(L0_attn_b_o, np.float32)),
        w_gate=np.ascontiguousarray(np.asarray(L0_ffn_w_gate, np.float32)),
        w_up=np.ascontiguousarray(np.asarray(L0_ffn_w_up, np.float32)),
        w_down=np.ascontiguousarray(np.asarray(L0_ffn_w_down, np.float32)),
        ident=ident, seg16=seg16, sel64=sel64, ones_row=ones_row,
    )
    in_maps = [dict(x_in=np.ascontiguousarray(x[i]), **common) for i in range(n_cores)]
    res = run_bass_kernel_spmd(nc, in_maps, core_ids=list(range(n_cores)))
    return np.stack([r["out"] for r in res.results], axis=0)


# version 25
# speedup vs baseline: 1.1221x; 1.0267x over previous
import numpy as np
from collections import deque
from contextlib import ExitStack

import concourse.bass as bass
import concourse.mybir as mybir
from concourse.bass_utils import run_bass_kernel_spmd

F32 = mybir.dt.float32
BF16 = mybir.dt.bfloat16
AF = mybir.ActivationFunctionType
ALU = mybir.AluOpType
AX = mybir.AxisListType

S = 4096
D = 1024
DFF = 3072
T = 512
NT = S // T
NCH = D // 128
NFF = DFF // 128
MAGIC = 12582912.0
EXPSC = float(np.float32(1.0) / np.float32(1024.0))


class Sched:
    ENGS = ("sync", "tensor", "vector", "scalar", "gpsimd")

    def __init__(self, nc):
        self.nc = nc
        self.q = {e: [] for e in self.ENGS}
        self.cnt = {}

    def emit(self, eng, fn, sem=None, k=1):
        ev = None
        if sem is not None:
            self.cnt[sem.name] = self.cnt.get(sem.name, 0) + k
            ev = (sem, self.cnt[sem.name])
            self.q[eng].append(lambda e, fn=fn, sem=sem, k=k: fn(e).then_inc(sem, k))
        else:
            self.q[eng].append(fn)
        return ev

    def wait(self, eng, *evs):
        best = {}
        for e in evs:
            if e is None:
                continue
            sem, val = e
            if best.get(sem.name, (None, -1))[1] < val:
                best[sem.name] = (sem, val)
        for sem, val in best.values():
            self.q[eng].append(lambda e, sem=sem, val=val: e.wait_ge(sem, val))

    def barrier(self, sB):
        self.cnt[sB.name] = self.cnt.get(sB.name, 0) + len(self.ENGS)
        tgt = self.cnt[sB.name]
        for eng in self.ENGS:
            self.q[eng].append(lambda e, sB=sB: e.nop().then_inc(sB, 1))
            self.q[eng].append(lambda e, sB=sB, tgt=tgt: e.wait_ge(sB, tgt))

    def run(self):
        with self.nc.Block() as block:
            block.sync(lambda e: [f(e) for f in self.q["sync"]])
            block.tensor(lambda e: [f(e) for f in self.q["tensor"]])
            block.vector(lambda e: [f(e) for f in self.q["vector"]])
            block.scalar(lambda e: [f(e) for f in self.q["scalar"]])
            block.gpsimd(lambda e: [f(e) for f in self.q["gpsimd"]])


def build_nc(n_cores=8):
    nc = bass.Bass("TRN2", target_bir_lowering=False, detect_race_conditions=False)

    x_in = nc.dram_tensor("x_in", [S, D], F32, kind="ExternalInput").ap()
    w_q = nc.dram_tensor("w_q", [D, D], F32, kind="ExternalInput").ap()
    b_q = nc.dram_tensor("b_q", [D], F32, kind="ExternalInput").ap()
    w_k = nc.dram_tensor("w_k", [D, D], F32, kind="ExternalInput").ap()
    b_k = nc.dram_tensor("b_k", [D], F32, kind="ExternalInput").ap()
    w_v = nc.dram_tensor("w_v", [D, D], F32, kind="ExternalInput").ap()
    b_v = nc.dram_tensor("b_v", [D], F32, kind="ExternalInput").ap()
    w_o = nc.dram_tensor("w_o", [D, D], F32, kind="ExternalInput").ap()
    b_o = nc.dram_tensor("b_o", [D], F32, kind="ExternalInput").ap()
    w_gate = nc.dram_tensor("w_gate", [DFF, D], F32, kind="ExternalInput").ap()
    w_up = nc.dram_tensor("w_up", [DFF, D], F32, kind="ExternalInput").ap()
    w_down = nc.dram_tensor("w_down", [D, DFF], F32, kind="ExternalInput").ap()
    ident = nc.dram_tensor("ident", [128, 128], F32, kind="ExternalInput").ap()
    seg16 = nc.dram_tensor("seg16", [128, NCH * 16], F32, kind="ExternalInput").ap()
    sel64 = nc.dram_tensor("sel64", [16, NCH * 128], F32, kind="ExternalInput").ap()
    ones_row = nc.dram_tensor("ones_row", [1, 128], F32, kind="ExternalInput").ap()

    out = nc.dram_tensor("out", [S, D], F32, kind="ExternalOutput").ap()

    xn_s = nc.dram_tensor("xn_s", [NT, NCH, 128, T], F32, kind="Internal").ap()
    xnb_s = nc.dram_tensor("xnb_s", [NT, NCH, 128, T], BF16, kind="Internal").ap()
    acc_s = nc.dram_tensor("acc_s", [NT, 4, 128, D], F32, kind="Internal").ap()
    up_s = nc.dram_tensor("up_s", [NT, NFF, 128, T], F32, kind="Internal").ap()
    gate_s = nc.dram_tensor("gate_s", [NT, NFF, 128, T], F32, kind="Internal").ap()
    t_s = nc.dram_tensor("t_s", [NT, NFF, 128, T], F32, kind="Internal").ap()
    o_s = nc.dram_tensor("o_s", [NT, NFF, 128, T], F32, kind="Internal").ap()
    wupT_s = nc.dram_tensor("wupT_s", [NFF, NCH, 128, 128], BF16, kind="Internal").ap()
    wgateT_s = nc.dram_tensor("wgateT_s", [NFF, NCH, 128, 128], BF16, kind="Internal").ap()
    wdownT_s = nc.dram_tensor("wdownT_s", [NFF, 128, D], BF16, kind="Internal").ap()
    cc_in = [nc.dram_tensor(f"cc_in{i}", [1, 16], F32, kind="Internal").ap() for i in range(4)]
    cc_out = [nc.dram_tensor(f"cc_out{i}", [1, 16], F32, kind="Internal").ap() for i in range(4)]
    xp_s = nc.dram_tensor("xp_s", [128, 1], F32, kind="Internal").ap()

    ps_sc = [nc.alloc_psum_tensor(f"ps_sc{i}", [128, 1024], F32).ap() for i in range(3)]
    ps_mm = [nc.alloc_psum_tensor(f"ps_mm{i}", [128, 512], F32).ap() for i in range(2)]

    es = ExitStack()
    NSL = 28
    sDma = [es.enter_context(nc.semaphore(f"sD{i}")) for i in range(NSL)]
    sT = es.enter_context(nc.semaphore("sT"))
    sV = es.enter_context(nc.semaphore("sV"))
    sS = es.enter_context(nc.semaphore("sS"))
    sG = es.enter_context(nc.semaphore("sG"))
    sC = es.enter_context(nc.semaphore("sC"))
    sB = es.enter_context(nc.semaphore("sB"))

    K = Sched(nc)
    A = nc.alloc_sbuf_tensor

    idn = A("idn", [128, 128], F32).ap()
    idn_bf = A("idn_bf", [128, 128], BF16).ap()
    seg = A("seg", [128, NCH * 16], F32).ap()
    sel_bf = A("sel_bf", [16, NCH * 128], BF16).ap()
    ones1 = A("ones1", [1, 128], F32).ap()
    bqs = A("bqs", [128, NCH], F32).ap()
    bks = A("bks", [128, NCH], F32).ap()
    bvs = A("bvs", [128, NCH], F32).ap()
    bos = A("bos", [128, NCH], F32).ap()
    mx_xn = A("mx_xn", [128, 1], F32).ap()
    mx_up = A("mx_up", [128, 1], F32).ap()
    mx_gate = A("mx_gate", [128, 1], F32).ap()
    mx_t = A("mx_t", [128, 1], F32).ap()
    mx_o = A("mx_o", [128, 1], F32).ap()
    mxrow = A("mxrow", [1, 128], F32).ap()
    scals = A("scals", [1, 16], F32).ap()
    rx_b = A("rx_b", [128, 1], F32).ap()
    mup_b = A("mup_b", [128, 1], F32).ap()
    mg_b = A("mg_b", [128, 1], F32).ap()
    sgh_b = A("sgh_b", [128, 1], F32).ap()
    mt_b = A("mt_b", [128, 1], F32).ap()
    mo_b = A("mo_b", [128, 1], F32).ap()
    so_b = A("so_b", [128, 1], F32).ap()
    dummy = A("dump_exp", [128, 1024], BF16).ap()
    junkA = A("junkA", [128, 64], F32).ap()
    junkB = A("junkB", [128, 64], F32).ap()
    K.emit("vector", lambda e: e.memset(junkA, 0.0))

    def vgap(n=1):
        for _ in range(n):
            K.emit("vector", lambda e: e.tensor_copy(out=junkB, in_=junkA))

    dma_rr = [0]

    def dma(ap_out, ap_in, eng="sync", slow=False):
        sem = sDma[dma_rr[0] % NSL]
        dma_rr[0] += 1
        prev = K.cnt.get(sem.name, 0)
        if prev:
            K.q[eng].append(lambda e, sem=sem, prev=prev: e.wait_ge(sem, prev))
        return K.emit(eng, lambda e, o=ap_out, i=ap_in, s=slow: e.dma_start(
            out=o, in_=i, allow_slow_non_contiguous=s), sem, 16)

    selbf_ev = K.emit("gpsimd", lambda e: e.dma_start(out=sel_bf, in_=sel64), sG, 16)
    ld_ev = [dma(idn, ident), dma(seg, seg16), dma(ones1, ones_row),
             dma(bqs, b_q.rearrange("(c p) -> p c", p=128), slow=True),
             dma(bks, b_k.rearrange("(c p) -> p c", p=128), slow=True),
             dma(bvs, b_v.rearrange("(c p) -> p c", p=128), slow=True),
             dma(bos, b_o.rearrange("(c p) -> p c", p=128), slow=True)]
    K.emit("scalar", lambda e: e.preload_act_table(AF.Exp))
    for t in (mx_xn, mx_up, mx_gate, mx_t, mx_o, scals):
        K.emit("vector", lambda e, t=t: e.memset(t, 0.0))
    K.wait("vector", *ld_ev)
    idnbf_ev = K.emit("vector", lambda e: e.tensor_copy(out=idn_bf, in_=idn), sV)

    free = {"sc0": None, "sc1": None, "sc2": None, "mm0": None, "mm1": None}
    mmi = [0]
    sci = [0]

    def next_mm():
        i = mmi[0] % 2
        mmi[0] += 1
        return (f"mm{i}", ps_mm[i])

    next_ms = next_mm

    def next_sc():
        i = sci[0] % 3
        sci[0] += 1
        return (f"sc{i}", ps_sc[i])

    scopeW = ExitStack()
    Wq = scopeW.enter_context(nc.sbuf_tensor("Wq", [128, NCH * D], BF16)).ap()
    Wk = scopeW.enter_context(nc.sbuf_tensor("Wk", [128, NCH * D], BF16)).ap()
    Wv = scopeW.enter_context(nc.sbuf_tensor("Wv", [128, NCH * D], BF16)).ap()
    Wo = scopeW.enter_context(nc.sbuf_tensor("Wo", [128, NCH * D], BF16)).ap()

    prep = ExitStack()
    Aw = lambda name, shape, dt: prep.enter_context(nc.sbuf_tensor(name, shape, dt)).ap()
    NPB = 2
    wst = [Aw(f"wst{i}", [128, 1024], F32) for i in range(NPB)]
    wlp = Aw("wlp", [128, 1024], F32)
    bmax = Aw("bmax", [128, 32], F32)
    brec = Aw("brec", [128, 32], F32)
    wbf = [Aw(f"wbf{i}", [128, 1024], BF16) for i in range(NPB)]
    wtrb = [Aw(f"wtrb{i}", [128, 1024], BF16) for i in range(NPB)]

    wst_free = [None] * NPB
    for wi, (wdr, wsb) in enumerate(((w_q, Wq), (w_k, Wk), (w_v, Wv), (w_o, Wo))):
        for kc in range(NCH):
            i2 = (wi * NCH + kc) % NPB
            K.wait("sync", wst_free[i2])
            lev = dma(wst[i2][:, 0:D], wdr[kc * 128:(kc + 1) * 128, :])
            K.wait("vector", lev)
            wst_free[i2] = K.emit(
                "vector", lambda e, i2=i2, wsb=wsb, kc=kc: e.tensor_copy(
                    out=wsb[:, kc * D:(kc + 1) * D], in_=wst[i2][:, 0:D]), sV)
    qkvo_ready = wst_free[:]

    wbf_free = [None] * NPB
    wtrb_free = [None] * NPB
    prep_stores = []

    def lpbq_chunk(wdr, oc, n_in, dst_ap_fn, ci, c0=0):
        i2 = ci % NPB
        nb = n_in // 32
        nbk = n_in // 128
        K.wait("sync", wst_free[i2])
        lev = dma(wst[i2][:, 0:n_in], wdr[oc * 128:(oc + 1) * 128, c0:c0 + n_in])
        K.wait("vector", lev, wbf_free[i2])
        K.emit("vector", lambda e: e.reduce_max(
            out=bmax[:, 0:nb], in_=wst[i2][:, 0:n_in].rearrange("p (b t) -> p b t", b=nb),
            axis=AX.X, apply_absolute_value=True))
        vgap()
        K.emit("vector", lambda e: e.tensor_scalar(
            out=bmax[:, 0:nb], in0=bmax[:, 0:nb], scalar1=1.0 / 7.0, scalar2=1e-12,
            op0=ALU.mult, op1=ALU.max))
        vgap()
        K.emit("vector", lambda e: e.reciprocal(out=brec[:, 0:nb], in_=bmax[:, 0:nb]))
        vgap()
        wst_free[i2] = K.emit("vector", lambda e: e.tensor_tensor(
            out=wlp[:, 0:n_in].rearrange("p (b t) -> p b t", b=nb),
            in0=wst[i2][:, 0:n_in].rearrange("p (b t) -> p b t", b=nb),
            in1=brec[:, 0:nb].rearrange("p (b o) -> p b o", o=1).broadcast_to([128, nb, 32]),
            op=ALU.mult), sV)
        K.emit("vector", lambda e: e.tensor_scalar(
            out=wlp[:, 0:n_in], in0=wlp[:, 0:n_in], scalar1=MAGIC, scalar2=MAGIC,
            op0=ALU.add, op1=ALU.subtract))
        K.emit("vector", lambda e: e.tensor_scalar(
            out=wlp[:, 0:n_in], in0=wlp[:, 0:n_in], scalar1=7.0, scalar2=-8.0,
            op0=ALU.min, op1=ALU.max))
        cast_ev = K.emit("vector", lambda e: e.tensor_tensor(
            out=wbf[i2][:, 0:n_in].rearrange("p (b t) -> p b t", b=nb),
            in0=wlp[:, 0:n_in].rearrange("p (b t) -> p b t", b=nb),
            in1=bmax[:, 0:nb].rearrange("p (b o) -> p b o", o=1).broadcast_to([128, nb, 32]),
            op=ALU.mult), sV)
        cpevs = []
        tev = None
        for ic in range(nbk):
            nm, ps = next_mm()
            K.wait("tensor", cast_ev, free[nm], idnbf_ev)
            tev = K.emit("tensor", lambda e, ps=ps, i2=i2, ic=ic: e.transpose(
                ps.bitcast(BF16)[:, 0:128], wbf[i2][:, ic * 128:(ic + 1) * 128], idn_bf), sT)
            K.wait("vector", tev, wtrb_free[i2] if ic == 0 else None)
            cpev = K.emit("vector", lambda e, ps=ps, i2=i2, ic=ic: e.tensor_copy(
                out=wtrb[i2][:, ic * 128:(ic + 1) * 128], in_=ps.bitcast(BF16)[:, 0:128]), sV)
            free[nm] = cpev
            cpevs.append(cpev)
        wbf_free[i2] = tev
        K.wait("sync", *cpevs)
        sev = dma(dst_ap_fn(oc), wtrb[i2][:, 0:nbk * 128].rearrange("p (b f) -> p b f", f=128))
        wtrb_free[i2] = sev
        prep_stores.append(sev)

    wready = {}
    lpbq_thunks = deque()
    ci = 0
    for oc in range(NFF):
        lpbq_thunks.append(lambda oc=oc, ci=ci: lpbq_chunk(
            w_up, oc, D, lambda oc: wupT_s[oc].transpose([1, 0, 2]), ci,
        )); ci += 1
        lpbq_thunks.append(("mark", ("up", oc)))
    for oc in range(NFF):
        lpbq_thunks.append(lambda oc=oc, ci=ci: lpbq_chunk(
            w_gate, oc, D, lambda oc: wgateT_s[oc].transpose([1, 0, 2]), ci,
        )); ci += 1
        lpbq_thunks.append(("mark", ("gate", oc)))
    for oc in range(NCH):
        for sub in range(3):
            lpbq_thunks.append(lambda oc=oc, ci=ci, sub=sub: lpbq_chunk(
                w_down, oc, D,
                lambda oc, sub=sub: wdownT_s[sub * 8:(sub + 1) * 8, :,
                                             oc * 128:(oc + 1) * 128].transpose([1, 0, 2]),
                ci, c0=sub * D)); ci += 1
    lpbq_thunks.append(("mark", ("down", 0)))

    pp = ExitStack()
    Ap = lambda name, shape, dt: pp.enter_context(nc.sbuf_tensor(name, shape, dt)).ap()
    px2 = [Ap("px0", [128, 4096], F32)] * 2
    pxn = Ap("pxn", [128, 4096], F32)
    pfm = Ap("pfm", [128, 4096], F32)
    psq = pfm
    pbf = Ap("pbf", [128, 4096], BF16)
    stt = Ap("stt", [128, 48], F32)
    sum4 = stt[:, 0:4]; sq4 = stt[:, 4:8]; m4 = stt[:, 8:12]; a4 = stt[:, 12:16]
    y4 = stt[:, 16:20]; t4a = stt[:, 20:24]; e4 = stt[:, 28:32]; xm1 = stt[:, 32:33]

    xns_ev = [None] * NT
    xnb_ev = [None] * NT
    pp_st = []
    px_free = [None, None]
    lx_evs = [None] * NT
    for tile in range(NT):
        t0 = tile * T
        K.wait("sync", px_free[0], *pp_st[-2:])
        lx_evs[tile] = dma(px2[tile % 2].rearrange("p (j f) -> p j f", j=4),
                           x_in[t0:t0 + T, :].rearrange("(j p) f -> p j f", p=128))
        px = px2[tile % 2]
        lx = lx_evs[tile]
        K.wait("vector", lx)
        K.emit("vector", lambda e, px=px: e.reduce_sum(
            out=sum4, in_=px.rearrange("p (j f) -> p j f", j=4), axis=AX.X))
        K.emit("vector", lambda e, px=px: e.tensor_tensor(out=psq, in0=px, in1=px, op=ALU.mult))
        K.emit("vector", lambda e: e.reduce_sum(
            out=sq4, in_=psq.rearrange("p (j f) -> p j f", j=4), axis=AX.X))
        K.emit("vector", lambda e: e.tensor_scalar(
            out=m4, in0=sum4, scalar1=1.0 / D, scalar2=None, op0=ALU.mult))
        vgap()
        K.emit("vector", lambda e: e.tensor_scalar(
            out=a4, in0=sq4, scalar1=1.0 / D, scalar2=None, op0=ALU.mult))
        K.emit("vector", lambda e: e.tensor_tensor(out=e4, in0=m4, in1=m4, op=ALU.mult))
        vgap()
        K.emit("vector", lambda e: e.tensor_tensor(out=a4, in0=a4, in1=e4, op=ALU.subtract))
        vgap()
        K.emit("vector", lambda e: e.tensor_scalar(
            out=a4, in0=a4, scalar1=1e-5, scalar2=None, op0=ALU.add))
        vgap()
        K.emit("vector", lambda e: e.reciprocal(out=y4, in_=a4))
        vgap()
        for _ in range(6):
            K.emit("vector", lambda e: e.tensor_tensor(out=t4a, in0=y4, in1=y4, op=ALU.mult))
            vgap()
            K.emit("vector", lambda e: e.tensor_tensor(out=t4a, in0=t4a, in1=a4, op=ALU.mult))
            vgap()
            K.emit("vector", lambda e: e.tensor_scalar(
                out=t4a, in0=t4a, scalar1=-0.5, scalar2=1.5, op0=ALU.mult, op1=ALU.add))
            vgap()
            K.emit("vector", lambda e: e.tensor_tensor(out=y4, in0=y4, in1=t4a, op=ALU.mult))
            vgap()
        vxn = None
        for j in range(4):
            vxn = K.emit("vector", lambda e, j=j, px=px: e.tensor_scalar(
                out=pxn[:, j * 1024:(j + 1) * 1024], in0=px[:, j * 1024:(j + 1) * 1024],
                scalar1=m4[:, j:j + 1], scalar2=y4[:, j:j + 1],
                op0=ALU.subtract, op1=ALU.mult), sV)
        K.emit("vector", lambda e: e.reduce_max(
            out=xm1, in_=pxn, axis=AX.X, apply_absolute_value=True))
        vgap()
        K.emit("vector", lambda e: e.tensor_tensor(out=mx_xn, in0=mx_xn, in1=xm1, op=ALU.max))
        vfm = []
        for c in range(NCH):
            nm, ps = next_ms()
            K.wait("tensor", vxn, free[nm])
            tev = None
            for j in range(4):
                tev = K.emit("tensor", lambda e, ps=ps, j=j, c=c: e.transpose(
                    ps[:, j * 128:(j + 1) * 128],
                    pxn[:, j * 1024 + c * 128: j * 1024 + (c + 1) * 128], idn), sT)
            K.wait("vector", tev)
            K.emit("vector", lambda e, ps=ps, c=c: e.tensor_copy(
                out=pfm[:, c * 512:(c + 1) * 512], in_=ps))
            ev = K.emit("vector", lambda e, ps=ps, c=c: e.tensor_copy(
                out=pbf[:, c * 512:(c + 1) * 512], in_=ps), sV)
            free[nm] = ev
            vfm.append(ev)
        px_free[0] = vxn
        K.wait("sync", *vfm)
        xns_ev[tile] = dma(xn_s[tile].transpose([1, 0, 2]),
                           pfm.rearrange("p (c t) -> p c t", c=NCH))
        xnb_ev[tile] = dma(xnb_s[tile].transpose([1, 0, 2]),
                           pbf.rearrange("p (c t) -> p c t", c=NCH))
        pp_st = [xns_ev[tile], xnb_ev[tile]]
    pp.close()
    K.barrier(sB)

    def allreduce_max(idx, srcs, n_vals, wait_evs=()):
        pk = []
        for i, src in enumerate(srcs):
            K.wait("sync", *wait_evs)
            ev0 = dma(xp_s, src)
            K.wait("sync", ev0)
            ev = dma(mxrow[0:1, 0:128], xp_s.rearrange("p o -> o p"))
            K.wait("vector", ev)
            pk.append(K.emit("vector", lambda e, i=i: e.reduce_max(
                out=scals[0:1, i:i + 1], in_=mxrow, axis=AX.X), sV))
            vgap()
        z = K.emit("vector", lambda e, n_vals=n_vals: e.memset(scals[0:1, n_vals:8], 0.0), sV)
        K.wait("gpsimd", z, *pk)
        gev = K.emit("gpsimd", lambda e, idx=idx: e.dma_start(
            out=cc_in[idx], in_=scals[0:1, 0:16]), sG, 16)
        K.wait("gpsimd", gev)
        cev = K.emit("gpsimd", lambda e, idx=idx: e.collective_compute(
            "AllReduce", ALU.max, replica_groups=[list(range(n_cores))],
            ins=[cc_in[idx]], outs=[cc_out[idx]]), sC, 1)
        K.wait("gpsimd", cev)
        return K.emit("gpsimd", lambda e, idx=idx: e.dma_start(
            out=scals[0:1, 0:16], in_=cc_out[idx]), sG, 16)

    def bcast(dst, src_col, wait_ev):
        nm, ps = next_ms()
        K.wait("tensor", wait_ev, free[nm])
        mev = K.emit("tensor", lambda e, ps=ps, src_col=src_col: e.matmul(
            ps[:, 0:1], ones1, scals[0:1, src_col:src_col + 1], start=True, stop=True), sT)
        K.wait("vector", mev)
        ev = K.emit("vector", lambda e, ps=ps, dst=dst: e.tensor_copy(out=dst, in_=ps[:, 0:1]), sV)
        vgap()
        free[nm] = ev
        return ev

    ar1 = allreduce_max(0, [mx_xn], 1)
    K.wait("vector", ar1)
    K.emit("vector", lambda e: e.tensor_scalar(
        out=scals[0:1, 8:9], in0=scals[0:1, 0:1], scalar1=1.0 / 32767.0, scalar2=1e-12,
        op0=ALU.mult, op1=ALU.max))
    vgap()
    sx_ev = K.emit("vector", lambda e: e.reciprocal(out=scals[0:1, 9:10], in_=scals[0:1, 8:9]), sV)
    vgap()
    rx_ev = bcast(rx_b, 9, sx_ev)

    main = ExitStack()
    scopeATT = ExitStack()
    Am = lambda name, shape, dt: scopeATT.enter_context(nc.sbuf_tensor(name, shape, dt)).ap()
    x_j = Am("x_j", [128, 1024], F32)
    xn_bf = Am("xn_bf", [128, 4096], BF16)
    Qb = Am("Qb", [128, 4096], BF16)
    Kb = Am("Kb", [128, 4096], BF16)
    Vf = Am("Vf", [128, 4096], F32)
    attn_f = Vf
    Qrep2 = [Am(f"Qrep{i}", [128, 4096], BF16) for i in range(2)]
    qk2 = [Am(f"qk2_{i}", [128, 512], F32) for i in range(2)]
    DVb = Am("DVb", [128, 4096], BF16)
    dn = Am("dn", [128, 64], F32)
    dnpAll2 = [Am(f"dnpAll{i}", [128, 128], F32) for i in range(2)]
    rec = Am("rec", [128, 64], F32)
    numer = Am("numer", [16, 512], F32)
    recD = Am("recD", [16, 512], F32)
    Dg_bf = Am("Dg_bf", [16, 512], BF16)
    scopeB = ExitStack()
    scopeCD = ExitStack()
    Ab = lambda name, shape, dt: scopeB.enter_context(nc.sbuf_tensor(name, shape, dt)).ap()
    xn_rd = Ab("xn_rd", [128, 2048], F32)
    xnq = Ab("xnq", [128, 4096], BF16)
    NWUB = 7
    wub = [Ab(f"wub{i}", [128, NCH * 128], BF16) for i in range(NWUB)]
    spl = [Ab(f"spl{i}", [128, 512], F32) for i in range(3)]
    splmx = Ab("splmx", [128, 1], F32)

    bstate = {"tile": -1, "xnq_ev": None, "xnq_users": [], "xn_rd_user": None}
    wub_free = [None] * NWUB
    wub_ld = [None] * NWUB
    spl_free = [None] * 3
    wubi = [0]
    bdone_evs = []
    bchain_list = []

    def prefetch_wchain():
        i = wubi[0]
        if i >= len(bchain_list):
            return
        wubi[0] += 1
        tile, which, oc = bchain_list[i]
        wsel = i % NWUB
        w_src = wupT_s if which == 0 else wgateT_s
        wkey = ("up" if which == 0 else "gate", oc)
        K.wait("sync", wub_free[wsel], wready[wkey])
        wub_ld[wsel] = dma(wub[wsel].rearrange("p (c f) -> p c f", c=NCH),
                           w_src[oc].transpose([1, 0, 2]))

    def prep_xnq(tile):
        qev = None
        for h in range(2):
            K.wait("sync", xns_ev[tile],
                   *(bstate["xnq_users"] if h == 0 else []), bstate["xn_rd_user"])
            lev = dma(xn_rd.rearrange("p (c t) -> p c t", c=NCH // 2),
                      xn_s[tile, h * 4:(h + 1) * 4].transpose([1, 0, 2]))
            K.wait("vector", lev, rx_ev)
            K.emit("vector", lambda e: e.tensor_scalar(
                out=xn_rd, in0=xn_rd, scalar1=rx_b, scalar2=None, op0=ALU.mult))
            K.emit("vector", lambda e: e.tensor_scalar(
                out=xn_rd, in0=xn_rd, scalar1=MAGIC, scalar2=MAGIC,
                op0=ALU.add, op1=ALU.subtract))
            qev = K.emit("vector", lambda e, h=h: e.tensor_scalar(
                out=xnq[:, h * 2048:(h + 1) * 2048], in0=xn_rd,
                scalar1=32767.0, scalar2=-32768.0,
                op0=ALU.min, op1=ALU.max), sV)
            bstate["xn_rd_user"] = qev
        bstate["tile"] = tile
        bstate["xnq_ev"] = qev
        bstate["xnq_users"] = []

    def b_chain(ci, tile, which, oc):
        if bstate["tile"] != tile:
            prep_xnq(tile)
        while wubi[0] < min(ci + NWUB, len(bchain_list)):
            prefetch_wchain()
        wsel = ci % NWUB
        wev = wub_ld[wsel]
        dst_s = up_s if which == 0 else gate_s
        mxacc = mx_up if which == 0 else mx_gate
        nm, ps = next_mm()
        K.wait("tensor", free[nm], wev, bstate["xnq_ev"])
        mev = None
        for ic in range(NCH):
            mev = K.emit("tensor", lambda e, ps=ps, wsel=wsel, ic=ic: e.matmul(
                ps, wub[wsel][:, ic * 128:(ic + 1) * 128],
                xnq[:, ic * 512:(ic + 1) * 512],
                start=(ic == 0), stop=(ic == NCH - 1)), sT)
        wub_free[wsel] = mev
        bstate["xnq_users"].append(mev)
        ssel = (which * NFF + oc) % 3
        sb = spl[ssel]
        K.wait("vector", mev, spl_free[ssel])
        ev = K.emit("vector", lambda e, ps=ps, sb=sb: e.tensor_copy(out=sb, in_=ps), sV)
        free[nm] = ev
        K.emit("vector", lambda e, sb=sb: e.reduce_max(
            out=splmx, in_=sb, axis=AX.X, apply_absolute_value=True))
        vgap()
        mxev = K.emit("vector", lambda e, mxacc=mxacc: e.tensor_tensor(
            out=mxacc, in0=mxacc, in1=splmx, op=ALU.max), sV)
        K.wait("sync", ev)
        spl_free[ssel] = dma(dst_s[tile, oc], sb)
        bdone_evs.append(mxev)

    for tile in range(NT):
        for oc in range(NFF):
            bchain_list.append((tile, 0, oc))
            bchain_list.append((tile, 1, oc))
    filler = deque()
    for ci, (tile, which, oc) in enumerate(bchain_list):
        filler.append(lambda ci=ci, tile=tile, which=which, oc=oc: b_chain(ci, tile, which, oc))

    cur_tile = [0]

    def pump(n):
        if cur_tile[0] == 0:
            return
        for _ in range(n):
            if filler:
                filler.popleft()()

    cd = {}

    NBLK = NT * NFF
    BPG = 6
    gate_flat = gate_s.rearrange("n c p t -> (n c) p t")
    t_flat = t_s.rearrange("n c p t -> (n c) p t")
    up_flat = up_s.rearrange("n c p t -> (n c) p t")
    o_flat = o_s.rearrange("n c p t -> (n c) p t")

    state = {"t_st": None, "last_tev": None, "o_st": None, "last_oev": None,
             "mup_ev": None, "mg_ev": None, "sgh_ev": None, "mt_ev": None,
             "mo_ev": None, "so_ev": None}

    def emit_ar2():
        scopeB.close()
        Ac = lambda name, shape, dt: scopeCD.enter_context(nc.sbuf_tensor(name, shape, dt)).ap()
        cd["g_rd"] = Ac("g_rd", [128, 3072], F32)
        cd["sig_f"] = Ac("sig_f", [128, 3072], F32)
        cd["t_wr"] = Ac("t_wr", [128, 3072], F32)
        cd["cmx"] = Ac("cmx", [128, 1], F32)
        cd["bdone"] = (bstate["xnq_users"][-1] if bstate["xnq_users"] else None,
                       wub_free[0], wub_free[1], wub_free[2], wub_free[3])
        ar2 = allreduce_max(1, [mx_up, mx_gate], 2, wait_evs=spl_free)
        K.wait("vector", ar2)
        K.emit("vector", lambda e: e.tensor_tensor(
            out=scals[0:1, 0:2], in0=scals[0:1, 0:2],
            in1=scals[0:1, 8:9].broadcast_to([1, 2]), op=ALU.mult))
        vgap()
        K.emit("vector", lambda e: e.tensor_scalar(
            out=scals[0:1, 0:2], in0=scals[0:1, 0:2], scalar1=1.0 / 32767.0, scalar2=1e-12,
            op0=ALU.mult, op1=ALU.max))
        vgap()
        K.emit("vector", lambda e: e.tensor_copy(out=scals[0:1, 10:12], in_=scals[0:1, 0:2]))
        K.emit("vector", lambda e: e.reciprocal(out=scals[0:1, 2:4], in_=scals[0:1, 0:2]))
        vgap()
        K.emit("vector", lambda e: e.tensor_tensor(
            out=scals[0:1, 2:4], in0=scals[0:1, 2:4],
            in1=scals[0:1, 8:9].broadcast_to([1, 2]), op=ALU.mult))
        vgap()
        K.emit("vector", lambda e: e.tensor_scalar(
            out=scals[0:1, 4:5], in0=scals[0:1, 11:12], scalar1=0.5, scalar2=None, op0=ALU.mult))
        vgap()
        b2 = K.emit("vector", lambda e: e.tensor_scalar(
            out=scals[0:1, 12:13], in0=scals[0:1, 11:12], scalar1=1.0 / 65536.0, scalar2=None,
            op0=ALU.mult), sV)
        vgap()
        state["mup_ev"] = bcast(mup_b, 2, b2)
        state["mg_ev"] = bcast(mg_b, 3, state["mup_ev"])
        state["sgh_ev"] = bcast(sgh_b, 4, state["mg_ev"])

    vfiller = deque()

    def vpump(n):
        for _ in range(n):
            while vfiller and isinstance(vfiller[0], tuple):
                _, key = vfiller.popleft()
                wready[key] = prep_stores[-1]
            if vfiller:
                vfiller.popleft()()
        while vfiller and isinstance(vfiller[0], tuple):
            _, key = vfiller.popleft()
            wready[key] = prep_stores[-1]

    vfiller.extend(lpbq_thunks)
    lpbq_thunks = None

    def enqueue_C(g0, g1):
        g_rd, sig_f, t_wr, cmx = cd["g_rd"], cd["sig_f"], cd["t_wr"], cd["cmx"]

        def step_load(g):
            blk = g * BPG
            K.wait("sync", *spl_free, state["last_tev"])
            if g == 0:
                K.wait("tensor", *cd["bdone"])
                K.wait("sync", *cd["bdone"])
            cd["lev"] = dma(g_rd.rearrange("p (b t) -> p b t", b=BPG),
                            gate_flat[blk:blk + BPG].transpose([1, 0, 2]))

        def step(fn, *waits, sem=None):
            def run():
                K.wait("vector", *[state.get(w) if isinstance(w, str) else w for w in waits])
                ev = K.emit("vector", fn, sem)
                return ev
            return run

        for g in range(g0, g1):
            blk = g * BPG
            vfiller.append(lambda g=g: step_load(g))
            def c1(g=g):
                K.wait("vector", cd["lev"], state["mg_ev"], state["sgh_ev"], state["t_st"])
                K.emit("vector", lambda e: e.tensor_scalar(
                    out=g_rd, in0=g_rd, scalar1=mg_b, scalar2=None, op0=ALU.mult))
            def c2():
                K.emit("vector", lambda e: e.tensor_scalar(
                    out=g_rd, in0=g_rd, scalar1=MAGIC, scalar2=MAGIC, op0=ALU.add, op1=ALU.subtract))
            def c3():
                gq_ev = K.emit("vector", lambda e: e.tensor_scalar(
                    out=g_rd, in0=g_rd, scalar1=32767.0, scalar2=-32768.0,
                    op0=ALU.min, op1=ALU.max), sV)
                K.wait("scalar", gq_ev, state["sgh_ev"])
                cd["tn_ev"] = K.emit("scalar", lambda e: e.activation(
                    out=sig_f, in_=g_rd, func=AF.Tanh, scale=sgh_b), sS)
            def c4():
                K.wait("vector", cd["tn_ev"])
                K.emit("vector", lambda e: e.tensor_scalar(
                    out=sig_f, in0=sig_f, scalar1=32768.0, scalar2=32768.0,
                    op0=ALU.mult, op1=ALU.add))
            def c5():
                K.emit("vector", lambda e: e.tensor_scalar(
                    out=sig_f, in0=sig_f, scalar1=MAGIC, scalar2=MAGIC,
                    op0=ALU.add, op1=ALU.subtract))
            def c6():
                K.emit("vector", lambda e: e.tensor_scalar(
                    out=sig_f, in0=sig_f, scalar1=65535.0, scalar2=0.0,
                    op0=ALU.min, op1=ALU.max))
            def c7(blk=blk):
                t_ev = K.emit("vector", lambda e: e.tensor_tensor(
                    out=t_wr, in0=g_rd, in1=sig_f, op=ALU.mult), sV)
                state["last_tev"] = t_ev
            def c8():
                K.emit("vector", lambda e: e.reduce_max(
                    out=cmx, in_=t_wr, axis=AX.X, apply_absolute_value=True))
                vgap()
            def c9(blk=blk):
                K.emit("vector", lambda e: e.tensor_tensor(
                    out=mx_t, in0=mx_t, in1=cmx, op=ALU.max))
                K.wait("sync", state["last_tev"])
                state["t_st"] = dma(t_flat[blk:blk + BPG].transpose([1, 0, 2]),
                                    t_wr.rearrange("p (b t) -> p b t", b=BPG))
            for f in (c1, c2, c3, c4, c5, c6, c7, c8, c9):
                vfiller.append(f)

    def emit_ar3():
        ar3 = allreduce_max(2, [mx_t], 1)
        K.wait("vector", ar3)
        K.emit("vector", lambda e: e.tensor_tensor(
            out=scals[0:1, 0:1], in0=scals[0:1, 0:1], in1=scals[0:1, 12:13], op=ALU.mult))
        vgap()
        K.emit("vector", lambda e: e.tensor_scalar(
            out=scals[0:1, 13:14], in0=scals[0:1, 0:1], scalar1=1.0 / 32767.0, scalar2=1e-12,
            op0=ALU.mult, op1=ALU.max))
        vgap()
        K.emit("vector", lambda e: e.reciprocal(out=scals[0:1, 5:6], in_=scals[0:1, 13:14]))
        vgap()
        b3 = K.emit("vector", lambda e: e.tensor_tensor(
            out=scals[0:1, 5:6], in0=scals[0:1, 5:6], in1=scals[0:1, 12:13], op=ALU.mult), sV)
        vgap()
        state["mt_ev"] = bcast(mt_b, 5, b3)

    def enqueue_D(g0, g1):
        g_rd, sig_f, t_wr, cmx = cd["g_rd"], cd["sig_f"], cd["t_wr"], cd["cmx"]

        def d0(g):
            blk = g * BPG
            K.wait("sync", state["t_st"], state["last_oev"])
            cd["lev1"] = dma(t_wr.rearrange("p (b t) -> p b t", b=BPG),
                             t_flat[blk:blk + BPG].transpose([1, 0, 2]))
            cd["lev2"] = dma(g_rd.rearrange("p (b t) -> p b t", b=BPG),
                             up_flat[blk:blk + BPG].transpose([1, 0, 2]))

        def d1():
            K.wait("vector", cd["lev1"], cd["lev2"], state["mt_ev"], state["mup_ev"],
                   state["o_st"])

        for g in range(g0, g1):
            blk = g * BPG
            vfiller.append(lambda g=g: d0(g))
            vfiller.append(d1)
            for buf, mb in ((t_wr, mt_b), (g_rd, mup_b)):
                vfiller.append(lambda buf=buf, mb=mb: K.emit(
                    "vector", lambda e, buf=buf, mb=mb: e.tensor_scalar(
                        out=buf, in0=buf, scalar1=mb, scalar2=None, op0=ALU.mult)))
                vfiller.append(lambda buf=buf: K.emit(
                    "vector", lambda e, buf=buf: e.tensor_scalar(
                        out=buf, in0=buf, scalar1=MAGIC, scalar2=MAGIC,
                        op0=ALU.add, op1=ALU.subtract)))
                vfiller.append(lambda buf=buf: K.emit(
                    "vector", lambda e, buf=buf: e.tensor_scalar(
                        out=buf, in0=buf, scalar1=32767.0, scalar2=-32768.0,
                        op0=ALU.min, op1=ALU.max)))
            def d2():
                o_ev = K.emit("vector", lambda e: e.tensor_tensor(
                    out=sig_f, in0=t_wr, in1=g_rd, op=ALU.mult), sV)
                state["last_oev"] = o_ev
            def d3():
                K.emit("vector", lambda e: e.reduce_max(
                    out=cmx, in_=sig_f, axis=AX.X, apply_absolute_value=True))
                vgap()
            def d4(blk=blk):
                K.emit("vector", lambda e: e.tensor_tensor(
                    out=mx_o, in0=mx_o, in1=cmx, op=ALU.max))
                K.wait("sync", state["last_oev"])
                state["o_st"] = dma(o_flat[blk:blk + BPG].transpose([1, 0, 2]),
                                    sig_f.rearrange("p (b t) -> p b t", b=BPG))
            for f in (d2, d3, d4):
                vfiller.append(f)

    def emit_ar4():
        ar4 = allreduce_max(3, [mx_o], 1)
        K.wait("vector", ar4)
        K.emit("vector", lambda e: e.tensor_tensor(
            out=scals[0:1, 6:7], in0=scals[0:1, 13:14], in1=scals[0:1, 10:11], op=ALU.mult))
        vgap()
        K.emit("vector", lambda e: e.tensor_tensor(
            out=scals[0:1, 0:1], in0=scals[0:1, 0:1], in1=scals[0:1, 6:7], op=ALU.mult))
        vgap()
        K.emit("vector", lambda e: e.tensor_scalar(
            out=scals[0:1, 14:15], in0=scals[0:1, 0:1], scalar1=1.0 / 32767.0, scalar2=1e-12,
            op0=ALU.mult, op1=ALU.max))
        vgap()
        K.emit("vector", lambda e: e.reciprocal(out=scals[0:1, 7:8], in_=scals[0:1, 14:15]))
        vgap()
        b4 = K.emit("vector", lambda e: e.tensor_tensor(
            out=scals[0:1, 7:8], in0=scals[0:1, 7:8], in1=scals[0:1, 6:7], op=ALU.mult), sV)
        vgap()
        state["mo_ev"] = bcast(mo_b, 7, b4)
        soc = K.emit("vector", lambda e: e.tensor_copy(
            out=scals[0:1, 15:16], in_=scals[0:1, 14:15]), sV)
        vgap()
        state["so_ev"] = bcast(so_b, 15, soc)


    acc_ev = [None] * NT
    K.wait("tensor", *ld_ev, *qkvo_ready, selbf_ev)
    K.wait("scalar", *ld_ev)

    prev = {"x_tm": None, "xn_bf": None, "qkv_mm": None, "qrep": None,
            "scores": None, "dv_mm": None, "wo_mm": None, "att_tr": None,
            "acc_st": None, "dg": None, "qk_seg": None}

    for tile in range(NT):
        cur_tile[0] = tile
        t0 = tile * T
        K.wait("sync", xnb_ev[tile], prev["qkv_mm"])
        lxn = dma(xn_bf.rearrange("p (c t) -> p c t", c=NCH),
                  xnb_s[tile].transpose([1, 0, 2]))
        K.wait("tensor", lxn)
        qkv_ev = {}
        last_qkv_mm = None
        for name, Wsb, bias, dstbf in (("q", Wq, bqs, Qb), ("k", Wk, bks, Kb), ("v", Wv, bvs, Vf)):
            for oc in range(NCH):
                nm, ps = next_mm()
                K.wait("tensor", free[nm])
                if name == "q" and oc == 0:
                    K.wait("tensor", prev["scores"], prev["qk_seg"])
                if name == "v" and oc == 0:
                    K.wait("tensor", prev["wo_mm"])
                    K.wait("vector", prev["att_tr"])
                mev = None
                for kc in range(NCH):
                    mev = K.emit("tensor", lambda e, ps=ps, Wsb=Wsb, kc=kc, oc=oc: e.matmul(
                        ps, Wsb[:, kc * D + oc * 128: kc * D + (oc + 1) * 128],
                        xn_bf[:, kc * 512:(kc + 1) * 512],
                        start=(kc == 0), stop=(kc == NCH - 1)), sT)
                last_qkv_mm = mev
                K.wait("vector", mev)
                if name == "q" and oc == 0:
                    K.wait("vector", prev["qrep"])
                ev = K.emit("vector", lambda e, ps=ps, dstbf=dstbf, bias=bias, oc=oc: e.tensor_scalar(
                    out=dstbf[:, oc * 512:(oc + 1) * 512], in0=ps,
                    scalar1=bias[:, oc:oc + 1], scalar2=None, op0=ALU.add), sV)
                free[nm] = ev
                qkv_ev[(name, oc)] = ev
                pump(1)
        prev["qkv_mm"] = last_qkv_mm
        nm_d, ps_d = next_sc()
        K.wait("tensor", free[nm_d])
        dev = None
        for c in range(NCH):
            b = qk2[c % 2]
            K.wait("vector", qkv_ev[("q", c)], qkv_ev[("k", c)],
                   None if c < 2 else dev)
            qev = K.emit("vector", lambda e, b=b, c=c: e.tensor_tensor(
                out=b, in0=Qb[:, c * 512:(c + 1) * 512], in1=Kb[:, c * 512:(c + 1) * 512],
                op=ALU.mult), sV)
            K.wait("tensor", qev)
            dev = K.emit("tensor", lambda e, ps_d=ps_d, b=b, c=c: e.matmul(
                ps_d[0:16, 0:512], seg[:, c * 16:(c + 1) * 16], b,
                start=(c == 0), stop=(c == NCH - 1)), sT)
        prev["qk_seg"] = dev
        K.wait("scalar", dev, prev["dg"])
        numer_ev = K.emit("scalar", lambda e, ps_d=ps_d: e.activation(
            out=numer, in_=ps_d[0:16, 0:512], func=AF.Exp, scale=EXPSC), sS)
        free[nm_d] = numer_ev
        all_qk = [qkv_ev[(n, oc)] for n in ("q", "k") for oc in range(NCH)]
        last_sc = None
        grevs = [[], []]
        for hl in range(2):
            Qrep = Qrep2[hl]
            K.wait("sync", *all_qk, prev.get(f"qrep_free{hl}"))
            for dhalf in range(2):
                for gpar in range(2):
                    grevs[hl].append(K.emit("sync", lambda e, Qrep=Qrep, dhalf=dhalf, gpar=gpar, hl=hl: e.dma_start(
                        out=Qrep[dhalf * 64:(dhalf + 1) * 64, :]
                            .rearrange("p (k t) -> p k t", t=512)[:, :, gpar * 256:(gpar + 1) * 256],
                        in_=Qb[gpar * 64:(gpar + 1) * 64, :]
                            .rearrange("p (c t) -> p c t", t=512)[:, :, hl * 256:(hl + 1) * 256]),
                        sDma[dma_rr[0] % NSL], 16))
                    dma_rr[0] += 1
        prev["qrep"] = grevs[1][-1]
        for hl in range(2):
            Qrep = Qrep2[hl]
            dnpAll = dnpAll2[hl]
            dnp_free = prev.get(f"dnp_free{hl}")
            grev = grevs[hl]
            K.wait("tensor", *grev)
            first_exp = True
            for r in range(8):
                for v in range(2):
                    m = r * 2 + v
                    lhsA = Kb[0:64,
                              r * 512 + hl * 256 + v * 128: r * 512 + hl * 256 + (v + 1) * 128]
                    lhsB = Kb[64:128,
                              r * 512 + hl * 256 + v * 128: r * 512 + hl * 256 + (v + 1) * 128]
                    for grp in range(4):
                        nmA, psA = next_sc()
                        nmB, psB = next_sc()
                        K.wait("tensor", free[nmA], free[nmB])
                        sevA = sevB = None
                        for half in range(2):
                            a0 = grp * 1024 + half * 512
                            sevA = K.emit("tensor", lambda e, psA=psA, lhsA=lhsA, a0=a0, half=half, Qrep=Qrep: e.matmul(
                                psA[:, half * 512:(half + 1) * 512],
                                lhsA, Qrep[0:64, a0:a0 + 512],
                                start=True, stop=True), sT)
                            sevB = K.emit("tensor", lambda e, psB=psB, lhsB=lhsB, a0=a0, half=half, Qrep=Qrep: e.matmul(
                                psB[:, half * 512:(half + 1) * 512],
                                lhsB, Qrep[64:128, a0:a0 + 512],
                                start=True, stop=True), sT)
                        if first_exp:
                            K.wait("scalar", sevA, dnp_free)
                            first_exp = False
                        else:
                            K.wait("scalar", sevA)
                        eevA = K.emit("scalar", lambda e, psA=psA, m=m, grp=grp, dnpAll=dnpAll: e.activation(
                            out=dummy, in_=psA, func=AF.Exp, scale=EXPSC,
                            accum_out=dnpAll[:, m * 8 + grp: m * 8 + grp + 1]), sS)
                        free[nmA] = eevA
                        K.wait("scalar", sevB)
                        eevB = K.emit("scalar", lambda e, psB=psB, m=m, grp=grp, dnpAll=dnpAll: e.activation(
                            out=dummy, in_=psB, func=AF.Exp, scale=EXPSC,
                            accum_out=dnpAll[:, m * 8 + 4 + grp: m * 8 + 4 + grp + 1]), sS)
                        free[nmB] = eevB
                        pump(1)
                        vpump(1)
                    last_sc = sevB
            K.wait("vector", eevB)
            K.emit("vector", lambda e, hl=hl, dnpAll=dnpAll: e.reduce_sum(
                out=dn[:, hl * 32:(hl + 1) * 32].rearrange("p (r w) -> p r w", w=4)[:, :, 0:2],
                in_=dnpAll.rearrange("p (m g) -> p m g", g=8)[:, :, 0:4], axis=AX.X))
            prev[f"dnp_free{hl}"] = K.emit("vector", lambda e, hl=hl, dnpAll=dnpAll: e.reduce_sum(
                out=dn[:, hl * 32:(hl + 1) * 32].rearrange("p (r w) -> p r w", w=4)[:, :, 2:4],
                in_=dnpAll.rearrange("p (m g) -> p m g", g=8)[:, :, 4:8], axis=AX.X), sV)
            prev[f"qrep_free{hl}"] = last_sc
        prev["scores"] = last_sc
        vgap()
        rc_ev = K.emit("vector", lambda e: e.reciprocal(out=rec, in_=dn), sV)
        for hl in range(2):
            for v in range(2):
                nm, ps = next_ms()
                K.wait("tensor", rc_ev, free[nm])
                tev = K.emit("tensor", lambda e, ps=ps, hl=hl, v=v: e.transpose(
                    ps[0:16, 0:128],
                    rec[:, hl * 32:(hl + 1) * 32].rearrange("p (g w) -> p g w", w=2)[:, :, v],
                    idn), sT)
                K.wait("vector", tev)
                ev = K.emit("vector", lambda e, ps=ps, hl=hl, v=v: e.tensor_copy(
                    out=recD[:, hl * 256 + v * 128: hl * 256 + (v + 1) * 128],
                    in_=ps[0:16, 0:128]), sV)
                free[nm] = ev
        K.wait("vector", numer_ev)
        vgap()
        dg_ev = K.emit("vector", lambda e: e.tensor_tensor(
            out=Dg_bf, in0=numer, in1=recD, op=ALU.mult), sV)
        vgap()
        prev["dg"] = dg_ev
        dv_ev = []
        last_dv_mm = None
        for c in range(NCH):
            nm, ps = next_ms()
            K.wait("tensor", dg_ev, free[nm])
            bev = K.emit("tensor", lambda e, ps=ps, c=c: e.matmul(
                ps, sel_bf[:, c * 128:(c + 1) * 128], Dg_bf, start=True, stop=True), sT)
            last_dv_mm = bev
            K.wait("vector", bev, qkv_ev[("v", c)])
            ev = K.emit("vector", lambda e, ps=ps, c=c: e.tensor_tensor(
                out=DVb[:, c * 512:(c + 1) * 512], in0=ps,
                in1=Vf[:, c * 512:(c + 1) * 512], op=ALU.mult), sV)
            free[nm] = ev
            dv_ev.append(ev)
        prev["dv_mm"] = last_dv_mm
        at_ev = {}
        last_wo = None
        for oc in range(NCH):
            nm, ps = next_mm()
            K.wait("tensor", free[nm], *dv_ev)
            mev = None
            for kc in range(NCH):
                mev = K.emit("tensor", lambda e, ps=ps, kc=kc, oc=oc: e.matmul(
                    ps, Wo[:, kc * D + oc * 128: kc * D + (oc + 1) * 128],
                    DVb[:, kc * 512:(kc + 1) * 512],
                    start=(kc == 0), stop=(kc == NCH - 1)), sT)
            last_wo = mev
            K.wait("vector", mev)
            ev = K.emit("vector", lambda e, ps=ps, oc=oc: e.tensor_scalar(
                out=attn_f[:, oc * 512:(oc + 1) * 512], in0=ps,
                scalar1=bos[:, oc:oc + 1], scalar2=None, op0=ALU.add), sV)
            free[nm] = ev
            at_ev[oc] = ev
            pump(1)
        prev["wo_mm"] = last_wo
        last_tt = None
        acc_last = prev["acc_st"]
        for j in range(4):
            K.wait("sync", acc_last, last_tt)
            lxj = dma(x_j, x_in[t0 + j * 128: t0 + (j + 1) * 128, :])
            jt_ev = []
            for half in range(2):
                nm, ps = next_ms()
                K.wait("tensor", free[nm],
                       *[at_ev[half * 4 + c4] for c4 in range(4)])
                tev = None
                for c4 in range(4):
                    c = half * 4 + c4
                    tev = K.emit("tensor", lambda e, ps=ps, c=c, c4=c4, j=j: e.transpose(
                        ps[:, c4 * 128:(c4 + 1) * 128],
                        attn_f[:, c * 512 + j * 128: c * 512 + (j + 1) * 128], idn), sT)
                K.wait("vector", tev, lxj)
                ev = K.emit("vector", lambda e, ps=ps, half=half: e.tensor_tensor(
                    out=x_j[:, half * 512:(half + 1) * 512],
                    in0=ps, in1=x_j[:, half * 512:(half + 1) * 512],
                    op=ALU.add), sV)
                free[nm] = ev
                jt_ev.append(ev)
            last_tt = jt_ev[-1]
            K.wait("sync", *jt_ev)
            acc_last = dma(acc_s[tile, j], x_j)
            pump(1)
        prev["att_tr"] = last_tt
        acc_ev[tile] = acc_last
        prev["acc_st"] = acc_last
        pump(4)

        if tile == 3:
            pump(len(filler))
            emit_ar2()
            enqueue_C(0, NBLK // BPG)
        if tile == 5:
            vpump(len(vfiller))
            emit_ar3()
            enqueue_D(0, NBLK // BPG)
        if tile == 7:
            vpump(len(vfiller))
            emit_ar4()

    scopeCD.close()
    K.wait("sync", acc_ev[NT - 1], prev["scores"], prev["qk_seg"], prev["wo_mm"],
           prev["dg"], prev["att_tr"], prev["qrep"], prev["dv_mm"], prev["qkv_mm"])
    K.wait("vector", prev["scores"], prev["wo_mm"])
    scopeATT.close()
    prep.close()
    scopeW.close()
    Ae = lambda name, shape, dt: main.enter_context(nc.sbuf_tensor(name, shape, dt)).ap()
    o_ch2 = [Ae(f"o_ch{i}", [128, 2048], F32) for i in range(2)]
    oq_bf2 = [Ae(f"oq_bf{i}", [128, NFF * 512], BF16) for i in range(2)]
    WdT = Ae("WdT", [128, NFF * D], BF16)
    acc_rd2 = [Ae(f"acc_rd{i}", [128, 4096], F32) for i in range(2)]

    K.wait("sync", wready[("down", 0)])
    wd_ld = []
    for ic in range(NFF):
        wd_ld.append(dma(WdT[:, ic * D:(ic + 1) * D], wdownT_s[ic]))
    K.wait("vector", state["so_ev"], *wd_ld)
    wds_ev = K.emit("vector", lambda e: e.tensor_scalar(
        out=WdT, in0=WdT, scalar1=so_b, scalar2=None, op0=ALU.mult), sV)

    out_st = []
    bank_ap = [ps_sc[i][:, h * 512:(h + 1) * 512] for i in range(3) for h in range(2)] \
        + [ps_mm[0], ps_mm[1]]
    bank_free = [free["sc0"], free["sc0"], free["sc1"], free["sc1"],
                 free["sc2"], free["sc2"], free["mm0"], free["mm1"]]
    bank_last = [None] * 8
    och_user = [None, None]
    acc_user = [None, None]
    qdq_gate = [None, None]
    for tile in range(NT):
        oq_bf = oq_bf2[tile % 2]
        acc_rd = acc_rd2[tile % 2]
        K.wait("sync", acc_ev[tile], acc_user[tile % 2])
        aev = dma(acc_rd.rearrange("p (j f) -> p j f", j=4),
                  acc_s[tile].transpose([1, 0, 2]))
        for ch in range(6):
            o_ch = o_ch2[ch % 2]
            K.wait("scalar", state["o_st"], och_user[ch % 2])
            state["o_st"] = None
            lev = dma(o_ch.rearrange("p (c t) -> p c t", c=4),
                      o_s[tile, ch * 4:(ch + 1) * 4].transpose([1, 0, 2]), eng="scalar")
            K.wait("vector", lev, state["mo_ev"], qdq_gate[tile % 2] if ch == 0 else None)
            state["mo_ev"] = None
            K.emit("vector", lambda e, o_ch=o_ch: e.tensor_scalar(
                out=o_ch, in0=o_ch, scalar1=mo_b, scalar2=None, op0=ALU.mult))
            K.emit("vector", lambda e, o_ch=o_ch: e.tensor_scalar(
                out=o_ch, in0=o_ch, scalar1=MAGIC, scalar2=MAGIC, op0=ALU.add, op1=ALU.subtract))
            ev = K.emit("vector", lambda e, ch=ch, o_ch=o_ch, oq_bf=oq_bf: e.tensor_scalar(
                out=oq_bf[:, ch * 2048:(ch + 1) * 2048], in0=o_ch,
                scalar1=32767.0, scalar2=-32768.0, op0=ALU.min, op1=ALU.max), sV)
            och_user[ch % 2] = ev
            K.wait("tensor", ev, wds_ev if (tile == 0 and ch == 0) else None)
            for b in range(8):
                j, nf = b // 2, b % 2
                if ch == 0:
                    K.wait("tensor", bank_free[b])
                for ic2 in range(4):
                    ic = ch * 4 + ic2
                    bank_last[b] = K.emit("tensor", lambda e, b=b, ic=ic, j=j, nf=nf, oq_bf=oq_bf: e.matmul(
                        bank_ap[b], oq_bf[:, ic * 512 + j * 128: ic * 512 + (j + 1) * 128],
                        WdT[:, ic * D + nf * 512: ic * D + (nf + 1) * 512],
                        start=(ic == 0), stop=(ic == NFF - 1)), sT)
        qdq_gate[tile % 2] = bank_last[7]
        ts_ev = []
        for b in range(8):
            j, nf = b // 2, b % 2
            K.wait("vector", bank_last[b], aev if b == 0 else None)
            ev = K.emit("vector", lambda e, b=b, j=j, nf=nf, acc_rd=acc_rd: e.tensor_tensor(
                out=acc_rd[:, j * 1024 + nf * 512: j * 1024 + (nf + 1) * 512],
                in0=bank_ap[b], in1=acc_rd[:, j * 1024 + nf * 512: j * 1024 + (nf + 1) * 512],
                op=ALU.add), sV)
            bank_free[b] = ev
            ts_ev.append(ev)
        K.wait("sync", *ts_ev)
        st = dma(out[tile * T:(tile + 1) * T, :].rearrange("(j p) f -> p j f", p=128),
                 acc_rd.rearrange("p (j f) -> p j f", j=4))
        out_st.append(st)
        acc_user[tile % 2] = st
    main.close()

    K.wait("sync", *out_st)
    K.run()
    es.close()
    return nc


def make_consts():
    ident = np.eye(128, dtype=np.float32)
    seg16 = np.zeros((128, NCH * 16), np.float32)
    for c in range(NCH):
        for p in range(128):
            seg16[p, c * 16 + 2 * c + p // 64] = 1.0
    sel64 = np.zeros((16, NCH * 128), np.float32)
    for c in range(NCH):
        for p in range(128):
            sel64[2 * c + p // 64, c * 128 + p] = 1.0
    ones_row = np.ones((1, 128), np.float32)
    return ident, seg16, sel64, ones_row


_NC_CACHE = {}


def kernel(L0_attn_X_embed, L0_attn_W_q, L0_attn_b_q, L0_attn_W_k, L0_attn_b_k,
           L0_attn_W_v, L0_attn_b_v, L0_attn_W_o, L0_attn_b_o,
           L0_ffn_w_gate, L0_ffn_w_up, L0_ffn_w_down):
    n_cores = 8
    if n_cores not in _NC_CACHE:
        _NC_CACHE[n_cores] = build_nc(n_cores)
    nc = _NC_CACHE[n_cores]
    ident, seg16, sel64, ones_row = make_consts()
    x = np.ascontiguousarray(np.asarray(L0_attn_X_embed, dtype=np.float32))
    common = dict(
        w_q=np.ascontiguousarray(np.asarray(L0_attn_W_q, np.float32)),
        b_q=np.ascontiguousarray(np.asarray(L0_attn_b_q, np.float32)),
        w_k=np.ascontiguousarray(np.asarray(L0_attn_W_k, np.float32)),
        b_k=np.ascontiguousarray(np.asarray(L0_attn_b_k, np.float32)),
        w_v=np.ascontiguousarray(np.asarray(L0_attn_W_v, np.float32)),
        b_v=np.ascontiguousarray(np.asarray(L0_attn_b_v, np.float32)),
        w_o=np.ascontiguousarray(np.asarray(L0_attn_W_o, np.float32)),
        b_o=np.ascontiguousarray(np.asarray(L0_attn_b_o, np.float32)),
        w_gate=np.ascontiguousarray(np.asarray(L0_ffn_w_gate, np.float32)),
        w_up=np.ascontiguousarray(np.asarray(L0_ffn_w_up, np.float32)),
        w_down=np.ascontiguousarray(np.asarray(L0_ffn_w_down, np.float32)),
        ident=ident, seg16=seg16, sel64=sel64, ones_row=ones_row,
    )
    in_maps = [dict(x_in=np.ascontiguousarray(x[i]), **common) for i in range(n_cores)]
    res = run_bass_kernel_spmd(nc, in_maps, core_ids=list(range(n_cores)))
    return np.stack([r["out"] for r in res.results], axis=0)
